# version 04
# baseline (speedup 1.0000x reference)
import contextlib
import numpy as np
import concourse.bass as bass
import concourse.mybir as mybir
from concourse.bass_utils import run_bass_kernel_spmd

F32 = mybir.dt.float32
BF16 = mybir.dt.bfloat16
I32 = mybir.dt.int32
ALU = mybir.AluOpType
AF = mybir.ActivationFunctionType

T = 2048
D = 1024
NCH = 8
QBS = 512
NQB = T // QBS
H = 16
DH = 64
NGRP = 2
CPG = 4
FF = 4096
NFS = 4

LN2 = float(np.float32(np.log(2.0)))

ENGS = ("te", "act", "dve", "sp", "gp")
STEP = {"te": 1, "act": 1, "dve": 1, "sp": 16, "gp": 16}


class Res:
    __slots__ = ("w", "r")

    def __init__(self):
        self.w = None
        self.r = []


QUEUES = ("sp", "gp")
QDEPTH = 6


class Sched:

    def __init__(self):
        self.stream = {e: [] for e in ENGS}
        self.tick = {e: 0 for e in ENGS}
        self.seen = {e: {} for e in ENGS}

    def _covered(self, e, pe, pt):
        key = (pe, (pt - 1) % QDEPTH) if pe in QUEUES else pe
        return self.seen[e].get(key, 0) >= pt

    def _cover(self, e, pe, pt):
        key = (pe, (pt - 1) % QDEPTH) if pe in QUEUES else pe
        if self.seen[e].get(key, 0) < pt:
            self.seen[e][key] = pt

    def emit(self, e, fn, reads=(), writes=()):
        need = {}

        def add(pe, pt):
            key = (pe, (pt - 1) % QDEPTH) if pe in QUEUES else pe
            if pt > need.get(key, (None, 0))[1]:
                need[key] = (pe, pt)

        for r in reads:
            if r.w is not None:
                add(*r.w)
        for w in writes:
            if w.w is not None:
                add(*w.w)
            for pe, pt in w.r:
                add(pe, pt)
        waits = []
        self.tick[e] += 1
        t = self.tick[e]
        if e in QUEUES and t > QDEPTH:
            pw = t - QDEPTH
            if not self._covered(e, e, pw):
                waits.append((e, pw))
                self._cover(e, e, pw)
        for key in sorted(need, key=str):
            pe, pt = need[key]
            if pe == "te" and e == "te":
                continue
            if self._covered(e, pe, pt):
                continue
            waits.append((pe, pt))
            self._cover(e, pe, pt)
        for r in reads:
            r.r.append((e, t))
        for w in writes:
            w.w = (e, t)
            w.r = []
        self.stream[e].append((waits, fn))


def host_layout(inputs):
    L = {}
    x = np.asarray(inputs["L0_attn_x"], dtype=np.float32)
    L["x_fm"] = np.ascontiguousarray(np.transpose(x, (0, 2, 1))).reshape(16 * D, T)

    def img(w):
        w = np.asarray(w, dtype=np.float32)
        k, n = w.shape
        return np.ascontiguousarray(w.reshape(k // 128, 128, n).transpose(1, 0, 2).reshape(128, (k // 128) * n))

    L["w0q"] = img(inputs["L0_attn_Wq"])
    L["w0k"] = img(inputs["L0_attn_Wk"])
    L["w0v"] = img(inputs["L0_attn_Wv"])
    L["w0o"] = np.ascontiguousarray(np.asarray(inputs["L0_attn_Wo"], dtype=np.float32))
    L["w8T"] = img(np.asarray(inputs["L0_ffn_weight_fp8"]).T)
    L["b0row"] = np.asarray(inputs["L0_ffn_bias"], dtype=np.float32).reshape(1, D)
    L["s_in"] = np.asarray(inputs["L0_ffn_input_scale"], dtype=np.float32).reshape(1, 1)
    L["s_w"] = np.asarray(inputs["L0_ffn_weight_scale"], dtype=np.float32).reshape(1, 1)

    perm = np.zeros(D, dtype=np.int64)
    for h in range(H):
        b = h * DH
        perm[b:b + 32] = b + np.arange(0, DH, 2)
        perm[b + 32:b + DH] = b + 1 + np.arange(0, DH, 2)
    wq = np.asarray(inputs["L1_attn_Wq"]).T[:, perm]
    wk = np.asarray(inputs["L1_attn_Wk"]).T[:, perm]
    L["w1q"] = np.concatenate([img(wq[:, g * 512:(g + 1) * 512]) for g in range(NGRP)], 0)
    L["w1k"] = np.concatenate([img(wk[:, g * 512:(g + 1) * 512]) for g in range(NGRP)], 0)
    wv = np.asarray(inputs["L1_attn_Wv"]).T
    L["w1v"] = np.concatenate([img(wv[:, g * 512:(g + 1) * 512]) for g in range(NGRP)], 0)
    wo = np.asarray(inputs["L1_attn_Wo"]).T
    L["w1o"] = np.concatenate([img(wo[g * 512:(g + 1) * 512, :]) for g in range(NGRP)], 0)
    bq = np.asarray(inputs["L1_attn_bq"])[perm]
    bk = np.asarray(inputs["L1_attn_bk"])[perm]
    L["bqcol"] = np.ascontiguousarray(bq.reshape(NCH, 128).T.astype(np.float32))
    L["bkcol"] = np.ascontiguousarray(bk.reshape(NCH, 128).T.astype(np.float32))
    L["bvrow"] = np.asarray(inputs["L1_attn_bv"], dtype=np.float32).reshape(1, D)
    L["borow"] = np.asarray(inputs["L1_attn_bo"], dtype=np.float32).reshape(1, D)

    w1f = np.asarray(inputs["L1_ffn_w1"])
    L["w1f"] = np.concatenate([img(w1f[:, s * 1024:(s + 1) * 1024]) for s in range(NFS)], 0)
    w2f = np.asarray(inputs["L1_ffn_w2"])
    L["w2f"] = np.concatenate([img(w2f[s * 1024:(s + 1) * 1024, :]) for s in range(NFS)], 0)
    L["b1col"] = np.ascontiguousarray(np.asarray(inputs["L1_ffn_b1"]).reshape(32, 128).T.astype(np.float32))
    L["b2row"] = np.asarray(inputs["L1_ffn_b2"], dtype=np.float32).reshape(1, D)
    L["iota"] = np.arange(T, dtype=np.float32).reshape(1, T)
    try:
        import jax
        import jax.numpy as jnp
        with jax.default_device(jax.local_devices(backend="cpu")[0]):
            theta_j = 10000.0 ** (-jnp.arange(0, DH, 2, dtype=jnp.float32) / DH)
            seq_j = jnp.arange(T, dtype=jnp.float32)
            fr_j = jnp.einsum('i,j->ij', seq_j, theta_j)
            sn = np.asarray(jnp.sin(fr_j)).astype(np.float32).T
            cs = np.asarray(jnp.cos(fr_j)).astype(np.float32).T
    except Exception:
        theta = (10000.0 ** (-np.arange(0, DH, 2, dtype=np.float32) / DH)).astype(np.float32)
        fr = np.einsum('i,j->ij', np.arange(T, dtype=np.float32), theta).astype(np.float32)
        sn = np.sin(fr).astype(np.float32).T
        cs = np.cos(fr).astype(np.float32).T
    L["tt1"] = np.ascontiguousarray(np.tile(cs, (4, 1)))
    L["tt2"] = np.ascontiguousarray(np.concatenate([-sn, sn, -sn, sn], 0))
    return L


DRAM_SPECS = [
    ("x_fm", [2 * D, T]), ("w0q", [128, 1024]), ("w0k", [128, 1024]), ("w0v", [128, 1024]),
    ("w0o", [128, 1024]), ("w8T", [128, 8 * 1024]), ("b0row", [1, D]), ("s_in", [1, 1]), ("s_w", [1, 1]),
    ("w1q", [256, 4096]), ("w1k", [256, 4096]), ("w1v", [256, 4096]), ("w1o", [256, 4096]),
    ("bqcol", [128, 8]), ("bkcol", [128, 8]), ("bvrow", [1, D]), ("borow", [1, D]),
    ("w1f", [512, 8192]), ("w2f", [512, 8192]), ("b1col", [128, 32]), ("b2row", [1, D]),
    ("iota", [1, T]), ("tt1", [128, T]), ("tt2", [128, T]),
]


def build_nc(n_seq=2, debug=False, phases=("l0a", "l0f", "l1a", "l1f")):
    nc = bass.Bass("TRN2", target_bir_lowering=False, debug=False)
    dr = {nm: nc.dram_tensor(nm, shp, F32, kind="ExternalInput").ap() for nm, shp in DRAM_SPECS}
    out = nc.dram_tensor("out_fm", [n_seq * D, T], F32, kind="ExternalOutput").ap()
    dbg = {}
    if debug:
        for nm in ("dbg0a", "dbg0f", "dbg1a"):
            dbg[nm] = nc.dram_tensor(nm, [n_seq * D, T], F32, kind="ExternalOutput").ap()

    ctx = contextlib.ExitStack()
    sb = lambda nm, shp, dt: ctx.enter_context(nc.sbuf_tensor(nm, shp, dt))

    X = sb("X", [128, NCH * T], F32)
    XN = sb("XN", [128, NCH * T], BF16)
    A0 = sb("A0", [128, 8192], BF16)
    A1 = sb("A1", [128, 8192], BF16)
    A2 = sb("A2", [128, 8192], BF16)
    A3 = sb("A3", [128, 4096], BF16)
    A4 = sb("A4", [128, 8192], BF16)
    TT1 = sb("TT1", [128, T], BF16)
    TT2 = sb("TT2", [128, T], BF16)
    MSK = sb("MSK", [128, 4 * QBS], BF16)
    MR = sb("MR", [33, QBS], F32)
    RR = sb("RR", [1, 1024], F32)
    TBS = sb("TBS", [128, 512], F32)
    SC1 = sb("SC1", [128, 512], F32)
    SC2 = sb("SC2", [128, 512], F32)
    SI = sb("SI", [128, 512], I32)
    SC4 = sb("SC4", [128, 512], F32)
    B0R = sb("B0R", [1, D], BF16)
    BVR = sb("BVR", [1, D], BF16)
    BOR = sb("BOR", [1, D], BF16)
    B2R = sb("B2R", [1, D], BF16)
    BQC = sb("BQC", [128, NCH], F32)
    BKC = sb("BKC", [128, NCH], F32)
    B1C = sb("B1C", [128, 32], F32)
    ONES = sb("ONES", [33, 128], F32)
    ONESB = sb("ONESB", [128, QBS], BF16)
    ONEC = sb("ONEC", [128, 1], F32)
    INDA = sb("INDA", [128, 1], F32)
    INDB = sb("INDB", [128, 1], F32)
    CST = sb("CST", [128, 8], F32)
    SCL = sb("SCL", [1, 16], F32)
    PS = [ctx.enter_context(nc.psum_tensor("PS%d" % i, [128, 512], F32)) for i in range(8)]

    C_LN2, C_3LN2, C_EPS, C_S1, C_SS, C_S0 = 0, 1, 2, 3, 4, 5

    s = Sched()
    R = lambda: Res()
    rX = [[R() for b in range(NQB)] for c in range(NCH)]
    rXN = [[R() for b in range(NQB)] for c in range(NCH)]
    rA0 = [R() for c in range(CPG)]
    rA1 = [R() for c in range(CPG)]
    rA2 = [R() for i in range(16)]
    rA3 = [R() for i in range(8)]
    rA4 = [R() for i in range(8)]
    rTT, rMSK = R(), R()
    rMR, rRR, rTBS = R(), R(), R()
    rSC1, rSC2, rSI, rSC4 = R(), R(), R(), R()
    rROWS, rC, rSCL = R(), R(), R()
    rPS = [R() for i in range(8)]

    def dma(q, outap, inap, reads=(), writes=()):
        s.emit(q, lambda e, o=outap, i=inap: e.dma_start(out=o, in_=i), reads=reads, writes=writes)

    def mm(ms, reads, writes):
        def fn(e, ms_=tuple(ms)):
            last = None
            for m in ms_:
                (o, l, r_, st, sp, tp), skip = m[:6], (len(m) > 6 and m[6])
                last = e.matmul(o, l, r_, start=st, stop=sp, tile_position=tp,
                                skip_group_check=bool(skip))
            return last
        s.emit("te", fn, reads=reads, writes=writes)

    def act(fn, reads, writes):
        s.emit("act", fn, reads=reads, writes=writes)

    def dve(fn, reads, writes):
        s.emit("dve", fn, reads=reads, writes=writes)

    def floorc(wap, rres):
        n = wap.shape[1]
        dve(lambda e: e.tensor_copy(SI[:, 0:n], wap), reads=[rres], writes=[rSI])
        dve(lambda e: e.tensor_tensor(TBS[:, 0:n], SI[:, 0:n], wap, ALU.is_gt),
            reads=[rSI, rres], writes=[rTBS])
        dve(lambda e: e.scalar_tensor_tensor(wap, TBS[:, 0:n], -1.0, SI[:, 0:n], ALU.mult, ALU.add),
            reads=[rTBS, rSI], writes=[rres])

    def floorc2(wap, rres):
        n = wap.shape[1]
        dve(lambda e: e.tensor_copy(SI[:, 0:n], wap), reads=[rres], writes=[rSI])
        dve(lambda e: e.tensor_tensor(TBS[:, 0:n], SI[:, 0:n], wap, ALU.is_gt),
            reads=[rSI, rres], writes=[rTBS])
        dve(lambda e: e.scalar_tensor_tensor(SC4[:, 0:n], SI[:, 0:n], -1.0, wap, ALU.add, ALU.is_gt),
            reads=[rSI, rres], writes=[rSC4])
        dve(lambda e: e.scalar_tensor_tensor(wap, SI[:, 0:n], 1.0, wap, ALU.add, ALU.is_le),
            reads=[rSI, rres], writes=[rres])
        dve(lambda e: e.tensor_tensor(wap, wap, TBS[:, 0:n], ALU.subtract),
            reads=[rres, rTBS], writes=[rres])
        dve(lambda e: e.tensor_tensor(wap, wap, SC4[:, 0:n], ALU.subtract),
            reads=[rres, rSC4], writes=[rres])
        dve(lambda e: e.tensor_tensor(wap, wap, SI[:, 0:n], ALU.add),
            reads=[rres, rSI], writes=[rres])

    Xs = lambda c, b: X[:, c * T + b * QBS: c * T + (b + 1) * QBS]
    XNs = lambda c, b: XN[:, c * T + b * QBS: c * T + (b + 1) * QBS]
    XNt = lambda c, tt: XN[:, c * T + tt * 128: c * T + (tt + 1) * 128]
    ARow = RR[0:1, 0:QBS]
    BRow = RR[0:1, QBS:2 * QBS]

    dma("gp", B0R[:], dr["b0row"], writes=[rROWS])
    dma("gp", BVR[:], dr["bvrow"], writes=[rROWS])
    dma("gp", BOR[:], dr["borow"], writes=[rROWS])
    dma("gp", B2R[:], dr["b2row"], writes=[rROWS])
    dma("sp", BQC[:], dr["bqcol"], writes=[rROWS])
    dma("sp", BKC[:], dr["bkcol"], writes=[rROWS])
    dma("sp", B1C[:], dr["b1col"], writes=[rROWS])
    dma("sp", SCL[:, 0:1], dr["s_in"], writes=[rSCL])
    dma("sp", SCL[:, 1:2], dr["s_w"], writes=[rSCL])

    def consts_fn(e):
        e.memset(ONES[:], 1.0)
        e.memset(ONESB[:], 1.0)
        e.memset(ONEC[:], 1.0)
        e.memset(INDA[0:64, :], 1.0)
        e.memset(INDA[64:128, :], 0.0)
        e.memset(INDB[0:64, :], 0.0)
        e.memset(INDB[64:128, :], 1.0)
        e.memset(CST[:, C_LN2:C_LN2 + 1], LN2)
        e.memset(CST[:, C_3LN2:C_3LN2 + 1], 3.0 * LN2)
        e.memset(CST[:, C_EPS:C_EPS + 1], 1e-5)
        e.memset(CST[:, C_S1:C_S1 + 1], 0.125)
        return e.memset(SCL[:, 8:9], 128.0)
    s.emit("dve", consts_fn, reads=[], writes=[rC, rSCL])

    dve(lambda e: e.reciprocal(SCL[:, 2:3], SCL[:, 0:1]), reads=[rSCL], writes=[rSCL])
    dve(lambda e: e.tensor_tensor(SCL[:, 3:4], SCL[:, 0:1], SCL[:, 1:2], ALU.mult), reads=[rSCL], writes=[rSCL])
    act(lambda e: e.activation(SCL[:, 6:7], SCL[:, 8:9], AF.Ln), reads=[rSCL], writes=[rSCL])
    dve(lambda e: e.tensor_scalar(SCL[:, 6:7], SCL[:, 6:7], -0.5, None, ALU.mult), reads=[rSCL], writes=[rSCL])
    act(lambda e: e.activation(SCL[:, 6:7], SCL[:, 6:7], AF.Exp), reads=[rSCL], writes=[rSCL])
    dve(lambda e: e.reciprocal(SCL[:, 4:5], SCL[:, 3:4]), reads=[rSCL], writes=[rSCL])
    dve(lambda e: e.tensor_scalar(B0R[:], B0R[:], SCL[:, 4:5], None, ALU.mult), reads=[rROWS, rSCL], writes=[rROWS])
    dve(lambda e: e.tensor_scalar(BQC[:], BQC[:], 0.125, None, ALU.mult), reads=[rROWS], writes=[rROWS])
    mm([(PS[7][:, 0:1], ONES[0:1, 0:128], SCL[0:1, 3:4], True, True, None)], reads=[rC, rSCL], writes=[rPS[7]])
    act(lambda e: e.activation(CST[:, C_SS:C_SS + 1], PS[7][:, 0:1], AF.Identity), reads=[rPS[7]], writes=[rC])
    mm([(PS[7][:, 0:1], ONES[0:1, 0:128], SCL[0:1, 6:7], True, True, None)], reads=[rC, rSCL], writes=[rPS[7]])
    act(lambda e: e.activation(CST[:, C_S0:C_S0 + 1], PS[7][:, 0:1], AF.Identity), reads=[rPS[7]], writes=[rC])

    dma("gp", TT1[:], dr["tt1"], writes=[rTT])
    dma("gp", TT2[:], dr["tt2"], writes=[rTT])

    dma("sp", BRow, dr["iota"][0:1, 0:QBS], writes=[rRR])
    dve(lambda e: e.memset(TBS[0:1, :], 1.0), reads=[], writes=[rTBS])
    for dpos in range(4):
        mm([(PS[6][:, 0:QBS], BRow[0:1, 0:128], TBS[0:1, :], True, False, None)],
           reads=[rRR, rC, rTBS], writes=[rPS[6]])
        dve(lambda e, d_=dpos: e.tensor_scalar(SC1[0:1, 0:QBS], BRow, -1.0, 128.0 * d_, ALU.mult, ALU.add),
            reads=[rRR], writes=[rSC1])
        mm([(PS[6][:, 0:QBS], ONES[0:1, 0:128], SC1[0:1, 0:QBS], False, True, None)],
           reads=[rC, rSC1], writes=[rPS[6]])
        dve(lambda e: e.tensor_scalar(SC2[:], PS[6][:, 0:QBS], 0.0, -3.0e38, ALU.is_gt, ALU.mult),
            reads=[rPS[6]], writes=[rSC2])
        dve(lambda e, d_=dpos: e.tensor_copy(MSK[:, d_ * QBS:(d_ + 1) * QBS], SC2[:]),
            reads=[rSC2], writes=[rMSK])

    def ln_pass(quant=False):
        for b in range(NQB):
            xres = [rX[c][b] for c in range(NCH)]
            mm([(PS[6][0:1, 0:QBS], ONEC[:], Xs(c, b), c == 0, c == NCH - 1, None) for c in range(NCH)],
               reads=[rC] + xres, writes=[rPS[6]])
            for c in range(NCH):
                act(lambda e, c_=c, b_=b: e.activation(SC2[:], Xs(c_, b_), AF.Square),
                    reads=[rX[c][b]], writes=[rSC2])
                mm([(PS[6][32:33, 0:QBS], ONEC[:], SC2[:], c == 0, c == NCH - 1, None)],
                   reads=[rC, rSC2], writes=[rPS[6]])
            dve(lambda e: e.tensor_scalar(BRow, PS[6][0:1, 0:QBS], 1.0 / D, None, ALU.mult),
                reads=[rPS[6]], writes=[rRR])
            dve(lambda e: e.tensor_scalar(ARow, PS[6][32:33, 0:QBS], 1.0 / D, None, ALU.mult),
                reads=[rPS[6]], writes=[rRR])
            dve(lambda e: e.tensor_tensor(SC1[0:1, 0:QBS], BRow, BRow, ALU.mult),
                reads=[rRR], writes=[rSC1])
            dve(lambda e: e.tensor_tensor(ARow, ARow, SC1[0:1, 0:QBS], ALU.subtract),
                reads=[rRR, rSC1], writes=[rRR])
            act(lambda e: e.activation(ARow, ARow, AF.Sqrt, bias=CST[0:1, C_EPS:C_EPS + 1]),
                reads=[rRR, rC], writes=[rRR])
            dve(lambda e: e.reciprocal(ARow, ARow), reads=[rRR], writes=[rRR])
            if quant:
                dve(lambda e: e.tensor_scalar(ARow, ARow, SCL[0:1, 2:3], None, ALU.mult),
                    reads=[rRR, rSCL], writes=[rRR])
            dve(lambda e: e.scalar_tensor_tensor(BRow, BRow, -1.0, ARow, ALU.mult, ALU.mult),
                reads=[rRR], writes=[rRR])
            mm([(PS[4][:, 0:QBS], ONES[0:1, 0:128], ARow, True, True, None)], reads=[rC, rRR], writes=[rPS[4]])
            mm([(PS[5][:, 0:QBS], ONES[0:1, 0:128], BRow, True, True, None)], reads=[rC, rRR], writes=[rPS[5]])
            for c in range(NCH):
                if not quant:
                    dve(lambda e, c_=c, b_=b: e.tensor_tensor(SC2[:], Xs(c_, b_), PS[4][:, 0:QBS], ALU.mult),
                        reads=[rX[c][b], rPS[4]], writes=[rSC2])
                    dve(lambda e, c_=c, b_=b: e.tensor_tensor(XNs(c_, b_), SC2[:], PS[5][:, 0:QBS], ALU.add),
                        reads=[rSC2, rPS[5]], writes=[rXN[c][b]])
                else:
                    quant_block(c, b)

    def quant_block(c, b):
        dve(lambda e, c_=c, b_=b: e.tensor_tensor(SC1[:], Xs(c_, b_), PS[4][:, 0:QBS], ALU.mult),
            reads=[rX[c][b], rPS[4]], writes=[rSC1])
        dve(lambda e: e.tensor_tensor(SC1[:], SC1[:], PS[5][:, 0:QBS], ALU.add),
            reads=[rSC1, rPS[5]], writes=[rSC1])
        dve(lambda e: e.tensor_scalar(SC1[:], SC1[:], 448.0, -448.0, ALU.min, ALU.max),
            reads=[rSC1], writes=[rSC1])
        act(lambda e: e.activation(SC2[:], SC1[:], AF.Square), reads=[rSC1], writes=[rSC2])
        dve(lambda e: e.tensor_scalar(SC2[:], SC2[:], 2.0 ** -24, None, ALU.max), reads=[rSC2], writes=[rSC2])
        act(lambda e: e.activation(SC2[:], SC2[:], AF.Ln), reads=[rSC2], writes=[rSC2])
        dve(lambda e: e.tensor_scalar(SC2[:], SC2[:], float(np.float32(0.5 / np.log(2.0))), None, ALU.mult),
            reads=[rSC2], writes=[rSC2])
        floorc(SC2[:], rSC2)
        dve(lambda e: e.tensor_copy(SI[:], SC2[:]), reads=[rSC2], writes=[rSI])
        dve(lambda e: e.tensor_scalar(SC2[:], SI[:], -6.0, -1.0, ALU.max, ALU.mult), reads=[rSI], writes=[rSC2])
        act(lambda e: e.activation(SC2[:], SC2[:], AF.Exp, bias=CST[:, C_3LN2:C_3LN2 + 1],
                                   scale=CST[:, C_LN2:C_LN2 + 1]),
            reads=[rSC2, rC], writes=[rSC2])
        dve(lambda e: e.tensor_scalar(SC2[:], SC2[:], 64.0, 0.5, ALU.mult, ALU.add), reads=[rSC2], writes=[rSC2])
        floorc(SC2[:], rSC2)
        dve(lambda e: e.tensor_tensor(SC1[:], SC1[:], SC2[:], ALU.mult), reads=[rSC1, rSC2], writes=[rSC1])
        dve(lambda e: e.tensor_scalar(SC1[:], SC1[:], 1.0 / 64.0, 0.5, ALU.mult, ALU.add),
            reads=[rSC1], writes=[rSC1])
        floorc2(SC1[:], rSC1)
        dve(lambda e: e.reciprocal(SC2[:], SC2[:]), reads=[rSC2], writes=[rSC2])
        dve(lambda e: e.tensor_tensor(SC1[:], SC1[:], SC2[:], ALU.mult), reads=[rSC1, rSC2], writes=[rSC1])
        dve(lambda e, c_=c, b_=b: e.tensor_scalar(XNs(c_, b_), SC1[:], 64.0, None, ALU.mult),
            reads=[rSC1], writes=[rXN[c][b]])

    for seq in range(n_seq):
        for c in range(NCH):
            dma("sp", X[:, c * T:(c + 1) * T], dr["x_fm"][seq * D + c * 128: seq * D + (c + 1) * 128, :],
                writes=[rX[c][b] for b in range(NQB)])

        if "l0a" not in phases:
            continue
        dma("gp", A4[:, 0:1024], dr["w0q"], writes=[rA4[0]])
        dma("gp", A4[:, 1024:2048], dr["w0k"], writes=[rA4[1]])
        dma("gp", A4[:, 2048:3072], dr["w0v"], writes=[rA4[2]])
        dma("gp", A4[:, 3072:4096], dr["w0o"], writes=[rA4[3]])
        ln_pass()
        for b in range(NQB):
            xnres = [rXN[c][b] for c in range(NCH)]
            mm([(PS[0][:, 0:QBS], A4[:, ic * 128:(ic + 1) * 128], XNs(ic, b), ic == 0, ic == NCH - 1, None)
                for ic in range(NCH)], reads=[rA4[0]] + xnres, writes=[rPS[0]])
            act(lambda e, b_=b: e.activation(A0[:, b_ * QBS:(b_ + 1) * QBS], PS[0][:, 0:QBS], AF.Identity,
                                             scale=CST[:, C_S0:C_S0 + 1]),
                reads=[rPS[0], rC], writes=[rA0[0]])
            mm([(PS[1][:, 0:QBS], A4[:, 1024 + ic * 128:1024 + (ic + 1) * 128], XNs(ic, b), ic == 0,
                 ic == NCH - 1, None) for ic in range(NCH)], reads=[rA4[1]] + xnres, writes=[rPS[1]])
            act(lambda e, b_=b: e.activation(A1[:, b_ * QBS:(b_ + 1) * QBS], PS[1][:, 0:QBS], AF.Identity),
                reads=[rPS[1]], writes=[rA1[0]])
            for tt in range(4):
                tta = b * 4 + tt
                mm([(PS[2][:, 0:128], XNt(ic, tta), A4[:, 2048 + ic * 128:2048 + (ic + 1) * 128],
                     ic == 0, ic == NCH - 1, None) for ic in range(NCH)],
                   reads=[rA4[2]] + xnres, writes=[rPS[2]])
                act(lambda e, t_=tta: e.activation(A2[:, t_ * 128:(t_ + 1) * 128], PS[2][:, 0:128], AF.Identity),
                    reads=[rPS[2]], writes=[rA2[tta // 4]])
        dve(lambda e: e.memset(SCL[0:1, 7:8], 0.0), reads=[rSCL], writes=[rSCL])
        for b in range(NQB):
            act(lambda e, b_=b: e.activation(SC2[:], A1[:, b_ * QBS:(b_ + 1) * QBS], AF.Square),
                reads=[rA1[0]], writes=[rSC2])
            mm([(PS[4][0:1, 0:QBS], ONEC[:], SC2[:], True, True, None)], reads=[rC, rSC2], writes=[rPS[4]])
            dve(lambda e: e.tensor_reduce(SC1[0:1, 0:1], PS[4][0:1, 0:QBS], mybir.AxisListType.X, ALU.max),
                reads=[rPS[4]], writes=[rSC1])
            dve(lambda e: e.tensor_tensor(SCL[0:1, 7:8], SCL[0:1, 7:8], SC1[0:1, 0:1], ALU.max),
                reads=[rSC1, rSCL], writes=[rSCL])
        for qb in range(NQB):
            qsl = slice(qb * QBS, (qb + 1) * QBS)
            nkc = 4 * (qb + 1)
            act(lambda e, q_=qsl: e.activation(SC2[:], A0[:, q_], AF.Square), reads=[rA0[0]], writes=[rSC2])
            mm([(PS[4][0:1, 0:QBS], ONEC[:], SC2[:], True, True, None)], reads=[rC, rSC2], writes=[rPS[4]])
            dve(lambda e: e.tensor_scalar(MR[0:1, :], PS[4][0:1, 0:QBS], SCL[0:1, 7:8], None, ALU.mult),
                reads=[rPS[4], rSCL], writes=[rMR])
            act(lambda e: e.activation(MR[0:1, :], MR[0:1, :], AF.Sqrt), reads=[rMR], writes=[rMR])
            dve(lambda e: e.tensor_scalar(MR[0:1, :], MR[0:1, :], -1.0, None, ALU.mult),
                reads=[rMR], writes=[rMR])

            def s_tile0(kc, qsl_=qsl):
                bank = 0 if (kc % 2 == 0) else 1
                mm([(PS[bank][:, 0:QBS], ONES[0:1, 0:128], MR[0:1, :], True, False, None),
                    (PS[bank][:, 0:QBS], A1[:, kc * 128:(kc + 1) * 128], A0[:, qsl_], False, True, None)],
                   reads=[rC, rMR, rA1[0], rA0[0]], writes=[rPS[bank]])
                return bank
            prev = s_tile0(0)
            for kc in range(nkc):
                bank = prev
                if kc + 1 < nkc:
                    prev = s_tile0(kc + 1)
                if kc >= 4 * qb:
                    dpos = kc - 4 * qb
                    dve(lambda e, bk=bank, d_=dpos: e.tensor_tensor(
                        PS[bk][:, 0:QBS], PS[bk][:, 0:QBS], MSK[:, d_ * QBS:(d_ + 1) * QBS], ALU.add),
                        reads=[rPS[bank], rMSK], writes=[rPS[bank]])
                pt = kc % 2
                ptap = A3[:, 2048 + pt * QBS: 2048 + (pt + 1) * QBS]
                act(lambda e, bk=bank, a_=ptap: e.activation(a_, PS[bk][:, 0:QBS], AF.Exp),
                    reads=[rPS[bank]], writes=[rA3[4 + pt]])
                mm([(PS[2][0:1, 0:QBS], ONESB[:, 0:1], ptap, kc == 0, kc == nkc - 1, None)],
                   reads=[rC, rA3[4 + pt]], writes=[rPS[2]])
                mm([(PS[3][:, 0:QBS], A2[:, kc * 128:(kc + 1) * 128], ptap, kc == 0, kc == nkc - 1, None)],
                   reads=[rA2[kc // 4], rA3[4 + pt]], writes=[rPS[3]])
            dve(lambda e: e.reciprocal(ARow, PS[2][0:1, 0:QBS]), reads=[rPS[2]], writes=[rRR])
            mm([(PS[4][:, 0:QBS], ONES[0:1, 0:128], ARow, True, True, None)],
               reads=[rC, rRR], writes=[rPS[4]])
            act(lambda e: e.activation(TBS[:], PS[4][:, 0:QBS], AF.Identity), reads=[rPS[4]], writes=[rTBS])
            dve(lambda e: e.tensor_tensor(A3[:, 0:QBS], PS[3][:, 0:QBS], TBS[:], ALU.mult),
                reads=[rPS[3], rTBS], writes=[rA3[0]])
            for dm in range(NCH):
                bank = 4 + (dm % 2)
                mm([(PS[bank][:, 0:QBS], A4[:, 3072 + dm * 128:3072 + (dm + 1) * 128], A3[:, 0:QBS],
                     True, True, None)], reads=[rA4[3], rA3[0]], writes=[rPS[bank]])
                dve(lambda e, d_=dm, bk=bank, q_=qb: e.tensor_tensor(Xs(d_, q_), Xs(d_, q_),
                                                                     PS[bk][:, 0:QBS], ALU.add),
                    reads=[rPS[bank], rX[dm][qb]], writes=[rX[dm][qb]])

        if debug:
            for c in range(NCH):
                dma("sp", dbg["dbg0a"][seq * D + c * 128: seq * D + (c + 1) * 128, :], X[:, c * T:(c + 1) * T],
                    reads=[rX[c][b] for b in range(NQB)])
        if "l0f" not in phases:
            continue
        dma("gp", A0[:, :], dr["w8T"], writes=[rA0[0], rA0[1], rA0[2], rA0[3]])
        ln_pass(quant=True)
        for b in range(NQB):
            xnres = [rXN[c][b] for c in range(NCH)]
            for oc in range(NCH):
                bank = oc % 2
                ms = [(PS[bank][:, 0:QBS], B0R[0:1, oc * 128:(oc + 1) * 128], ONESB[0:1, 0:QBS], True, False, None)]
                ms += [(PS[bank][:, 0:QBS], A0[:, ic * 1024 + oc * 128: ic * 1024 + (oc + 1) * 128],
                        XNs(ic, b), False, ic == NCH - 1, None) for ic in range(NCH)]
                mm(ms, reads=[rROWS, rC, rA0[0], rA0[1], rA0[2], rA0[3]] + xnres, writes=[rPS[bank]])
                dve(lambda e, o_=oc, b_=b, bk=bank: e.scalar_tensor_tensor(
                    Xs(o_, b_), PS[bk][:, 0:QBS], CST[:, C_SS:C_SS + 1], Xs(o_, b_), ALU.mult, ALU.add),
                    reads=[rPS[bank], rC, rX[oc][b]], writes=[rX[oc][b]])

        if debug:
            for c in range(NCH):
                dma("sp", dbg["dbg0f"][seq * D + c * 128: seq * D + (c + 1) * 128, :], X[:, c * T:(c + 1) * T],
                    reads=[rX[c][b] for b in range(NQB)])
        if "l1a" not in phases:
            continue
        ln_pass()
        for g in range(NGRP):
            dma("gp", A4[:, 0:4096], dr["w1q"][g * 128:(g + 1) * 128, :],
                writes=[rA4[0], rA4[1], rA4[2], rA4[3]])
            dma("gp", A4[:, 4096:8192], dr["w1k"][g * 128:(g + 1) * 128, :],
                writes=[rA4[4], rA4[5], rA4[6], rA4[7]])
            for cc in range(CPG):
                for b in range(NQB):
                    xnres = [rXN[c][b] for c in range(NCH)]
                    mm([(PS[0][:, 0:QBS], A4[:, ic * 512 + cc * 128: ic * 512 + (cc + 1) * 128],
                         XNs(ic, b), ic == 0, ic == NCH - 1, None) for ic in range(NCH)],
                       reads=[rA4[0], rA4[1], rA4[2], rA4[3]] + xnres, writes=[rPS[0]])
                    act(lambda e, c_=cc, b_=b, g_=g: e.activation(
                        A0[:, c_ * T + b_ * QBS: c_ * T + (b_ + 1) * QBS], PS[0][:, 0:QBS], AF.Identity,
                        bias=BQC[:, g_ * 4 + c_: g_ * 4 + c_ + 1], scale=CST[:, C_S1:C_S1 + 1]),
                        reads=[rPS[0], rROWS, rC], writes=[rA0[cc]])
                    mm([(PS[1][:, 0:QBS], A4[:, 4096 + ic * 512 + cc * 128: 4096 + ic * 512 + (cc + 1) * 128],
                         XNs(ic, b), ic == 0, ic == NCH - 1, None) for ic in range(NCH)],
                       reads=[rA4[4], rA4[5], rA4[6], rA4[7]] + xnres, writes=[rPS[1]])
                    act(lambda e, c_=cc, b_=b, g_=g: e.activation(
                        A1[:, c_ * T + b_ * QBS: c_ * T + (b_ + 1) * QBS], PS[1][:, 0:QBS], AF.Identity,
                        bias=BKC[:, g_ * 4 + c_: g_ * 4 + c_ + 1]),
                        reads=[rPS[1], rROWS], writes=[rA1[cc]])
            dma("gp", A4[:, 0:4096], dr["w1v"][g * 128:(g + 1) * 128, :],
                writes=[rA4[0], rA4[1], rA4[2], rA4[3]])
            for tt in range(16):
                b = tt // 4
                xnres = [rXN[c][b] for c in range(NCH)]
                ms = [(PS[2][:, 0:QBS], ONESB[0:1, 0:128], BVR[0:1, g * QBS:(g + 1) * QBS], True, False, None)]
                ms += [(PS[2][:, 0:QBS], XNt(ic, tt), A4[:, ic * 512:(ic + 1) * 512], False, ic == NCH - 1, None)
                       for ic in range(NCH)]
                mm(ms, reads=[rC, rROWS, rA4[0], rA4[1], rA4[2], rA4[3]] + xnres, writes=[rPS[2]])
                act(lambda e, t_=tt: e.activation(A2[:, t_ * QBS:(t_ + 1) * QBS], PS[2][:, 0:QBS], AF.Identity),
                    reads=[rPS[2]], writes=[rA2[tt]])
            for (buf, rbuf) in ((A0, rA0), (A1, rA1)):
                for cc in range(CPG):
                    csl = slice(cc * T, (cc + 1) * T)
                    for (dst0, src0) in ((0, 32), (32, 0), (64, 96), (96, 64)):
                        dma("sp", A3[dst0:dst0 + 32, 2048:4096], buf[src0:src0 + 32, csl],
                            reads=[rbuf[cc]], writes=list(rA3[4:8]))
                    dve(lambda e, b_=buf, c_=csl: e.tensor_tensor(A3[:, 0:2048], b_[:, c_], TT1[:], ALU.mult),
                        reads=[rbuf[cc], rTT], writes=list(rA3[0:4]))
                    dve(lambda e: e.tensor_tensor(A3[:, 2048:4096], A3[:, 2048:4096], TT2[:], ALU.mult),
                        reads=list(rA3[4:8]) + [rTT], writes=list(rA3[4:8]))
                    dve(lambda e, b_=buf, c_=csl: e.tensor_tensor(b_[:, c_], A3[:, 0:2048], A3[:, 2048:4096],
                                                                  ALU.add),
                        reads=list(rA3[0:8]), writes=[rbuf[cc]])
            dma("gp", A4[:, 4096:8192], dr["w1o"][g * 128:(g + 1) * 128, :],
                writes=[rA4[4], rA4[5], rA4[6], rA4[7]])
            dve(lambda e: e.memset(SCL[0:1, 7:8], 0.0), reads=[rSCL], writes=[rSCL])
            for b in range(NQB):
                for cc in range(CPG):
                    act(lambda e, c_=cc, b_=b: e.activation(SC2[:], A1[:, c_ * T + b_ * QBS: c_ * T + (b_ + 1) * QBS],
                                                            AF.Square),
                        reads=[rA1[cc]], writes=[rSC2])
                    mm([(PS[4][0:1, 0:QBS], ONEC[:], SC2[:], cc == 0, cc == CPG - 1, None)],
                       reads=[rC, rSC2], writes=[rPS[4]])
                dve(lambda e: e.tensor_reduce(SC1[0:1, 0:1], PS[4][0:1, 0:QBS], mybir.AxisListType.X, ALU.max),
                    reads=[rPS[4]], writes=[rSC1])
                dve(lambda e: e.tensor_tensor(SCL[0:1, 7:8], SCL[0:1, 7:8], SC1[0:1, 0:1], ALU.max),
                    reads=[rSC1, rSCL], writes=[rSCL])
            for qb in range(NQB):
                nkc = 4 * (qb + 1)
                for cc in range(CPG):
                    qsl = slice(cc * T + qb * QBS, cc * T + (qb + 1) * QBS)
                    act(lambda e, q_=qsl: e.activation(SC2[:], A0[:, q_], AF.Square), reads=[rA0[cc]], writes=[rSC2])
                    mm([(PS[4][0:1, 0:QBS], INDA[:], SC2[:], True, True, None),
                        (PS[4][32:33, 0:QBS], INDB[:], SC2[:], True, True, None)],
                       reads=[rC, rSC2], writes=[rPS[4]])
                    for r0 in (0, 32):
                        dve(lambda e, r_=r0: e.tensor_scalar(MR[r_:r_ + 1, :], PS[4][r_:r_ + 1, 0:QBS],
                                                             SCL[0:1, 7:8], None, ALU.mult),
                            reads=[rPS[4], rSCL], writes=[rMR])
                        act(lambda e, r_=r0: e.activation(MR[r_:r_ + 1, :], MR[r_:r_ + 1, :], AF.Sqrt),
                            reads=[rMR], writes=[rMR])
                        dve(lambda e, r_=r0: e.tensor_scalar(MR[r_:r_ + 1, :], MR[r_:r_ + 1, :],
                                                             -1.0, None, ALU.mult),
                            reads=[rMR], writes=[rMR])

                    def s_tile1(kc, cc_=cc, qsl_=qsl):
                        bA = 0 if (kc % 2 == 0) else 6
                        bB = 1 if (kc % 2 == 0) else 7
                        ksl = slice(cc_ * T + kc * 128, cc_ * T + (kc + 1) * 128)
                        mm([(PS[bA][:, 0:QBS], ONES[0:1, 0:128], MR[0:1, :], True, False, None),
                            (PS[bA][:, 0:QBS], A1[0:64, ksl], A0[0:64, qsl_], False, True, None)],
                           reads=[rC, rMR, rA1[cc_], rA0[cc_]], writes=[rPS[bA]])
                        mm([(PS[bB][:, 0:QBS], ONES[32:33, 0:128], MR[32:33, :], True, False, None),
                            (PS[bB][:, 0:QBS], A1[64:128, ksl], A0[64:128, qsl_], False, True, None)],
                           reads=[rC, rMR, rA1[cc_], rA0[cc_]], writes=[rPS[bB]])
                        return bA, bB
                    prev = s_tile1(0)
                    for kc in range(nkc):
                        bA, bB = prev
                        if kc + 1 < nkc:
                            prev = s_tile1(kc + 1)
                        if kc >= 4 * qb:
                            dpos = kc - 4 * qb
                            for bk in (bA, bB):
                                dve(lambda e, b_=bk, d_=dpos: e.tensor_tensor(
                                    PS[b_][:, 0:QBS], PS[b_][:, 0:QBS], MSK[:, d_ * QBS:(d_ + 1) * QBS], ALU.add),
                                    reads=[rPS[bk], rMSK], writes=[rPS[bk]])
                        ptA, ptB = (kc % 2) * 2, (kc % 2) * 2 + 1
                        apA = A3[:, 2048 + ptA * QBS: 2048 + (ptA + 1) * QBS]
                        apB = A3[:, 2048 + ptB * QBS: 2048 + (ptB + 1) * QBS]
                        act(lambda e, b_=bA, a_=apA: e.activation(a_, PS[b_][:, 0:QBS], AF.Exp),
                            reads=[rPS[bA]], writes=[rA3[4 + ptA]])
                        act(lambda e, b_=bB, a_=apB: e.activation(a_, PS[b_][:, 0:QBS], AF.Exp),
                            reads=[rPS[bB]], writes=[rA3[4 + ptB]])
                        mm([(PS[2][0:1, 0:QBS], ONESB[:, 0:1], apA, kc == 0, kc == nkc - 1, None),
                            (PS[2][32:33, 0:QBS], ONESB[:, 0:1], apB, kc == 0, kc == nkc - 1, None, True)],
                           reads=[rC, rA3[4 + ptA], rA3[4 + ptB]], writes=[rPS[2]])
                        lA = 2 * cc * DH
                        lB = (2 * cc + 1) * DH
                        mm([(PS[3][0:64, 0:QBS], A2[:, kc * QBS + lA: kc * QBS + lA + DH], apA,
                             kc == 0, kc == nkc - 1, (0, 0)),
                            (PS[3][64:128, 0:QBS], A2[:, kc * QBS + lB: kc * QBS + lB + DH], apB,
                             kc == 0, kc == nkc - 1, (0, 64), True)],
                           reads=[rA2[kc], rA3[4 + ptA], rA3[4 + ptB]], writes=[rPS[3]])
                    dve(lambda e: e.reciprocal(ARow, PS[2][0:1, 0:QBS]), reads=[rPS[2]], writes=[rRR])
                    dve(lambda e: e.reciprocal(BRow, PS[2][32:33, 0:QBS]), reads=[rPS[2]], writes=[rRR])
                    mm([(PS[4][0:64, 0:QBS], ONES[0:1, 0:64], ARow, True, True, (0, 0)),
                        (PS[4][64:128, 0:QBS], ONES[0:1, 0:64], BRow, True, True, (0, 64))],
                       reads=[rC, rRR], writes=[rPS[4]])
                    act(lambda e: e.activation(TBS[:], PS[4][:, 0:QBS], AF.Identity), reads=[rPS[4]], writes=[rTBS])
                    dve(lambda e, c_=cc: e.tensor_tensor(A3[:, c_ * QBS:(c_ + 1) * QBS], PS[3][:, 0:QBS],
                                                         TBS[:], ALU.mult),
                        reads=[rPS[3], rTBS], writes=[rA3[cc]])
                for dm in range(NCH):
                    bank = 5 if (dm % 2 == 0) else 6
                    ms = []
                    if g == 0:
                        ms.append((PS[bank][:, 0:QBS], BOR[0:1, dm * 128:(dm + 1) * 128], ONESB[0:1, 0:QBS],
                                   True, False, None))
                    ms += [(PS[bank][:, 0:QBS], A4[:, 4096 + cc * 1024 + dm * 128: 4096 + cc * 1024 + (dm + 1) * 128],
                            A3[:, cc * QBS:(cc + 1) * QBS], (g != 0 and cc == 0), cc == CPG - 1, None)
                           for cc in range(CPG)]
                    mm(ms, reads=[rROWS, rC, rA4[4], rA4[5], rA4[6], rA4[7]] + list(rA3[0:4]), writes=[rPS[bank]])
                    dve(lambda e, d_=dm, q_=qb, bk=bank: e.tensor_tensor(Xs(d_, q_), Xs(d_, q_),
                                                                         PS[bk][:, 0:QBS], ALU.add),
                        reads=[rPS[bank], rX[dm][qb]], writes=[rX[dm][qb]])

        if debug:
            for c in range(NCH):
                dma("sp", dbg["dbg1a"][seq * D + c * 128: seq * D + (c + 1) * 128, :], X[:, c * T:(c + 1) * T],
                    reads=[rX[c][b] for b in range(NQB)])
        if "l1f" not in phases:
            continue
        ln_pass()
        for fs in range(NFS):
            dma("gp", A0[:, :], dr["w1f"][fs * 128:(fs + 1) * 128, :], writes=[rA0[0], rA0[1], rA0[2], rA0[3]])
            dma("gp", A1[:, :], dr["w2f"][fs * 128:(fs + 1) * 128, :], writes=[rA1[0], rA1[1], rA1[2], rA1[3]])
            wread = [rA0[0], rA0[1], rA0[2], rA0[3]]
            w2read = [rA1[0], rA1[1], rA1[2], rA1[3]]
            for b in range(NQB):
                xnres = [rXN[c][b] for c in range(NCH)]
                for hc in range(NCH):
                    bank = hc % 2
                    mm([(PS[bank][:, 0:QBS], A0[:, ic * 1024 + hc * 128: ic * 1024 + (hc + 1) * 128],
                         XNs(ic, b), ic == 0, ic == NCH - 1, None) for ic in range(NCH)],
                       reads=wread + xnres, writes=[rPS[bank]])
                    act(lambda e, h_=hc, f_=fs, bk=bank: e.activation(
                        A2[:, h_ * QBS:(h_ + 1) * QBS], PS[bk][:, 0:QBS], AF.Gelu,
                        bias=B1C[:, f_ * 8 + h_: f_ * 8 + h_ + 1]),
                        reads=[rPS[bank], rROWS], writes=[rA2[hc]])
                for dm in range(NCH):
                    bank = 2 + (dm % 2)
                    ms = []
                    if fs == 0:
                        ms.append((PS[bank][:, 0:QBS], B2R[0:1, dm * 128:(dm + 1) * 128], ONESB[0:1, 0:QBS],
                                   True, False, None))
                    ms += [(PS[bank][:, 0:QBS], A1[:, hc * 1024 + dm * 128: hc * 1024 + (dm + 1) * 128],
                            A2[:, hc * QBS:(hc + 1) * QBS], (fs != 0 and hc == 0), hc == NCH - 1, None)
                           for hc in range(NCH)]
                    mm(ms, reads=[rROWS, rC] + w2read + [rA2[hc] for hc in range(NCH)], writes=[rPS[bank]])
                    dve(lambda e, d_=dm, b_=b, bk=bank: e.tensor_tensor(Xs(d_, b_), Xs(d_, b_),
                                                                        PS[bk][:, 0:QBS], ALU.add),
                        reads=[rPS[bank], rX[dm][b]], writes=[rX[dm][b]])

        for c in range(NCH):
            dma("sp", out[seq * D + c * 128: seq * D + (c + 1) * 128, :], X[:, c * T:(c + 1) * T],
                reads=[rX[c][b] for b in range(NQB)])

    with ctx:
        sems = {e: ctx.enter_context(nc.semaphore("sem_" + e)) for e in ("te", "act", "dve")}
        qsems = {q: [ctx.enter_context(nc.semaphore("sem_%s%d" % (q, i))) for i in range(QDEPTH)]
                 for q in QUEUES}

        def do_wait(eng, pe, pt):
            if pe in QUEUES:
                eng.wait_ge(qsems[pe][(pt - 1) % QDEPTH], 16 * ((pt - 1) // QDEPTH + 1))
            else:
                eng.wait_ge(sems[pe], pt)

        def run_queue(eng, q):
            for i, (waits, fn) in enumerate(s.stream[q]):
                for pe, pt in waits:
                    do_wait(eng, pe, pt)
                fn(eng).then_inc(qsems[q][i % QDEPTH], 16)
            for t in range(max(0, s.tick[q] - QDEPTH), s.tick[q]):
                do_wait(eng, q, t + 1)

        def run_eng(eng, e):
            for waits, fn in s.stream[e]:
                for pe, pt in waits:
                    do_wait(eng, pe, pt)
                fn(eng).then_inc(sems[e], 1)

        with nc.Block() as block:
            @block.sync
            def _(sync):
                run_queue(sync, "sp")

            @block.gpsimd
            def _(gp):
                run_queue(gp, "gp")

            @block.tensor
            def _(te):
                run_eng(te, "te")

            @block.scalar
            def _(sc):
                run_eng(sc, "act")

            @block.vector
            def _(ve):
                run_eng(ve, "dve")
    n_inst = sum(len(v) for v in s.stream.values())
    return nc, n_inst


def kernel(**inputs):
    L = host_layout(inputs)
    nc, n_inst = build_nc()
    in_maps = []
    for core in range(8):
        m = {nm: L[nm] for nm, _ in DRAM_SPECS if nm != "x_fm"}
        m["x_fm"] = np.ascontiguousarray(L["x_fm"][core * 2 * D:(core + 1) * 2 * D, :])
        in_maps.append(m)
    res = run_bass_kernel_spmd(nc, in_maps, core_ids=list(range(8)))
    outs = []
    for core in range(8):
        o = np.asarray(res.results[core]["out_fm"]).reshape(2, D, T)
        outs.append(np.transpose(o, (0, 2, 1)))
    return np.concatenate(outs, 0)


# version 05
# speedup vs baseline: 1.4689x; 1.4689x over previous
import contextlib
import numpy as np
import concourse.bass as bass
import concourse.mybir as mybir
from concourse.bass_utils import run_bass_kernel_spmd

F32 = mybir.dt.float32
BF16 = mybir.dt.bfloat16
I32 = mybir.dt.int32
ALU = mybir.AluOpType
AF = mybir.ActivationFunctionType

T = 2048
D = 1024
NCH = 8
QBS = 512
NQB = T // QBS
H = 16
DH = 64
NGRP = 2
CPG = 4
FF = 4096
NFS = 4

LN2 = float(np.float32(np.log(2.0)))

ENGS = ("te", "act", "dve", "sp", "gp")
STEP = {"te": 1, "act": 1, "dve": 1, "sp": 16, "gp": 16}


class Res:
    __slots__ = ("w", "r")

    def __init__(self):
        self.w = None
        self.r = []


QUEUES = ("sp", "gp")
QDEPTH = 6


class Sched:

    def __init__(self):
        self.stream = {e: [] for e in ENGS}
        self.tick = {e: 0 for e in ENGS}
        self.seen = {e: {} for e in ENGS}

    def _covered(self, e, pe, pt):
        key = (pe, (pt - 1) % QDEPTH) if pe in QUEUES else pe
        return self.seen[e].get(key, 0) >= pt

    def _cover(self, e, pe, pt):
        key = (pe, (pt - 1) % QDEPTH) if pe in QUEUES else pe
        if self.seen[e].get(key, 0) < pt:
            self.seen[e][key] = pt

    def emit(self, e, fn, reads=(), writes=()):
        need = {}

        def add(pe, pt):
            key = (pe, (pt - 1) % QDEPTH) if pe in QUEUES else pe
            if pt > need.get(key, (None, 0))[1]:
                need[key] = (pe, pt)

        for r in reads:
            if r.w is not None:
                add(*r.w)
        for w in writes:
            if w.w is not None:
                add(*w.w)
            for pe, pt in w.r:
                add(pe, pt)
        waits = []
        self.tick[e] += 1
        t = self.tick[e]
        if e in QUEUES and t > QDEPTH:
            pw = t - QDEPTH
            if not self._covered(e, e, pw):
                waits.append((e, pw))
                self._cover(e, e, pw)
        for key in sorted(need, key=str):
            pe, pt = need[key]
            if pe == "te" and e == "te":
                continue
            if self._covered(e, pe, pt):
                continue
            waits.append((pe, pt))
            self._cover(e, pe, pt)
        for r in reads:
            r.r.append((e, t))
        for w in writes:
            w.w = (e, t)
            w.r = []
        self.stream[e].append((waits, fn))


def host_layout(inputs):
    L = {}
    x = np.asarray(inputs["L0_attn_x"], dtype=np.float32)
    L["x_fm"] = np.ascontiguousarray(np.transpose(x, (0, 2, 1))).reshape(16 * D, T)

    def img(w):
        w = np.asarray(w, dtype=np.float32)
        k, n = w.shape
        return np.ascontiguousarray(w.reshape(k // 128, 128, n).transpose(1, 0, 2).reshape(128, (k // 128) * n))

    L["w0q"] = img(inputs["L0_attn_Wq"])
    L["w0k"] = img(inputs["L0_attn_Wk"])
    L["w0v"] = img(inputs["L0_attn_Wv"])
    L["w0o"] = np.ascontiguousarray(np.asarray(inputs["L0_attn_Wo"], dtype=np.float32))
    L["w8T"] = img(np.asarray(inputs["L0_ffn_weight_fp8"]).T)
    L["b0row"] = np.asarray(inputs["L0_ffn_bias"], dtype=np.float32).reshape(1, D)
    L["s_in"] = np.asarray(inputs["L0_ffn_input_scale"], dtype=np.float32).reshape(1, 1)
    L["s_w"] = np.asarray(inputs["L0_ffn_weight_scale"], dtype=np.float32).reshape(1, 1)

    perm = np.zeros(D, dtype=np.int64)
    for h in range(H):
        b = h * DH
        perm[b:b + 32] = b + np.arange(0, DH, 2)
        perm[b + 32:b + DH] = b + 1 + np.arange(0, DH, 2)
    wq = np.asarray(inputs["L1_attn_Wq"]).T[:, perm]
    wk = np.asarray(inputs["L1_attn_Wk"]).T[:, perm]
    L["w1q"] = np.concatenate([img(wq[:, g * 512:(g + 1) * 512]) for g in range(NGRP)], 0)
    L["w1k"] = np.concatenate([img(wk[:, g * 512:(g + 1) * 512]) for g in range(NGRP)], 0)
    wv = np.asarray(inputs["L1_attn_Wv"]).T
    L["w1v"] = np.concatenate([img(wv[:, g * 512:(g + 1) * 512]) for g in range(NGRP)], 0)
    wo = np.asarray(inputs["L1_attn_Wo"]).T
    L["w1o"] = np.concatenate([img(wo[g * 512:(g + 1) * 512, :]) for g in range(NGRP)], 0)
    bq = np.asarray(inputs["L1_attn_bq"])[perm]
    bk = np.asarray(inputs["L1_attn_bk"])[perm]
    L["bqcol"] = np.ascontiguousarray(bq.reshape(NCH, 128).T.astype(np.float32))
    L["bkcol"] = np.ascontiguousarray(bk.reshape(NCH, 128).T.astype(np.float32))
    L["bvrow"] = np.asarray(inputs["L1_attn_bv"], dtype=np.float32).reshape(1, D)
    L["borow"] = np.asarray(inputs["L1_attn_bo"], dtype=np.float32).reshape(1, D)

    w1f = np.asarray(inputs["L1_ffn_w1"])
    L["w1f"] = np.concatenate([img(w1f[:, s * 1024:(s + 1) * 1024]) for s in range(NFS)], 0)
    w2f = np.asarray(inputs["L1_ffn_w2"])
    L["w2f"] = np.concatenate([img(w2f[s * 1024:(s + 1) * 1024, :]) for s in range(NFS)], 0)
    L["b1col"] = np.ascontiguousarray(np.asarray(inputs["L1_ffn_b1"]).reshape(32, 128).T.astype(np.float32))
    L["b2row"] = np.asarray(inputs["L1_ffn_b2"], dtype=np.float32).reshape(1, D)
    L["iota"] = np.arange(T, dtype=np.float32).reshape(1, T)
    try:
        import jax
        import jax.numpy as jnp
        with jax.default_device(jax.local_devices(backend="cpu")[0]):
            theta_j = 10000.0 ** (-jnp.arange(0, DH, 2, dtype=jnp.float32) / DH)
            seq_j = jnp.arange(T, dtype=jnp.float32)
            fr_j = jnp.einsum('i,j->ij', seq_j, theta_j)
            sn = np.asarray(jnp.sin(fr_j)).astype(np.float32).T
            cs = np.asarray(jnp.cos(fr_j)).astype(np.float32).T
    except Exception:
        theta = (10000.0 ** (-np.arange(0, DH, 2, dtype=np.float32) / DH)).astype(np.float32)
        fr = np.einsum('i,j->ij', np.arange(T, dtype=np.float32), theta).astype(np.float32)
        sn = np.sin(fr).astype(np.float32).T
        cs = np.cos(fr).astype(np.float32).T
    L["tt1"] = np.ascontiguousarray(np.tile(cs, (4, 1)))
    L["tt2"] = np.ascontiguousarray(np.concatenate([-sn, sn, -sn, sn], 0))
    return L


DRAM_SPECS = [
    ("x_fm", [2 * D, T]), ("w0q", [128, 1024]), ("w0k", [128, 1024]), ("w0v", [128, 1024]),
    ("w0o", [128, 1024]), ("w8T", [128, 8 * 1024]), ("b0row", [1, D]), ("s_in", [1, 1]), ("s_w", [1, 1]),
    ("w1q", [256, 4096]), ("w1k", [256, 4096]), ("w1v", [256, 4096]), ("w1o", [256, 4096]),
    ("bqcol", [128, 8]), ("bkcol", [128, 8]), ("bvrow", [1, D]), ("borow", [1, D]),
    ("w1f", [512, 8192]), ("w2f", [512, 8192]), ("b1col", [128, 32]), ("b2row", [1, D]),
    ("iota", [1, T]), ("tt1", [128, T]), ("tt2", [128, T]),
]


def build_nc(n_seq=2, debug=False, phases=("l0a", "l0f", "l1a", "l1f")):
    nc = bass.Bass("TRN2", target_bir_lowering=False, debug=False)
    dr = {nm: nc.dram_tensor(nm, shp, F32, kind="ExternalInput").ap() for nm, shp in DRAM_SPECS}
    out = nc.dram_tensor("out_fm", [n_seq * D, T], F32, kind="ExternalOutput").ap()
    dbg = {}
    if debug:
        for nm in ("dbg0a", "dbg0f", "dbg1a"):
            dbg[nm] = nc.dram_tensor(nm, [n_seq * D, T], F32, kind="ExternalOutput").ap()

    ctx = contextlib.ExitStack()
    sb = lambda nm, shp, dt: ctx.enter_context(nc.sbuf_tensor(nm, shp, dt))

    X = sb("X", [128, NCH * T], F32)
    XN = sb("XN", [128, NCH * T], BF16)
    A0 = sb("A0", [128, 8192], BF16)
    A1 = sb("A1", [128, 8192], BF16)
    A2 = sb("A2", [128, 8192], BF16)
    A3 = sb("A3", [128, 4096], BF16)
    A4 = sb("A4", [128, 8192], BF16)
    TT1 = sb("TT1", [128, T], BF16)
    TT2 = sb("TT2", [128, T], BF16)
    MSK = sb("MSK", [128, 4 * QBS], BF16)
    ARB = sb("ARB", [1, 1024], BF16)
    MHC = sb("MHC", [128, 16], F32)
    QMX = sb("QMX", [1, 32], F32)
    RR = sb("RR", [1, 1024], F32)
    TBS = sb("TBS", [128, 512], F32)
    SC1 = sb("SC1", [128, 512], F32)
    SC2 = sb("SC2", [128, 512], F32)
    SI = sb("SI", [128, 512], I32)
    SC4 = sb("SC4", [128, 512], F32)
    B0R = sb("B0R", [1, D], BF16)
    BVR = sb("BVR", [1, D], BF16)
    BOR = sb("BOR", [1, D], BF16)
    B2R = sb("B2R", [1, D], BF16)
    BQC = sb("BQC", [128, NCH], F32)
    BKC = sb("BKC", [128, NCH], F32)
    B1C = sb("B1C", [128, 32], F32)
    ONES = sb("ONES", [33, 128], F32)
    ONESB = sb("ONESB", [128, QBS], BF16)
    ONEC = sb("ONEC", [128, 1], F32)
    INDA = sb("INDA", [128, 1], F32)
    INDB = sb("INDB", [128, 1], F32)
    CST = sb("CST", [128, 8], F32)
    SCL = sb("SCL", [1, 16], F32)
    PS = [ctx.enter_context(nc.psum_tensor("PS%d" % i, [128, 512], F32)) for i in range(8)]

    C_LN2, C_3LN2, C_EPS, C_S1, C_SS, C_S0 = 0, 1, 2, 3, 4, 5

    s = Sched()
    R = lambda: Res()
    rX = [[R() for b in range(NQB)] for c in range(NCH)]
    rXN = [[R() for b in range(NQB)] for c in range(NCH)]
    rA0 = [R() for c in range(CPG)]
    rA1 = [R() for c in range(CPG)]
    rA2 = [R() for i in range(16)]
    rA3 = [R() for i in range(8)]
    rA4 = [R() for i in range(8)]
    rTT, rMSK = R(), R()
    rMH, rQM, rRR, rTBS, rARB = R(), R(), R(), R(), R()
    rSC1, rSC2, rSI, rSC4 = R(), R(), R(), R()
    rROWS, rC, rSCL = R(), R(), R()
    rPS = [R() for i in range(8)]

    def dma(q, outap, inap, reads=(), writes=()):
        s.emit(q, lambda e, o=outap, i=inap: e.dma_start(out=o, in_=i), reads=reads, writes=writes)

    def mm(ms, reads, writes):
        def fn(e, ms_=tuple(ms)):
            last = None
            for m in ms_:
                (o, l, r_, st, sp, tp), skip = m[:6], (len(m) > 6 and m[6])
                last = e.matmul(o, l, r_, start=st, stop=sp, tile_position=tp,
                                skip_group_check=bool(skip))
            return last
        s.emit("te", fn, reads=reads, writes=writes)

    def act(fn, reads, writes):
        s.emit("act", fn, reads=reads, writes=writes)

    def dve(fn, reads, writes):
        s.emit("dve", fn, reads=reads, writes=writes)

    def floorc(wap, rres):
        n = wap.shape[1]
        dve(lambda e: e.tensor_copy(SI[:, 0:n], wap), reads=[rres], writes=[rSI])
        dve(lambda e: e.tensor_tensor(TBS[:, 0:n], SI[:, 0:n], wap, ALU.is_gt),
            reads=[rSI, rres], writes=[rTBS])
        dve(lambda e: e.scalar_tensor_tensor(wap, TBS[:, 0:n], -1.0, SI[:, 0:n], ALU.mult, ALU.add),
            reads=[rTBS, rSI], writes=[rres])

    def floorc2(wap, rres):
        n = wap.shape[1]
        dve(lambda e: e.tensor_copy(SI[:, 0:n], wap), reads=[rres], writes=[rSI])
        dve(lambda e: e.tensor_tensor(TBS[:, 0:n], SI[:, 0:n], wap, ALU.is_gt),
            reads=[rSI, rres], writes=[rTBS])
        dve(lambda e: e.scalar_tensor_tensor(SC4[:, 0:n], SI[:, 0:n], -1.0, wap, ALU.add, ALU.is_gt),
            reads=[rSI, rres], writes=[rSC4])
        dve(lambda e: e.scalar_tensor_tensor(wap, SI[:, 0:n], 1.0, wap, ALU.add, ALU.is_le),
            reads=[rSI, rres], writes=[rres])
        dve(lambda e: e.tensor_tensor(wap, wap, TBS[:, 0:n], ALU.subtract),
            reads=[rres, rTBS], writes=[rres])
        dve(lambda e: e.tensor_tensor(wap, wap, SC4[:, 0:n], ALU.subtract),
            reads=[rres, rSC4], writes=[rres])
        dve(lambda e: e.tensor_tensor(wap, wap, SI[:, 0:n], ALU.add),
            reads=[rres, rSI], writes=[rres])

    Xs = lambda c, b: X[:, c * T + b * QBS: c * T + (b + 1) * QBS]
    XNs = lambda c, b: XN[:, c * T + b * QBS: c * T + (b + 1) * QBS]
    XNt = lambda c, tt: XN[:, c * T + tt * 128: c * T + (tt + 1) * 128]
    ARow = RR[0:1, 0:QBS]
    BRow = RR[0:1, QBS:2 * QBS]

    dma("gp", B0R[:], dr["b0row"], writes=[rROWS])
    dma("gp", BVR[:], dr["bvrow"], writes=[rROWS])
    dma("gp", BOR[:], dr["borow"], writes=[rROWS])
    dma("gp", B2R[:], dr["b2row"], writes=[rROWS])
    dma("sp", BQC[:], dr["bqcol"], writes=[rROWS])
    dma("sp", BKC[:], dr["bkcol"], writes=[rROWS])
    dma("sp", B1C[:], dr["b1col"], writes=[rROWS])
    dma("sp", SCL[:, 0:1], dr["s_in"], writes=[rSCL])
    dma("sp", SCL[:, 1:2], dr["s_w"], writes=[rSCL])

    def consts_fn(e):
        e.memset(ONES[:], 1.0)
        e.memset(ONESB[:], 1.0)
        e.memset(ONEC[:], 1.0)
        e.memset(INDA[0:64, :], 1.0)
        e.memset(INDA[64:128, :], 0.0)
        e.memset(INDB[0:64, :], 0.0)
        e.memset(INDB[64:128, :], 1.0)
        e.memset(CST[:, C_LN2:C_LN2 + 1], LN2)
        e.memset(CST[:, C_3LN2:C_3LN2 + 1], 3.0 * LN2)
        e.memset(CST[:, C_EPS:C_EPS + 1], 1e-5)
        e.memset(CST[:, C_S1:C_S1 + 1], 0.125)
        return e.memset(SCL[:, 8:9], 128.0)
    s.emit("dve", consts_fn, reads=[], writes=[rC, rSCL])

    dve(lambda e: e.reciprocal(SCL[:, 2:3], SCL[:, 0:1]), reads=[rSCL], writes=[rSCL])
    dve(lambda e: e.tensor_tensor(SCL[:, 3:4], SCL[:, 0:1], SCL[:, 1:2], ALU.mult), reads=[rSCL], writes=[rSCL])
    act(lambda e: e.activation(SCL[:, 6:7], SCL[:, 8:9], AF.Ln), reads=[rSCL], writes=[rSCL])
    dve(lambda e: e.tensor_scalar(SCL[:, 6:7], SCL[:, 6:7], -0.5, None, ALU.mult), reads=[rSCL], writes=[rSCL])
    act(lambda e: e.activation(SCL[:, 6:7], SCL[:, 6:7], AF.Exp), reads=[rSCL], writes=[rSCL])
    dve(lambda e: e.reciprocal(SCL[:, 4:5], SCL[:, 3:4]), reads=[rSCL], writes=[rSCL])
    dve(lambda e: e.tensor_scalar(B0R[:], B0R[:], SCL[:, 4:5], None, ALU.mult), reads=[rROWS, rSCL], writes=[rROWS])
    dve(lambda e: e.tensor_scalar(BQC[:], BQC[:], 0.125, None, ALU.mult), reads=[rROWS], writes=[rROWS])
    mm([(PS[7][:, 0:1], ONES[0:1, 0:128], SCL[0:1, 3:4], True, True, None)], reads=[rC, rSCL], writes=[rPS[7]])
    act(lambda e: e.activation(CST[:, C_SS:C_SS + 1], PS[7][:, 0:1], AF.Identity), reads=[rPS[7]], writes=[rC])
    mm([(PS[7][:, 0:1], ONES[0:1, 0:128], SCL[0:1, 6:7], True, True, None)], reads=[rC, rSCL], writes=[rPS[7]])
    act(lambda e: e.activation(CST[:, C_S0:C_S0 + 1], PS[7][:, 0:1], AF.Identity), reads=[rPS[7]], writes=[rC])

    dma("gp", TT1[:], dr["tt1"], writes=[rTT])
    dma("gp", TT2[:], dr["tt2"], writes=[rTT])

    dma("sp", BRow, dr["iota"][0:1, 0:QBS], writes=[rRR])
    dve(lambda e: e.memset(TBS[0:1, :], 1.0), reads=[], writes=[rTBS])
    for dpos in range(4):
        mm([(PS[6][:, 0:QBS], BRow[0:1, 0:128], TBS[0:1, :], True, False, None)],
           reads=[rRR, rC, rTBS], writes=[rPS[6]])
        dve(lambda e, d_=dpos: e.tensor_scalar(SC1[0:1, 0:QBS], BRow, -1.0, 128.0 * d_, ALU.mult, ALU.add),
            reads=[rRR], writes=[rSC1])
        mm([(PS[6][:, 0:QBS], ONES[0:1, 0:128], SC1[0:1, 0:QBS], False, True, None)],
           reads=[rC, rSC1], writes=[rPS[6]])
        dve(lambda e: e.tensor_scalar(SC2[:], PS[6][:, 0:QBS], 0.0, -3.0e38, ALU.is_gt, ALU.mult),
            reads=[rPS[6]], writes=[rSC2])
        dve(lambda e, d_=dpos: e.tensor_copy(MSK[:, d_ * QBS:(d_ + 1) * QBS], SC2[:]),
            reads=[rSC2], writes=[rMSK])

    def ln_pass(quant=False):
        for b in range(NQB):
            xres = [rX[c][b] for c in range(NCH)]
            mm([(PS[6][0:1, 0:QBS], ONEC[:], Xs(c, b), c == 0, c == NCH - 1, None) for c in range(NCH)],
               reads=[rC] + xres, writes=[rPS[6]])
            for c in range(NCH):
                act(lambda e, c_=c, b_=b: e.activation(SC2[:], Xs(c_, b_), AF.Square),
                    reads=[rX[c][b]], writes=[rSC2])
                mm([(PS[6][32:33, 0:QBS], ONEC[:], SC2[:], c == 0, c == NCH - 1, None)],
                   reads=[rC, rSC2], writes=[rPS[6]])
            dve(lambda e: e.tensor_scalar(BRow, PS[6][0:1, 0:QBS], 1.0 / D, None, ALU.mult),
                reads=[rPS[6]], writes=[rRR])
            dve(lambda e: e.tensor_scalar(ARow, PS[6][32:33, 0:QBS], 1.0 / D, None, ALU.mult),
                reads=[rPS[6]], writes=[rRR])
            dve(lambda e: e.tensor_tensor(SC1[0:1, 0:QBS], BRow, BRow, ALU.mult),
                reads=[rRR], writes=[rSC1])
            dve(lambda e: e.tensor_tensor(ARow, ARow, SC1[0:1, 0:QBS], ALU.subtract),
                reads=[rRR, rSC1], writes=[rRR])
            act(lambda e: e.activation(ARow, ARow, AF.Sqrt, bias=CST[0:1, C_EPS:C_EPS + 1]),
                reads=[rRR, rC], writes=[rRR])
            dve(lambda e: e.reciprocal(ARow, ARow), reads=[rRR], writes=[rRR])
            if quant:
                dve(lambda e: e.tensor_scalar(ARow, ARow, SCL[0:1, 2:3], None, ALU.mult),
                    reads=[rRR, rSCL], writes=[rRR])
            dve(lambda e: e.scalar_tensor_tensor(BRow, BRow, -1.0, ARow, ALU.mult, ALU.mult),
                reads=[rRR], writes=[rRR])
            if not quant:
                dve(lambda e: e.tensor_copy(ARB[0:1, 0:QBS], ARow), reads=[rRR], writes=[rARB])
                dve(lambda e: e.tensor_copy(ARB[0:1, QBS:2 * QBS], BRow), reads=[rRR], writes=[rARB])
                mm([(PS[4][:, 0:QBS], ONESB[0:1, 0:128], ARB[0:1, 0:QBS], True, True, None)],
                   reads=[rC, rARB], writes=[rPS[4]])
                mm([(PS[5][:, 0:QBS], ONESB[0:1, 0:128], ARB[0:1, QBS:2 * QBS], True, True, None)],
                   reads=[rC, rARB], writes=[rPS[5]])
            else:
                mm([(PS[4][:, 0:QBS], ONES[0:1, 0:128], ARow, True, True, None)], reads=[rC, rRR], writes=[rPS[4]])
                mm([(PS[5][:, 0:QBS], ONES[0:1, 0:128], BRow, True, True, None)], reads=[rC, rRR], writes=[rPS[5]])
            for c in range(NCH):
                if not quant:
                    dve(lambda e, c_=c, b_=b: e.tensor_tensor(SC2[:], Xs(c_, b_), PS[4][:, 0:QBS], ALU.mult),
                        reads=[rX[c][b], rPS[4]], writes=[rSC2])
                    dve(lambda e, c_=c, b_=b: e.tensor_tensor(XNs(c_, b_), SC2[:], PS[5][:, 0:QBS], ALU.add),
                        reads=[rSC2, rPS[5]], writes=[rXN[c][b]])
                else:
                    quant_block(c, b)

    def quant_block(c, b):
        dve(lambda e, c_=c, b_=b: e.tensor_tensor(SC1[:], Xs(c_, b_), PS[4][:, 0:QBS], ALU.mult),
            reads=[rX[c][b], rPS[4]], writes=[rSC1])
        dve(lambda e: e.tensor_tensor(SC1[:], SC1[:], PS[5][:, 0:QBS], ALU.add),
            reads=[rSC1, rPS[5]], writes=[rSC1])
        dve(lambda e: e.tensor_scalar(SC1[:], SC1[:], 448.0, -448.0, ALU.min, ALU.max),
            reads=[rSC1], writes=[rSC1])
        act(lambda e: e.activation(SC2[:], SC1[:], AF.Square), reads=[rSC1], writes=[rSC2])
        dve(lambda e: e.tensor_scalar(SC2[:], SC2[:], 2.0 ** -24, None, ALU.max), reads=[rSC2], writes=[rSC2])
        act(lambda e: e.activation(SC2[:], SC2[:], AF.Ln), reads=[rSC2], writes=[rSC2])
        dve(lambda e: e.tensor_scalar(SC2[:], SC2[:], float(np.float32(0.5 / np.log(2.0))), None, ALU.mult),
            reads=[rSC2], writes=[rSC2])
        floorc(SC2[:], rSC2)
        dve(lambda e: e.tensor_copy(SI[:], SC2[:]), reads=[rSC2], writes=[rSI])
        dve(lambda e: e.tensor_scalar(SC2[:], SI[:], -6.0, -1.0, ALU.max, ALU.mult), reads=[rSI], writes=[rSC2])
        act(lambda e: e.activation(SC2[:], SC2[:], AF.Exp, bias=CST[:, C_3LN2:C_3LN2 + 1],
                                   scale=CST[:, C_LN2:C_LN2 + 1]),
            reads=[rSC2, rC], writes=[rSC2])
        dve(lambda e: e.tensor_scalar(SC2[:], SC2[:], 64.0, 0.5, ALU.mult, ALU.add), reads=[rSC2], writes=[rSC2])
        floorc(SC2[:], rSC2)
        dve(lambda e: e.tensor_tensor(SC1[:], SC1[:], SC2[:], ALU.mult), reads=[rSC1, rSC2], writes=[rSC1])
        dve(lambda e: e.tensor_scalar(SC1[:], SC1[:], 1.0 / 64.0, 0.5, ALU.mult, ALU.add),
            reads=[rSC1], writes=[rSC1])
        floorc2(SC1[:], rSC1)
        dve(lambda e: e.reciprocal(SC2[:], SC2[:]), reads=[rSC2], writes=[rSC2])
        dve(lambda e: e.tensor_tensor(SC1[:], SC1[:], SC2[:], ALU.mult), reads=[rSC1, rSC2], writes=[rSC1])
        dve(lambda e, c_=c, b_=b: e.tensor_scalar(XNs(c_, b_), SC1[:], 64.0, None, ALU.mult),
            reads=[rSC1], writes=[rXN[c][b]])

    for seq in range(n_seq):
        for c in range(NCH):
            dma("sp", X[:, c * T:(c + 1) * T], dr["x_fm"][seq * D + c * 128: seq * D + (c + 1) * 128, :],
                writes=[rX[c][b] for b in range(NQB)])

        if "l0a" not in phases:
            continue
        dma("gp", A4[:, 0:1024], dr["w0q"], writes=[rA4[0]])
        dma("gp", A4[:, 1024:2048], dr["w0k"], writes=[rA4[1]])
        dma("gp", A4[:, 2048:3072], dr["w0v"], writes=[rA4[2]])
        dma("gp", A4[:, 3072:4096], dr["w0o"], writes=[rA4[3]])
        ln_pass()
        for b in range(NQB):
            xnres = [rXN[c][b] for c in range(NCH)]
            mm([(PS[0][:, 0:QBS], A4[:, ic * 128:(ic + 1) * 128], XNs(ic, b), ic == 0, ic == NCH - 1, None)
                for ic in range(NCH)], reads=[rA4[0]] + xnres, writes=[rPS[0]])
            act(lambda e, b_=b: e.activation(A0[:, b_ * QBS:(b_ + 1) * QBS], PS[0][:, 0:QBS], AF.Identity,
                                             scale=CST[:, C_S0:C_S0 + 1]),
                reads=[rPS[0], rC], writes=[rA0[0]])
            mm([(PS[1][:, 0:QBS], A4[:, 1024 + ic * 128:1024 + (ic + 1) * 128], XNs(ic, b), ic == 0,
                 ic == NCH - 1, None) for ic in range(NCH)], reads=[rA4[1]] + xnres, writes=[rPS[1]])
            act(lambda e, b_=b: e.activation(A1[:, b_ * QBS:(b_ + 1) * QBS], PS[1][:, 0:QBS], AF.Identity),
                reads=[rPS[1]], writes=[rA1[0]])
            for tt in range(4):
                tta = b * 4 + tt
                mm([(PS[2][:, 0:128], XNt(ic, tta), A4[:, 2048 + ic * 128:2048 + (ic + 1) * 128],
                     ic == 0, ic == NCH - 1, None) for ic in range(NCH)],
                   reads=[rA4[2]] + xnres, writes=[rPS[2]])
                act(lambda e, t_=tta: e.activation(A2[:, t_ * 128:(t_ + 1) * 128], PS[2][:, 0:128], AF.Identity),
                    reads=[rPS[2]], writes=[rA2[tta // 4]])
        dve(lambda e: e.memset(SCL[0:1, 7:8], 0.0), reads=[rSCL], writes=[rSCL])
        for b in range(NQB):
            act(lambda e, b_=b: e.activation(SC2[:], A1[:, b_ * QBS:(b_ + 1) * QBS], AF.Square),
                reads=[rA1[0]], writes=[rSC2])
            mm([(PS[4][0:1, 0:QBS], ONEC[:], SC2[:], True, True, None)], reads=[rC, rSC2], writes=[rPS[4]])
            dve(lambda e: e.tensor_reduce(SC1[0:1, 0:1], PS[4][0:1, 0:QBS], mybir.AxisListType.X, ALU.max),
                reads=[rPS[4]], writes=[rSC1])
            dve(lambda e: e.tensor_tensor(SCL[0:1, 7:8], SCL[0:1, 7:8], SC1[0:1, 0:1], ALU.max),
                reads=[rSC1, rSCL], writes=[rSCL])
        dve(lambda e: e.memset(QMX[0:1, 0:1], 0.0), reads=[rQM], writes=[rQM])
        for qb in range(NQB):
            qsl = slice(qb * QBS, (qb + 1) * QBS)
            act(lambda e, q_=qsl: e.activation(SC2[:], A0[:, q_], AF.Square), reads=[rA0[0]], writes=[rSC2])
            mm([(PS[4][0:1, 0:QBS], ONEC[:], SC2[:], True, True, None)], reads=[rC, rSC2], writes=[rPS[4]])
            dve(lambda e: e.tensor_reduce(SC1[0:1, 0:1], PS[4][0:1, 0:QBS], mybir.AxisListType.X, ALU.max),
                reads=[rPS[4]], writes=[rSC1])
            dve(lambda e: e.tensor_tensor(QMX[0:1, 0:1], QMX[0:1, 0:1], SC1[0:1, 0:1], ALU.max),
                reads=[rSC1, rQM], writes=[rQM])
        dve(lambda e: e.tensor_scalar(QMX[0:1, 0:1], QMX[0:1, 0:1], SCL[0:1, 7:8], None, ALU.mult),
            reads=[rQM, rSCL], writes=[rQM])
        act(lambda e: e.activation(QMX[0:1, 0:1], QMX[0:1, 0:1], AF.Sqrt), reads=[rQM], writes=[rQM])
        dve(lambda e: e.tensor_scalar(QMX[0:1, 0:1], QMX[0:1, 0:1], -1.0, None, ALU.mult),
            reads=[rQM], writes=[rQM])
        mm([(PS[4][:, 0:1], ONES[0:1, 0:128], QMX[0:1, 0:1], True, True, None)], reads=[rC, rQM], writes=[rPS[4]])
        act(lambda e: e.activation(MHC[:, 0:1], PS[4][:, 0:1], AF.Identity), reads=[rPS[4]], writes=[rMH])
        for qb in range(NQB):
            qsl = slice(qb * QBS, (qb + 1) * QBS)
            nkc = 4 * (qb + 1)

            def s_tile0(kc, qsl_=qsl):
                bank = 0 if (kc % 2 == 0) else 1
                mm([(PS[bank][:, 0:QBS], A1[:, kc * 128:(kc + 1) * 128], A0[:, qsl_], True, True, None)],
                   reads=[rA1[0], rA0[0]], writes=[rPS[bank]])
                return bank
            prev = s_tile0(0)
            for kc in range(nkc):
                bank = prev
                if kc + 1 < nkc:
                    prev = s_tile0(kc + 1)
                if kc >= 4 * qb:
                    dpos = kc - 4 * qb
                    dve(lambda e, bk=bank, d_=dpos: e.tensor_tensor(
                        PS[bk][:, 0:QBS], PS[bk][:, 0:QBS], MSK[:, d_ * QBS:(d_ + 1) * QBS], ALU.add),
                        reads=[rPS[bank], rMSK], writes=[rPS[bank]])
                pt = kc % 2
                ptap = A3[:, 2048 + pt * QBS: 2048 + (pt + 1) * QBS]
                act(lambda e, bk=bank, a_=ptap: e.activation(a_, PS[bk][:, 0:QBS], AF.Exp, bias=MHC[:, 0:1]),
                    reads=[rPS[bank], rMH], writes=[rA3[4 + pt]])
                mm([(PS[2][0:1, 0:QBS], ONESB[:, 0:1], ptap, kc == 0, kc == nkc - 1, None)],
                   reads=[rC, rA3[4 + pt]], writes=[rPS[2]])
                mm([(PS[3][:, 0:QBS], A2[:, kc * 128:(kc + 1) * 128], ptap, kc == 0, kc == nkc - 1, None)],
                   reads=[rA2[kc // 4], rA3[4 + pt]], writes=[rPS[3]])
            dve(lambda e: e.reciprocal(ARow, PS[2][0:1, 0:QBS]), reads=[rPS[2]], writes=[rRR])
            dve(lambda e: e.tensor_copy(ARB[0:1, 0:QBS], ARow), reads=[rRR], writes=[rARB])
            mm([(PS[4][:, 0:QBS], ONESB[0:1, 0:128], ARB[0:1, 0:QBS], True, True, None)],
               reads=[rC, rARB], writes=[rPS[4]])
            act(lambda e: e.activation(TBS[:], PS[4][:, 0:QBS], AF.Identity), reads=[rPS[4]], writes=[rTBS])
            dve(lambda e: e.tensor_tensor(A3[:, 0:QBS], PS[3][:, 0:QBS], TBS[:], ALU.mult),
                reads=[rPS[3], rTBS], writes=[rA3[0]])
            for dm in range(NCH):
                bank = 4 + (dm % 2)
                mm([(PS[bank][:, 0:QBS], A4[:, 3072 + dm * 128:3072 + (dm + 1) * 128], A3[:, 0:QBS],
                     True, True, None)], reads=[rA4[3], rA3[0]], writes=[rPS[bank]])
                dve(lambda e, d_=dm, bk=bank, q_=qb: e.tensor_tensor(Xs(d_, q_), Xs(d_, q_),
                                                                     PS[bk][:, 0:QBS], ALU.add),
                    reads=[rPS[bank], rX[dm][qb]], writes=[rX[dm][qb]])

        if debug:
            for c in range(NCH):
                dma("sp", dbg["dbg0a"][seq * D + c * 128: seq * D + (c + 1) * 128, :], X[:, c * T:(c + 1) * T],
                    reads=[rX[c][b] for b in range(NQB)])
        if "l0f" not in phases:
            continue
        dma("gp", A0[:, :], dr["w8T"], writes=[rA0[0], rA0[1], rA0[2], rA0[3]])
        ln_pass(quant=True)
        for b in range(NQB):
            xnres = [rXN[c][b] for c in range(NCH)]
            for oc in range(NCH):
                bank = oc % 2
                ms = [(PS[bank][:, 0:QBS], B0R[0:1, oc * 128:(oc + 1) * 128], ONESB[0:1, 0:QBS], True, False, None)]
                ms += [(PS[bank][:, 0:QBS], A0[:, ic * 1024 + oc * 128: ic * 1024 + (oc + 1) * 128],
                        XNs(ic, b), False, ic == NCH - 1, None) for ic in range(NCH)]
                mm(ms, reads=[rROWS, rC, rA0[0], rA0[1], rA0[2], rA0[3]] + xnres, writes=[rPS[bank]])
                dve(lambda e, o_=oc, b_=b, bk=bank: e.scalar_tensor_tensor(
                    Xs(o_, b_), PS[bk][:, 0:QBS], CST[:, C_SS:C_SS + 1], Xs(o_, b_), ALU.mult, ALU.add),
                    reads=[rPS[bank], rC, rX[oc][b]], writes=[rX[oc][b]])

        if debug:
            for c in range(NCH):
                dma("sp", dbg["dbg0f"][seq * D + c * 128: seq * D + (c + 1) * 128, :], X[:, c * T:(c + 1) * T],
                    reads=[rX[c][b] for b in range(NQB)])
        if "l1a" not in phases:
            continue
        ln_pass()
        for g in range(NGRP):
            dma("gp", A4[:, 0:4096], dr["w1q"][g * 128:(g + 1) * 128, :],
                writes=[rA4[0], rA4[1], rA4[2], rA4[3]])
            dma("gp", A4[:, 4096:8192], dr["w1k"][g * 128:(g + 1) * 128, :],
                writes=[rA4[4], rA4[5], rA4[6], rA4[7]])
            for cc in range(CPG):
                for b in range(NQB):
                    xnres = [rXN[c][b] for c in range(NCH)]
                    mm([(PS[0][:, 0:QBS], A4[:, ic * 512 + cc * 128: ic * 512 + (cc + 1) * 128],
                         XNs(ic, b), ic == 0, ic == NCH - 1, None) for ic in range(NCH)],
                       reads=[rA4[0], rA4[1], rA4[2], rA4[3]] + xnres, writes=[rPS[0]])
                    act(lambda e, c_=cc, b_=b, g_=g: e.activation(
                        A0[:, c_ * T + b_ * QBS: c_ * T + (b_ + 1) * QBS], PS[0][:, 0:QBS], AF.Identity,
                        bias=BQC[:, g_ * 4 + c_: g_ * 4 + c_ + 1], scale=CST[:, C_S1:C_S1 + 1]),
                        reads=[rPS[0], rROWS, rC], writes=[rA0[cc]])
                    mm([(PS[1][:, 0:QBS], A4[:, 4096 + ic * 512 + cc * 128: 4096 + ic * 512 + (cc + 1) * 128],
                         XNs(ic, b), ic == 0, ic == NCH - 1, None) for ic in range(NCH)],
                       reads=[rA4[4], rA4[5], rA4[6], rA4[7]] + xnres, writes=[rPS[1]])
                    act(lambda e, c_=cc, b_=b, g_=g: e.activation(
                        A1[:, c_ * T + b_ * QBS: c_ * T + (b_ + 1) * QBS], PS[1][:, 0:QBS], AF.Identity,
                        bias=BKC[:, g_ * 4 + c_: g_ * 4 + c_ + 1]),
                        reads=[rPS[1], rROWS], writes=[rA1[cc]])
            dma("gp", A4[:, 0:4096], dr["w1v"][g * 128:(g + 1) * 128, :],
                writes=[rA4[0], rA4[1], rA4[2], rA4[3]])
            for tt in range(16):
                b = tt // 4
                xnres = [rXN[c][b] for c in range(NCH)]
                ms = [(PS[2][:, 0:QBS], ONESB[0:1, 0:128], BVR[0:1, g * QBS:(g + 1) * QBS], True, False, None)]
                ms += [(PS[2][:, 0:QBS], XNt(ic, tt), A4[:, ic * 512:(ic + 1) * 512], False, ic == NCH - 1, None)
                       for ic in range(NCH)]
                mm(ms, reads=[rC, rROWS, rA4[0], rA4[1], rA4[2], rA4[3]] + xnres, writes=[rPS[2]])
                act(lambda e, t_=tt: e.activation(A2[:, t_ * QBS:(t_ + 1) * QBS], PS[2][:, 0:QBS], AF.Identity),
                    reads=[rPS[2]], writes=[rA2[tt]])
            for (buf, rbuf) in ((A0, rA0), (A1, rA1)):
                for cc in range(CPG):
                    csl = slice(cc * T, (cc + 1) * T)
                    for (dst0, src0) in ((0, 32), (32, 0), (64, 96), (96, 64)):
                        dma("sp", A3[dst0:dst0 + 32, 2048:4096], buf[src0:src0 + 32, csl],
                            reads=[rbuf[cc]], writes=list(rA3[4:8]))
                    dve(lambda e, b_=buf, c_=csl: e.tensor_tensor(A3[:, 0:2048], b_[:, c_], TT1[:], ALU.mult),
                        reads=[rbuf[cc], rTT], writes=list(rA3[0:4]))
                    dve(lambda e: e.tensor_tensor(A3[:, 2048:4096], A3[:, 2048:4096], TT2[:], ALU.mult),
                        reads=list(rA3[4:8]) + [rTT], writes=list(rA3[4:8]))
                    dve(lambda e, b_=buf, c_=csl: e.tensor_tensor(b_[:, c_], A3[:, 0:2048], A3[:, 2048:4096],
                                                                  ALU.add),
                        reads=list(rA3[0:8]), writes=[rbuf[cc]])
            dma("gp", A4[:, 4096:8192], dr["w1o"][g * 128:(g + 1) * 128, :],
                writes=[rA4[4], rA4[5], rA4[6], rA4[7]])
            dve(lambda e: e.memset(SCL[0:1, 7:8], 0.0), reads=[rSCL], writes=[rSCL])
            for b in range(NQB):
                for cc in range(CPG):
                    act(lambda e, c_=cc, b_=b: e.activation(SC2[:], A1[:, c_ * T + b_ * QBS: c_ * T + (b_ + 1) * QBS],
                                                            AF.Square),
                        reads=[rA1[cc]], writes=[rSC2])
                    mm([(PS[4][0:1, 0:QBS], ONEC[:], SC2[:], cc == 0, cc == CPG - 1, None)],
                       reads=[rC, rSC2], writes=[rPS[4]])
                dve(lambda e: e.tensor_reduce(SC1[0:1, 0:1], PS[4][0:1, 0:QBS], mybir.AxisListType.X, ALU.max),
                    reads=[rPS[4]], writes=[rSC1])
                dve(lambda e: e.tensor_tensor(SCL[0:1, 7:8], SCL[0:1, 7:8], SC1[0:1, 0:1], ALU.max),
                    reads=[rSC1, rSCL], writes=[rSCL])
            for cc in range(CPG):
                dve(lambda e, c_=cc: e.memset(QMX[0:1, 2 * c_:2 * c_ + 2], 0.0), reads=[rQM], writes=[rQM])
                for qb in range(NQB):
                    qsl = slice(cc * T + qb * QBS, cc * T + (qb + 1) * QBS)
                    act(lambda e, q_=qsl: e.activation(SC2[:], A0[:, q_], AF.Square), reads=[rA0[cc]], writes=[rSC2])
                    mm([(PS[4][0:1, 0:QBS], INDA[:], SC2[:], True, True, None),
                        (PS[4][32:33, 0:QBS], INDB[:], SC2[:], True, True, None)],
                       reads=[rC, rSC2], writes=[rPS[4]])
                    for (r0, hoff) in ((0, 0), (32, 1)):
                        dve(lambda e, r_=r0: e.tensor_reduce(SC1[0:1, 0:1], PS[4][r_:r_ + 1, 0:QBS],
                                                             mybir.AxisListType.X, ALU.max),
                            reads=[rPS[4]], writes=[rSC1])
                        dve(lambda e, c_=cc, h_=hoff: e.tensor_tensor(
                            QMX[0:1, 2 * c_ + h_:2 * c_ + h_ + 1], QMX[0:1, 2 * c_ + h_:2 * c_ + h_ + 1],
                            SC1[0:1, 0:1], ALU.max),
                            reads=[rSC1, rQM], writes=[rQM])
            dve(lambda e: e.tensor_scalar(QMX[0:1, 0:8], QMX[0:1, 0:8], SCL[0:1, 7:8], None, ALU.mult),
                reads=[rQM, rSCL], writes=[rQM])
            act(lambda e: e.activation(QMX[0:1, 0:8], QMX[0:1, 0:8], AF.Sqrt), reads=[rQM], writes=[rQM])
            dve(lambda e: e.tensor_scalar(QMX[0:1, 0:8], QMX[0:1, 0:8], -1.0, None, ALU.mult),
                reads=[rQM], writes=[rQM])
            mm([(PS[4][:, 0:8], ONES[0:1, 0:128], QMX[0:1, 0:8], True, True, None)], reads=[rC, rQM], writes=[rPS[4]])
            act(lambda e: e.activation(MHC[:, 0:8], PS[4][:, 0:8], AF.Identity), reads=[rPS[4]], writes=[rMH])
            for qb in range(NQB):
                nkc = 4 * (qb + 1)
                for cc in range(CPG):
                    qsl = slice(cc * T + qb * QBS, cc * T + (qb + 1) * QBS)

                    def s_tile1(kc, cc_=cc, qsl_=qsl):
                        bA = 0 if (kc % 2 == 0) else 6
                        bB = 1 if (kc % 2 == 0) else 7
                        ksl = slice(cc_ * T + kc * 128, cc_ * T + (kc + 1) * 128)
                        mm([(PS[bA][:, 0:QBS], A1[0:64, ksl], A0[0:64, qsl_], True, True, None)],
                           reads=[rA1[cc_], rA0[cc_]], writes=[rPS[bA]])
                        mm([(PS[bB][:, 0:QBS], A1[64:128, ksl], A0[64:128, qsl_], True, True, None)],
                           reads=[rA1[cc_], rA0[cc_]], writes=[rPS[bB]])
                        return bA, bB
                    prev = s_tile1(0)
                    for kc in range(nkc):
                        bA, bB = prev
                        if kc + 1 < nkc:
                            prev = s_tile1(kc + 1)
                        if kc >= 4 * qb:
                            dpos = kc - 4 * qb
                            for bk in (bA, bB):
                                dve(lambda e, b_=bk, d_=dpos: e.tensor_tensor(
                                    PS[b_][:, 0:QBS], PS[b_][:, 0:QBS], MSK[:, d_ * QBS:(d_ + 1) * QBS], ALU.add),
                                    reads=[rPS[bk], rMSK], writes=[rPS[bk]])
                        ptA, ptB = (kc % 2) * 2, (kc % 2) * 2 + 1
                        apA = A3[:, 2048 + ptA * QBS: 2048 + (ptA + 1) * QBS]
                        apB = A3[:, 2048 + ptB * QBS: 2048 + (ptB + 1) * QBS]
                        act(lambda e, b_=bA, a_=apA, c_=cc: e.activation(
                            a_, PS[b_][:, 0:QBS], AF.Exp, bias=MHC[:, 2 * c_:2 * c_ + 1]),
                            reads=[rPS[bA], rMH], writes=[rA3[4 + ptA]])
                        act(lambda e, b_=bB, a_=apB, c_=cc: e.activation(
                            a_, PS[b_][:, 0:QBS], AF.Exp, bias=MHC[:, 2 * c_ + 1:2 * c_ + 2]),
                            reads=[rPS[bB], rMH], writes=[rA3[4 + ptB]])
                        mm([(PS[2][0:1, 0:QBS], ONESB[:, 0:1], apA, kc == 0, kc == nkc - 1, None),
                            (PS[2][32:33, 0:QBS], ONESB[:, 0:1], apB, kc == 0, kc == nkc - 1, None, True)],
                           reads=[rC, rA3[4 + ptA], rA3[4 + ptB]], writes=[rPS[2]])
                        lA = 2 * cc * DH
                        lB = (2 * cc + 1) * DH
                        mm([(PS[3][0:64, 0:QBS], A2[:, kc * QBS + lA: kc * QBS + lA + DH], apA,
                             kc == 0, kc == nkc - 1, (0, 0)),
                            (PS[3][64:128, 0:QBS], A2[:, kc * QBS + lB: kc * QBS + lB + DH], apB,
                             kc == 0, kc == nkc - 1, (0, 64), True)],
                           reads=[rA2[kc], rA3[4 + ptA], rA3[4 + ptB]], writes=[rPS[3]])
                    dve(lambda e: e.reciprocal(ARow, PS[2][0:1, 0:QBS]), reads=[rPS[2]], writes=[rRR])
                    dve(lambda e: e.reciprocal(BRow, PS[2][32:33, 0:QBS]), reads=[rPS[2]], writes=[rRR])
                    dve(lambda e: e.tensor_copy(ARB[0:1, 0:QBS], ARow), reads=[rRR], writes=[rARB])
                    dve(lambda e: e.tensor_copy(ARB[0:1, QBS:2 * QBS], BRow), reads=[rRR], writes=[rARB])
                    mm([(PS[4][0:64, 0:QBS], ONESB[0:1, 0:64], ARB[0:1, 0:QBS], True, True, (0, 0)),
                        (PS[4][64:128, 0:QBS], ONESB[0:1, 0:64], ARB[0:1, QBS:2 * QBS], True, True, (0, 64))],
                       reads=[rC, rARB], writes=[rPS[4]])
                    act(lambda e: e.activation(TBS[:], PS[4][:, 0:QBS], AF.Identity), reads=[rPS[4]], writes=[rTBS])
                    dve(lambda e, c_=cc: e.tensor_tensor(A3[:, c_ * QBS:(c_ + 1) * QBS], PS[3][:, 0:QBS],
                                                         TBS[:], ALU.mult),
                        reads=[rPS[3], rTBS], writes=[rA3[cc]])
                for dm in range(NCH):
                    bank = 5 if (dm % 2 == 0) else 6
                    ms = []
                    if g == 0:
                        ms.append((PS[bank][:, 0:QBS], BOR[0:1, dm * 128:(dm + 1) * 128], ONESB[0:1, 0:QBS],
                                   True, False, None))
                    ms += [(PS[bank][:, 0:QBS], A4[:, 4096 + cc * 1024 + dm * 128: 4096 + cc * 1024 + (dm + 1) * 128],
                            A3[:, cc * QBS:(cc + 1) * QBS], (g != 0 and cc == 0), cc == CPG - 1, None)
                           for cc in range(CPG)]
                    mm(ms, reads=[rROWS, rC, rA4[4], rA4[5], rA4[6], rA4[7]] + list(rA3[0:4]), writes=[rPS[bank]])
                    dve(lambda e, d_=dm, q_=qb, bk=bank: e.tensor_tensor(Xs(d_, q_), Xs(d_, q_),
                                                                         PS[bk][:, 0:QBS], ALU.add),
                        reads=[rPS[bank], rX[dm][qb]], writes=[rX[dm][qb]])

        if debug:
            for c in range(NCH):
                dma("sp", dbg["dbg1a"][seq * D + c * 128: seq * D + (c + 1) * 128, :], X[:, c * T:(c + 1) * T],
                    reads=[rX[c][b] for b in range(NQB)])
        if "l1f" not in phases:
            continue
        ln_pass()
        for fs in range(NFS):
            dma("gp", A0[:, :], dr["w1f"][fs * 128:(fs + 1) * 128, :], writes=[rA0[0], rA0[1], rA0[2], rA0[3]])
            dma("gp", A1[:, :], dr["w2f"][fs * 128:(fs + 1) * 128, :], writes=[rA1[0], rA1[1], rA1[2], rA1[3]])
            wread = [rA0[0], rA0[1], rA0[2], rA0[3]]
            w2read = [rA1[0], rA1[1], rA1[2], rA1[3]]
            for b in range(NQB):
                xnres = [rXN[c][b] for c in range(NCH)]
                for hc in range(NCH):
                    bank = hc % 2
                    mm([(PS[bank][:, 0:QBS], A0[:, ic * 1024 + hc * 128: ic * 1024 + (hc + 1) * 128],
                         XNs(ic, b), ic == 0, ic == NCH - 1, None) for ic in range(NCH)],
                       reads=wread + xnres, writes=[rPS[bank]])
                    act(lambda e, h_=hc, f_=fs, bk=bank: e.activation(
                        A2[:, h_ * QBS:(h_ + 1) * QBS], PS[bk][:, 0:QBS], AF.Gelu,
                        bias=B1C[:, f_ * 8 + h_: f_ * 8 + h_ + 1]),
                        reads=[rPS[bank], rROWS], writes=[rA2[hc]])
                for dm in range(NCH):
                    bank = 2 + (dm % 2)
                    ms = []
                    if fs == 0:
                        ms.append((PS[bank][:, 0:QBS], B2R[0:1, dm * 128:(dm + 1) * 128], ONESB[0:1, 0:QBS],
                                   True, False, None))
                    ms += [(PS[bank][:, 0:QBS], A1[:, hc * 1024 + dm * 128: hc * 1024 + (dm + 1) * 128],
                            A2[:, hc * QBS:(hc + 1) * QBS], (fs != 0 and hc == 0), hc == NCH - 1, None)
                           for hc in range(NCH)]
                    mm(ms, reads=[rROWS, rC] + w2read + [rA2[hc] for hc in range(NCH)], writes=[rPS[bank]])
                    dve(lambda e, d_=dm, b_=b, bk=bank: e.tensor_tensor(Xs(d_, b_), Xs(d_, b_),
                                                                        PS[bk][:, 0:QBS], ALU.add),
                        reads=[rPS[bank], rX[dm][b]], writes=[rX[dm][b]])

        for c in range(NCH):
            dma("sp", out[seq * D + c * 128: seq * D + (c + 1) * 128, :], X[:, c * T:(c + 1) * T],
                reads=[rX[c][b] for b in range(NQB)])

    with ctx:
        sems = {e: ctx.enter_context(nc.semaphore("sem_" + e)) for e in ("te", "act", "dve")}
        qsems = {q: [ctx.enter_context(nc.semaphore("sem_%s%d" % (q, i))) for i in range(QDEPTH)]
                 for q in QUEUES}

        def do_wait(eng, pe, pt):
            if pe in QUEUES:
                eng.wait_ge(qsems[pe][(pt - 1) % QDEPTH], 16 * ((pt - 1) // QDEPTH + 1))
            else:
                eng.wait_ge(sems[pe], pt)

        def run_queue(eng, q):
            for i, (waits, fn) in enumerate(s.stream[q]):
                for pe, pt in waits:
                    do_wait(eng, pe, pt)
                fn(eng).then_inc(qsems[q][i % QDEPTH], 16)
            for t in range(max(0, s.tick[q] - QDEPTH), s.tick[q]):
                do_wait(eng, q, t + 1)

        def run_eng(eng, e):
            for waits, fn in s.stream[e]:
                for pe, pt in waits:
                    do_wait(eng, pe, pt)
                fn(eng).then_inc(sems[e], 1)

        with nc.Block() as block:
            @block.sync
            def _(sync):
                run_queue(sync, "sp")

            @block.gpsimd
            def _(gp):
                run_queue(gp, "gp")

            @block.tensor
            def _(te):
                run_eng(te, "te")

            @block.scalar
            def _(sc):
                run_eng(sc, "act")

            @block.vector
            def _(ve):
                run_eng(ve, "dve")
    n_inst = sum(len(v) for v in s.stream.values())
    return nc, n_inst


def kernel(**inputs):
    L = host_layout(inputs)
    nc, n_inst = build_nc()
    in_maps = []
    for core in range(8):
        m = {nm: L[nm] for nm, _ in DRAM_SPECS if nm != "x_fm"}
        m["x_fm"] = np.ascontiguousarray(L["x_fm"][core * 2 * D:(core + 1) * 2 * D, :])
        in_maps.append(m)
    res = run_bass_kernel_spmd(nc, in_maps, core_ids=list(range(8)))
    outs = []
    for core in range(8):
        o = np.asarray(res.results[core]["out_fm"]).reshape(2, D, T)
        outs.append(np.transpose(o, (0, 2, 1)))
    return np.concatenate(outs, 0)


# version 06
# speedup vs baseline: 1.5669x; 1.0667x over previous
import contextlib
import numpy as np
import concourse.bass as bass
import concourse.mybir as mybir
from concourse.bass_utils import run_bass_kernel_spmd

F32 = mybir.dt.float32
BF16 = mybir.dt.bfloat16
I32 = mybir.dt.int32
ALU = mybir.AluOpType
AF = mybir.ActivationFunctionType

T = 2048
D = 1024
NCH = 8
QBS = 512
NQB = T // QBS
H = 16
DH = 64
NGRP = 2
CPG = 4
FF = 4096
NFS = 4

LN2 = float(np.float32(np.log(2.0)))

ENGS = ("te", "act", "dve", "sp", "gp")
STEP = {"te": 1, "act": 1, "dve": 1, "sp": 16, "gp": 16}


class Res:
    __slots__ = ("w", "r")

    def __init__(self):
        self.w = None
        self.r = []


QUEUES = ("sp", "gp")
QDEPTH = 6


class Sched:

    def __init__(self):
        self.stream = {e: [] for e in ENGS}
        self.tick = {e: 0 for e in ENGS}
        self.seen = {e: {} for e in ENGS}

    def _covered(self, e, pe, pt):
        key = (pe, (pt - 1) % QDEPTH) if pe in QUEUES else pe
        return self.seen[e].get(key, 0) >= pt

    def _cover(self, e, pe, pt):
        key = (pe, (pt - 1) % QDEPTH) if pe in QUEUES else pe
        if self.seen[e].get(key, 0) < pt:
            self.seen[e][key] = pt

    def emit(self, e, fn, reads=(), writes=()):
        need = {}

        def add(pe, pt):
            key = (pe, (pt - 1) % QDEPTH) if pe in QUEUES else pe
            if pt > need.get(key, (None, 0))[1]:
                need[key] = (pe, pt)

        for r in reads:
            if r.w is not None:
                add(*r.w)
        for w in writes:
            if w.w is not None:
                add(*w.w)
            for pe, pt in w.r:
                add(pe, pt)
        waits = []
        self.tick[e] += 1
        t = self.tick[e]
        if e in QUEUES and t > QDEPTH:
            pw = t - QDEPTH
            if not self._covered(e, e, pw):
                waits.append((e, pw))
                self._cover(e, e, pw)
        for key in sorted(need, key=str):
            pe, pt = need[key]
            if pe == "te" and e == "te":
                continue
            if self._covered(e, pe, pt):
                continue
            waits.append((pe, pt))
            self._cover(e, pe, pt)
        for r in reads:
            r.r.append((e, t))
        for w in writes:
            w.w = (e, t)
            w.r = []
        self.stream[e].append((waits, fn))


def host_layout(inputs):
    L = {}
    x = np.asarray(inputs["L0_attn_x"], dtype=np.float32)
    L["x_fm"] = np.ascontiguousarray(np.transpose(x, (0, 2, 1))).reshape(16 * D, T)

    def img(w):
        w = np.asarray(w, dtype=np.float32)
        k, n = w.shape
        return np.ascontiguousarray(w.reshape(k // 128, 128, n).transpose(1, 0, 2).reshape(128, (k // 128) * n))

    L["w0q"] = img(inputs["L0_attn_Wq"])
    L["w0k"] = img(inputs["L0_attn_Wk"])
    L["w0v"] = img(inputs["L0_attn_Wv"])
    L["w0o"] = np.ascontiguousarray(np.asarray(inputs["L0_attn_Wo"], dtype=np.float32))
    L["w8T"] = img(np.asarray(inputs["L0_ffn_weight_fp8"]).T)
    L["b0row"] = np.asarray(inputs["L0_ffn_bias"], dtype=np.float32).reshape(1, D)
    L["s_in"] = np.asarray(inputs["L0_ffn_input_scale"], dtype=np.float32).reshape(1, 1)
    L["s_w"] = np.asarray(inputs["L0_ffn_weight_scale"], dtype=np.float32).reshape(1, 1)

    perm = np.zeros(D, dtype=np.int64)
    for h in range(H):
        b = h * DH
        perm[b:b + 32] = b + np.arange(0, DH, 2)
        perm[b + 32:b + DH] = b + 1 + np.arange(0, DH, 2)
    wq = np.asarray(inputs["L1_attn_Wq"]).T[:, perm]
    wk = np.asarray(inputs["L1_attn_Wk"]).T[:, perm]
    L["w1q"] = np.concatenate([img(wq[:, g * 512:(g + 1) * 512]) for g in range(NGRP)], 0)
    L["w1k"] = np.concatenate([img(wk[:, g * 512:(g + 1) * 512]) for g in range(NGRP)], 0)
    wv = np.asarray(inputs["L1_attn_Wv"]).T
    L["w1v"] = np.concatenate([img(wv[:, g * 512:(g + 1) * 512]) for g in range(NGRP)], 0)
    wo = np.asarray(inputs["L1_attn_Wo"]).T
    L["w1o"] = np.concatenate([img(wo[g * 512:(g + 1) * 512, :]) for g in range(NGRP)], 0)
    bq = np.asarray(inputs["L1_attn_bq"])[perm]
    bk = np.asarray(inputs["L1_attn_bk"])[perm]
    L["bqcol"] = np.ascontiguousarray(bq.reshape(NCH, 128).T.astype(np.float32))
    L["bkcol"] = np.ascontiguousarray(bk.reshape(NCH, 128).T.astype(np.float32))
    L["bvrow"] = np.asarray(inputs["L1_attn_bv"], dtype=np.float32).reshape(1, D)
    L["borow"] = np.asarray(inputs["L1_attn_bo"], dtype=np.float32).reshape(1, D)

    w1f = np.asarray(inputs["L1_ffn_w1"])
    L["w1f"] = np.concatenate([img(w1f[:, s * 1024:(s + 1) * 1024]) for s in range(NFS)], 0)
    w2f = np.asarray(inputs["L1_ffn_w2"])
    L["w2f"] = np.concatenate([img(w2f[s * 1024:(s + 1) * 1024, :]) for s in range(NFS)], 0)
    L["b1col"] = np.ascontiguousarray(np.asarray(inputs["L1_ffn_b1"]).reshape(32, 128).T.astype(np.float32))
    L["b2row"] = np.asarray(inputs["L1_ffn_b2"], dtype=np.float32).reshape(1, D)
    L["iota"] = np.arange(T, dtype=np.float32).reshape(1, T)
    sels = np.zeros((128, 64), dtype=np.float32)
    for j in range(8):
        sels[:, j * 8 + j] = 1.0
    L["sels"] = sels
    selb = np.zeros((8, 1024), dtype=np.float32)
    for cc in range(4):
        selb[2 * cc, cc * 128: cc * 128 + 64] = 1.0
        selb[2 * cc + 1, cc * 128 + 64: cc * 128 + 128] = 1.0
    for qb in range(4):
        selb[qb, 512 + qb * 128: 512 + (qb + 1) * 128] = 1.0
    L["selb"] = selb
    try:
        import jax
        import jax.numpy as jnp
        with jax.default_device(jax.local_devices(backend="cpu")[0]):
            theta_j = 10000.0 ** (-jnp.arange(0, DH, 2, dtype=jnp.float32) / DH)
            seq_j = jnp.arange(T, dtype=jnp.float32)
            fr_j = jnp.einsum('i,j->ij', seq_j, theta_j)
            sn = np.asarray(jnp.sin(fr_j)).astype(np.float32).T
            cs = np.asarray(jnp.cos(fr_j)).astype(np.float32).T
    except Exception:
        theta = (10000.0 ** (-np.arange(0, DH, 2, dtype=np.float32) / DH)).astype(np.float32)
        fr = np.einsum('i,j->ij', np.arange(T, dtype=np.float32), theta).astype(np.float32)
        sn = np.sin(fr).astype(np.float32).T
        cs = np.cos(fr).astype(np.float32).T
    L["tt1"] = np.ascontiguousarray(np.tile(cs, (4, 1)))
    L["tt2"] = np.ascontiguousarray(np.concatenate([-sn, sn, -sn, sn], 0))
    return L


DRAM_SPECS = [
    ("x_fm", [2 * D, T]), ("w0q", [128, 1024]), ("w0k", [128, 1024]), ("w0v", [128, 1024]),
    ("w0o", [128, 1024]), ("w8T", [128, 8 * 1024]), ("b0row", [1, D]), ("s_in", [1, 1]), ("s_w", [1, 1]),
    ("w1q", [256, 4096]), ("w1k", [256, 4096]), ("w1v", [256, 4096]), ("w1o", [256, 4096]),
    ("bqcol", [128, 8]), ("bkcol", [128, 8]), ("bvrow", [1, D]), ("borow", [1, D]),
    ("w1f", [512, 8192]), ("w2f", [512, 8192]), ("b1col", [128, 32]), ("b2row", [1, D]),
    ("sels", [128, 64]), ("selb", [8, 1024]),
    ("iota", [1, T]), ("tt1", [128, T]), ("tt2", [128, T]),
]


def build_nc(n_seq=2, debug=False, phases=("l0a", "l0f", "l1a", "l1f")):
    nc = bass.Bass("TRN2", target_bir_lowering=False, debug=False)
    dr = {nm: nc.dram_tensor(nm, shp, F32, kind="ExternalInput").ap() for nm, shp in DRAM_SPECS}
    out = nc.dram_tensor("out_fm", [n_seq * D, T], F32, kind="ExternalOutput").ap()
    dbg = {}
    if debug:
        for nm in ("dbg0a", "dbg0f", "dbg1a"):
            dbg[nm] = nc.dram_tensor(nm, [n_seq * D, T], F32, kind="ExternalOutput").ap()

    ctx = contextlib.ExitStack()
    sb = lambda nm, shp, dt: ctx.enter_context(nc.sbuf_tensor(nm, shp, dt))

    X = sb("X", [128, NCH * T], F32)
    XN = sb("XN", [128, NCH * T], BF16)
    A0 = sb("A0", [128, 8192], BF16)
    A1 = sb("A1", [128, 8192], BF16)
    A2 = sb("A2", [128, 8192], BF16)
    A3 = sb("A3", [128, 4096], BF16)
    A4 = sb("A4", [128, 8192], BF16)
    TT1 = sb("TT1", [128, T], BF16)
    TT2 = sb("TT2", [128, T], BF16)
    MSK = sb("MSK", [128, 4 * QBS], BF16)
    SELS = sb("SELS", [128, 64], BF16)
    SELB = sb("SELB", [8, 1024], BF16)
    TBR = sb("TBR", [8, 1024], BF16)
    TBSB = sb("TBSB", [128, QBS], BF16)
    MHC = sb("MHC", [128, 16], F32)
    QMX = sb("QMX", [1, 32], F32)
    RR = sb("RR", [1, 1024], F32)
    TBS = sb("TBS", [128, 512], F32)
    SC1 = sb("SC1", [128, 512], F32)
    SC2 = sb("SC2", [128, 512], F32)
    SI = sb("SI", [128, 512], I32)
    B0R = sb("B0R", [1, D], BF16)
    BVR = sb("BVR", [1, D], BF16)
    BOR = sb("BOR", [1, D], BF16)
    B2R = sb("B2R", [1, D], BF16)
    BQC = sb("BQC", [128, NCH], F32)
    BKC = sb("BKC", [128, NCH], F32)
    B1C = sb("B1C", [128, 32], F32)
    ONES = sb("ONES", [33, 128], F32)
    ONESB = sb("ONESB", [128, QBS], BF16)
    ONEC = sb("ONEC", [128, 1], F32)
    INDA = sb("INDA", [128, 1], F32)
    INDB = sb("INDB", [128, 1], F32)
    CST = sb("CST", [128, 8], F32)
    SCL = sb("SCL", [1, 16], F32)
    PS = [ctx.enter_context(nc.psum_tensor("PS%d" % i, [128, 512], F32)) for i in range(8)]

    C_LN2, C_3LN2, C_EPS, C_S1, C_SS, C_S0 = 0, 1, 2, 3, 4, 5

    s = Sched()
    R = lambda: Res()
    rX = [[R() for b in range(NQB)] for c in range(NCH)]
    rXN = [[R() for b in range(NQB)] for c in range(NCH)]
    rA0 = [R() for c in range(CPG)]
    rA1 = [R() for c in range(CPG)]
    rA2 = [R() for i in range(16)]
    rA3 = [R() for i in range(8)]
    rA4 = [R() for i in range(8)]
    rTT, rMSK = R(), R()
    rMH, rQM, rRR, rTBS = R(), R(), R(), R()
    rSEL, rTBR, rTBSB = R(), R(), R()
    rSC1, rSC2, rSI = R(), R(), R()
    rROWS, rC, rSCL = R(), R(), R()
    rPS = [R() for i in range(8)]

    def dma(q, outap, inap, reads=(), writes=()):
        s.emit(q, lambda e, o=outap, i=inap: e.dma_start(out=o, in_=i), reads=reads, writes=writes)

    def mm(ms, reads, writes):
        def fn(e, ms_=tuple(ms)):
            last = None
            for m in ms_:
                (o, l, r_, st, sp, tp), skip = m[:6], (len(m) > 6 and m[6])
                last = e.matmul(o, l, r_, start=st, stop=sp, tile_position=tp,
                                skip_group_check=bool(skip))
            return last
        s.emit("te", fn, reads=reads, writes=writes)

    def act(fn, reads, writes):
        s.emit("act", fn, reads=reads, writes=writes)

    def dve(fn, reads, writes):
        s.emit("dve", fn, reads=reads, writes=writes)

    def floorc(wap, rres):
        n = wap.shape[1]
        dve(lambda e: e.tensor_copy(SI[:, 0:n], wap), reads=[rres], writes=[rSI])
        dve(lambda e: e.tensor_tensor(TBS[:, 0:n], SI[:, 0:n], wap, ALU.is_gt),
            reads=[rSI, rres], writes=[rTBS])
        dve(lambda e: e.scalar_tensor_tensor(wap, TBS[:, 0:n], -1.0, SI[:, 0:n], ALU.mult, ALU.add),
            reads=[rTBS, rSI], writes=[rres])

    def floorc2(wap, rres):
        n = wap.shape[1]
        dve(lambda e: e.tensor_copy(SI[:, 0:n], wap), reads=[rres], writes=[rSI])
        dve(lambda e: e.tensor_tensor(TBS[:, 0:n], SI[:, 0:n], wap, ALU.is_gt),
            reads=[rSI, rres], writes=[rTBS])
        dve(lambda e: e.scalar_tensor_tensor(TBSB[:, 0:n], SI[:, 0:n], -1.0, wap, ALU.add, ALU.is_gt),
            reads=[rSI, rres], writes=[rTBSB])
        dve(lambda e: e.scalar_tensor_tensor(wap, SI[:, 0:n], 1.0, wap, ALU.add, ALU.is_le),
            reads=[rSI, rres], writes=[rres])
        dve(lambda e: e.tensor_tensor(wap, wap, TBS[:, 0:n], ALU.subtract),
            reads=[rres, rTBS], writes=[rres])
        dve(lambda e: e.tensor_tensor(wap, wap, TBSB[:, 0:n], ALU.subtract),
            reads=[rres, rTBSB], writes=[rres])
        dve(lambda e: e.tensor_tensor(wap, wap, SI[:, 0:n], ALU.add),
            reads=[rres, rSI], writes=[rres])

    Xs = lambda c, b: X[:, c * T + b * QBS: c * T + (b + 1) * QBS]
    XNs = lambda c, b: XN[:, c * T + b * QBS: c * T + (b + 1) * QBS]
    XNt = lambda c, tt: XN[:, c * T + tt * 128: c * T + (tt + 1) * 128]
    ARow = RR[0:1, 0:QBS]
    BRow = RR[0:1, QBS:2 * QBS]

    dma("gp", B0R[:], dr["b0row"], writes=[rROWS])
    dma("gp", BVR[:], dr["bvrow"], writes=[rROWS])
    dma("gp", BOR[:], dr["borow"], writes=[rROWS])
    dma("gp", B2R[:], dr["b2row"], writes=[rROWS])
    dma("sp", BQC[:], dr["bqcol"], writes=[rROWS])
    dma("sp", BKC[:], dr["bkcol"], writes=[rROWS])
    dma("sp", B1C[:], dr["b1col"], writes=[rROWS])
    dma("sp", SCL[:, 0:1], dr["s_in"], writes=[rSCL])
    dma("sp", SCL[:, 1:2], dr["s_w"], writes=[rSCL])

    def consts_fn(e):
        e.memset(ONES[:], 1.0)
        e.memset(ONESB[:], 1.0)
        e.memset(ONEC[:], 1.0)
        e.memset(INDA[0:64, :], 1.0)
        e.memset(INDA[64:128, :], 0.0)
        e.memset(INDB[0:64, :], 0.0)
        e.memset(INDB[64:128, :], 1.0)
        e.memset(CST[:, C_LN2:C_LN2 + 1], LN2)
        e.memset(CST[:, C_3LN2:C_3LN2 + 1], 3.0 * LN2)
        e.memset(CST[:, C_EPS:C_EPS + 1], 1e-5)
        e.memset(CST[:, C_S1:C_S1 + 1], 0.125)
        return e.memset(SCL[:, 8:9], 128.0)
    s.emit("dve", consts_fn, reads=[], writes=[rC, rSCL])

    dve(lambda e: e.reciprocal(SCL[:, 2:3], SCL[:, 0:1]), reads=[rSCL], writes=[rSCL])
    dve(lambda e: e.tensor_tensor(SCL[:, 3:4], SCL[:, 0:1], SCL[:, 1:2], ALU.mult), reads=[rSCL], writes=[rSCL])
    act(lambda e: e.activation(SCL[:, 6:7], SCL[:, 8:9], AF.Ln), reads=[rSCL], writes=[rSCL])
    dve(lambda e: e.tensor_scalar(SCL[:, 6:7], SCL[:, 6:7], -0.5, None, ALU.mult), reads=[rSCL], writes=[rSCL])
    act(lambda e: e.activation(SCL[:, 6:7], SCL[:, 6:7], AF.Exp), reads=[rSCL], writes=[rSCL])
    dve(lambda e: e.reciprocal(SCL[:, 4:5], SCL[:, 3:4]), reads=[rSCL], writes=[rSCL])
    dve(lambda e: e.tensor_scalar(B0R[:], B0R[:], SCL[:, 4:5], None, ALU.mult), reads=[rROWS, rSCL], writes=[rROWS])
    dve(lambda e: e.tensor_scalar(BQC[:], BQC[:], 0.125, None, ALU.mult), reads=[rROWS], writes=[rROWS])
    mm([(PS[7][:, 0:1], ONES[0:1, 0:128], SCL[0:1, 3:4], True, True, None)], reads=[rC, rSCL], writes=[rPS[7]])
    act(lambda e: e.activation(CST[:, C_SS:C_SS + 1], PS[7][:, 0:1], AF.Identity), reads=[rPS[7]], writes=[rC])
    mm([(PS[7][:, 0:1], ONES[0:1, 0:128], SCL[0:1, 6:7], True, True, None)], reads=[rC, rSCL], writes=[rPS[7]])
    act(lambda e: e.activation(CST[:, C_S0:C_S0 + 1], PS[7][:, 0:1], AF.Identity), reads=[rPS[7]], writes=[rC])

    dma("gp", SELS[:], dr["sels"], writes=[rSEL])
    dma("gp", SELB[:], dr["selb"], writes=[rSEL])
    dve(lambda e: e.memset(TBR[:], 0.0), reads=[], writes=[rTBR])

    dma("gp", TT1[:], dr["tt1"], writes=[rTT])
    dma("gp", TT2[:], dr["tt2"], writes=[rTT])

    dma("sp", BRow, dr["iota"][0:1, 0:QBS], writes=[rRR])
    dve(lambda e: e.memset(TBS[0:1, :], 1.0), reads=[], writes=[rTBS])
    for dpos in range(4):
        mm([(PS[6][:, 0:QBS], BRow[0:1, 0:128], TBS[0:1, :], True, False, None)],
           reads=[rRR, rC, rTBS], writes=[rPS[6]])
        dve(lambda e, d_=dpos: e.tensor_scalar(SC1[0:1, 0:QBS], BRow, -1.0, 128.0 * d_, ALU.mult, ALU.add),
            reads=[rRR], writes=[rSC1])
        mm([(PS[6][:, 0:QBS], ONES[0:1, 0:128], SC1[0:1, 0:QBS], False, True, None)],
           reads=[rC, rSC1], writes=[rPS[6]])
        dve(lambda e: e.tensor_scalar(SC2[:], PS[6][:, 0:QBS], 0.0, -3.0e38, ALU.is_gt, ALU.mult),
            reads=[rPS[6]], writes=[rSC2])
        dve(lambda e, d_=dpos: e.tensor_copy(MSK[:, d_ * QBS:(d_ + 1) * QBS], SC2[:]),
            reads=[rSC2], writes=[rMSK])

    def ln_pass(quant=False):
        for b in range(NQB):
            xres = [rX[c][b] for c in range(NCH)]
            mm([(PS[6][0:1, 0:QBS], ONEC[:], Xs(c, b), c == 0, c == NCH - 1, None) for c in range(NCH)],
               reads=[rC] + xres, writes=[rPS[6]])
            for c in range(NCH):
                act(lambda e, c_=c, b_=b: e.activation(SC2[:], Xs(c_, b_), AF.Square),
                    reads=[rX[c][b]], writes=[rSC2])
                mm([(PS[6][32:33, 0:QBS], ONEC[:], SC2[:], c == 0, c == NCH - 1, None)],
                   reads=[rC, rSC2], writes=[rPS[6]])
            dve(lambda e: e.tensor_scalar(BRow, PS[6][0:1, 0:QBS], 1.0 / D, None, ALU.mult),
                reads=[rPS[6]], writes=[rRR])
            dve(lambda e: e.tensor_scalar(ARow, PS[6][32:33, 0:QBS], 1.0 / D, None, ALU.mult),
                reads=[rPS[6]], writes=[rRR])
            dve(lambda e: e.tensor_tensor(SC1[0:1, 0:QBS], BRow, BRow, ALU.mult),
                reads=[rRR], writes=[rSC1])
            dve(lambda e: e.tensor_tensor(ARow, ARow, SC1[0:1, 0:QBS], ALU.subtract),
                reads=[rRR, rSC1], writes=[rRR])
            act(lambda e: e.activation(ARow, ARow, AF.Sqrt, bias=CST[0:1, C_EPS:C_EPS + 1]),
                reads=[rRR, rC], writes=[rRR])
            dve(lambda e: e.reciprocal(ARow, ARow), reads=[rRR], writes=[rRR])
            if quant:
                dve(lambda e: e.tensor_scalar(ARow, ARow, SCL[0:1, 2:3], None, ALU.mult),
                    reads=[rRR, rSCL], writes=[rRR])
            dve(lambda e: e.scalar_tensor_tensor(BRow, BRow, -1.0, ARow, ALU.mult, ALU.mult),
                reads=[rRR], writes=[rRR])
            if not quant:
                dve(lambda e: e.tensor_copy(TBR[0:1, 0:QBS], ARow), reads=[rRR], writes=[rTBR])
                dve(lambda e: e.tensor_copy(TBR[0:1, QBS:2 * QBS], BRow), reads=[rRR], writes=[rTBR])
                mm([(PS[4][:, 0:QBS], ONESB[0:1, 0:128], TBR[0:1, 0:QBS], True, True, None)],
                   reads=[rC, rTBR], writes=[rPS[4]])
                mm([(PS[5][:, 0:QBS], ONESB[0:1, 0:128], TBR[0:1, QBS:2 * QBS], True, True, None)],
                   reads=[rC, rTBR], writes=[rPS[5]])
            else:
                mm([(PS[4][:, 0:QBS], ONES[0:1, 0:128], ARow, True, True, None)], reads=[rC, rRR], writes=[rPS[4]])
                mm([(PS[5][:, 0:QBS], ONES[0:1, 0:128], BRow, True, True, None)], reads=[rC, rRR], writes=[rPS[5]])
            for c in range(NCH):
                if not quant:
                    dve(lambda e, c_=c, b_=b: e.tensor_tensor(SC2[:], Xs(c_, b_), PS[4][:, 0:QBS], ALU.mult),
                        reads=[rX[c][b], rPS[4]], writes=[rSC2])
                    dve(lambda e, c_=c, b_=b: e.tensor_tensor(XNs(c_, b_), SC2[:], PS[5][:, 0:QBS], ALU.add),
                        reads=[rSC2, rPS[5]], writes=[rXN[c][b]])
                else:
                    quant_block(c, b)

    def quant_block(c, b):
        dve(lambda e, c_=c, b_=b: e.tensor_tensor(SC1[:], Xs(c_, b_), PS[4][:, 0:QBS], ALU.mult),
            reads=[rX[c][b], rPS[4]], writes=[rSC1])
        dve(lambda e: e.tensor_tensor(SC1[:], SC1[:], PS[5][:, 0:QBS], ALU.add),
            reads=[rSC1, rPS[5]], writes=[rSC1])
        dve(lambda e: e.tensor_scalar(SC1[:], SC1[:], 448.0, -448.0, ALU.min, ALU.max),
            reads=[rSC1], writes=[rSC1])
        act(lambda e: e.activation(SC2[:], SC1[:], AF.Square), reads=[rSC1], writes=[rSC2])
        dve(lambda e: e.tensor_scalar(SC2[:], SC2[:], 2.0 ** -24, None, ALU.max), reads=[rSC2], writes=[rSC2])
        act(lambda e: e.activation(SC2[:], SC2[:], AF.Ln), reads=[rSC2], writes=[rSC2])
        dve(lambda e: e.tensor_scalar(SC2[:], SC2[:], float(np.float32(0.5 / np.log(2.0))), None, ALU.mult),
            reads=[rSC2], writes=[rSC2])
        floorc(SC2[:], rSC2)
        dve(lambda e: e.tensor_copy(SI[:], SC2[:]), reads=[rSC2], writes=[rSI])
        dve(lambda e: e.tensor_scalar(SC2[:], SI[:], -6.0, -1.0, ALU.max, ALU.mult), reads=[rSI], writes=[rSC2])
        act(lambda e: e.activation(SC2[:], SC2[:], AF.Exp, bias=CST[:, C_3LN2:C_3LN2 + 1],
                                   scale=CST[:, C_LN2:C_LN2 + 1]),
            reads=[rSC2, rC], writes=[rSC2])
        dve(lambda e: e.tensor_scalar(SC2[:], SC2[:], 64.0, 0.5, ALU.mult, ALU.add), reads=[rSC2], writes=[rSC2])
        floorc(SC2[:], rSC2)
        dve(lambda e: e.tensor_tensor(SC1[:], SC1[:], SC2[:], ALU.mult), reads=[rSC1, rSC2], writes=[rSC1])
        dve(lambda e: e.tensor_scalar(SC1[:], SC1[:], 1.0 / 64.0, 0.5, ALU.mult, ALU.add),
            reads=[rSC1], writes=[rSC1])
        floorc2(SC1[:], rSC1)
        dve(lambda e: e.reciprocal(SC2[:], SC2[:]), reads=[rSC2], writes=[rSC2])
        dve(lambda e: e.tensor_tensor(SC1[:], SC1[:], SC2[:], ALU.mult), reads=[rSC1, rSC2], writes=[rSC1])
        dve(lambda e, c_=c, b_=b: e.tensor_scalar(XNs(c_, b_), SC1[:], 64.0, None, ALU.mult),
            reads=[rSC1], writes=[rXN[c][b]])

    for seq in range(n_seq):
        for c in range(NCH):
            dma("sp", X[:, c * T:(c + 1) * T], dr["x_fm"][seq * D + c * 128: seq * D + (c + 1) * 128, :],
                writes=[rX[c][b] for b in range(NQB)])

        if "l0a" not in phases:
            continue
        dma("gp", A4[:, 0:1024], dr["w0q"], writes=[rA4[0]])
        dma("gp", A4[:, 1024:2048], dr["w0k"], writes=[rA4[1]])
        dma("gp", A4[:, 2048:3072], dr["w0v"], writes=[rA4[2]])
        dma("gp", A4[:, 3072:4096], dr["w0o"], writes=[rA4[3]])
        ln_pass()
        for b in range(NQB):
            xnres = [rXN[c][b] for c in range(NCH)]
            mm([(PS[0][:, 0:QBS], A4[:, ic * 128:(ic + 1) * 128], XNs(ic, b), ic == 0, ic == NCH - 1, None)
                for ic in range(NCH)], reads=[rA4[0]] + xnres, writes=[rPS[0]])
            act(lambda e, b_=b: e.activation(A0[:, b_ * QBS:(b_ + 1) * QBS], PS[0][:, 0:QBS], AF.Identity,
                                             scale=CST[:, C_S0:C_S0 + 1]),
                reads=[rPS[0], rC], writes=[rA0[0]])
            mm([(PS[1][:, 0:QBS], A4[:, 1024 + ic * 128:1024 + (ic + 1) * 128], XNs(ic, b), ic == 0,
                 ic == NCH - 1, None) for ic in range(NCH)], reads=[rA4[1]] + xnres, writes=[rPS[1]])
            act(lambda e, b_=b: e.activation(A1[:, b_ * QBS:(b_ + 1) * QBS], PS[1][:, 0:QBS], AF.Identity),
                reads=[rPS[1]], writes=[rA1[0]])
            for tt in range(4):
                tta = b * 4 + tt
                mm([(PS[2][:, 0:128], XNt(ic, tta), A4[:, 2048 + ic * 128:2048 + (ic + 1) * 128],
                     ic == 0, ic == NCH - 1, None) for ic in range(NCH)],
                   reads=[rA4[2]] + xnres, writes=[rPS[2]])
                act(lambda e, t_=tta: e.activation(A2[:, t_ * 128:(t_ + 1) * 128], PS[2][:, 0:128], AF.Identity),
                    reads=[rPS[2]], writes=[rA2[tta // 4]])
        dve(lambda e: e.memset(SCL[0:1, 7:8], 0.0), reads=[rSCL], writes=[rSCL])
        for b in range(NQB):
            act(lambda e, b_=b: e.activation(SC2[:], A1[:, b_ * QBS:(b_ + 1) * QBS], AF.Square),
                reads=[rA1[0]], writes=[rSC2])
            mm([(PS[4][0:1, 0:QBS], ONEC[:], SC2[:], True, True, None)], reads=[rC, rSC2], writes=[rPS[4]])
            dve(lambda e: e.tensor_reduce(SC1[0:1, 0:1], PS[4][0:1, 0:QBS], mybir.AxisListType.X, ALU.max),
                reads=[rPS[4]], writes=[rSC1])
            dve(lambda e: e.tensor_tensor(SCL[0:1, 7:8], SCL[0:1, 7:8], SC1[0:1, 0:1], ALU.max),
                reads=[rSC1, rSCL], writes=[rSCL])
        dve(lambda e: e.memset(QMX[0:1, 0:1], 0.0), reads=[rQM], writes=[rQM])
        for qb in range(NQB):
            qsl = slice(qb * QBS, (qb + 1) * QBS)
            act(lambda e, q_=qsl: e.activation(SC2[:], A0[:, q_], AF.Square), reads=[rA0[0]], writes=[rSC2])
            mm([(PS[4][0:1, 0:QBS], ONEC[:], SC2[:], True, True, None)], reads=[rC, rSC2], writes=[rPS[4]])
            dve(lambda e: e.tensor_reduce(SC1[0:1, 0:1], PS[4][0:1, 0:QBS], mybir.AxisListType.X, ALU.max),
                reads=[rPS[4]], writes=[rSC1])
            dve(lambda e: e.tensor_tensor(QMX[0:1, 0:1], QMX[0:1, 0:1], SC1[0:1, 0:1], ALU.max),
                reads=[rSC1, rQM], writes=[rQM])
        dve(lambda e: e.tensor_scalar(QMX[0:1, 0:1], QMX[0:1, 0:1], SCL[0:1, 7:8], None, ALU.mult),
            reads=[rQM, rSCL], writes=[rQM])
        act(lambda e: e.activation(QMX[0:1, 0:1], QMX[0:1, 0:1], AF.Sqrt), reads=[rQM], writes=[rQM])
        dve(lambda e: e.tensor_scalar(QMX[0:1, 0:1], QMX[0:1, 0:1], -1.0, None, ALU.mult),
            reads=[rQM], writes=[rQM])
        mm([(PS[4][:, 0:1], ONES[0:1, 0:128], QMX[0:1, 0:1], True, True, None)], reads=[rC, rQM], writes=[rPS[4]])
        act(lambda e: e.activation(MHC[:, 0:1], PS[4][:, 0:1], AF.Identity), reads=[rPS[4]], writes=[rMH])
        OB0 = (3, 5, 6, 7)
        for qb in range(NQB):
            qsl = slice(qb * QBS, (qb + 1) * QBS)
            nkc = 4 * (qb + 1)
            obank = OB0[qb]

            def s_tile0(kc, qsl_=qsl):
                bank = 0 if (kc % 2 == 0) else 1
                mm([(PS[bank][:, 0:QBS], A1[:, kc * 128:(kc + 1) * 128], A0[:, qsl_], True, True, None)],
                   reads=[rA1[0], rA0[0]], writes=[rPS[bank]])
                return bank
            prev = s_tile0(0)
            for kc in range(nkc):
                bank = prev
                if kc + 1 < nkc:
                    prev = s_tile0(kc + 1)
                if kc >= 4 * qb:
                    dpos = kc - 4 * qb
                    dve(lambda e, bk=bank, d_=dpos: e.tensor_tensor(
                        PS[bk][:, 0:QBS], PS[bk][:, 0:QBS], MSK[:, d_ * QBS:(d_ + 1) * QBS], ALU.add),
                        reads=[rPS[bank], rMSK], writes=[rPS[bank]])
                pt = kc % 2
                ptap = A3[:, 2048 + pt * QBS: 2048 + (pt + 1) * QBS]
                act(lambda e, bk=bank, a_=ptap: e.activation(a_, PS[bk][:, 0:QBS], AF.Exp, bias=MHC[:, 0:1]),
                    reads=[rPS[bank], rMH], writes=[rA3[4 + pt]])
                mm([(PS[2][0:8, 0:QBS], SELS[:, qb * 8:(qb + 1) * 8], ptap,
                     qb == 0 and kc == 0, qb == NQB - 1 and kc == nkc - 1, None)],
                   reads=[rSEL, rA3[4 + pt]], writes=[rPS[2]])
                mm([(PS[obank][:, 0:QBS], A2[:, kc * 128:(kc + 1) * 128], ptap, kc == 0, kc == nkc - 1, None)],
                   reads=[rA2[kc // 4], rA3[4 + pt]], writes=[rPS[obank]])
        dve(lambda e: e.reciprocal(SC2[0:4, 0:QBS], PS[2][0:4, 0:QBS]), reads=[rPS[2]], writes=[rSC2])
        dve(lambda e: e.tensor_copy(TBR[0:4, 0:QBS], SC2[0:4, 0:QBS]), reads=[rSC2], writes=[rTBR])
        for qb in range(NQB):
            obank = OB0[qb]
            mm([(PS[4][:, 0:QBS], SELB[:, 512 + qb * 128: 512 + (qb + 1) * 128], TBR[:, 0:QBS], True, True, None)],
               reads=[rSEL, rTBR], writes=[rPS[4]])
            act(lambda e: e.activation(TBS[:], PS[4][:, 0:QBS], AF.Identity), reads=[rPS[4]], writes=[rTBS])
            dve(lambda e, bk=obank, q_=qb: e.tensor_tensor(A3[:, q_ * QBS:(q_ + 1) * QBS], PS[bk][:, 0:QBS],
                                                           TBS[:], ALU.mult),
                reads=[rPS[obank], rTBS], writes=[rA3[qb]])
        for qb in range(NQB):
            for dm in range(NCH):
                bank = 3 if (dm % 2 == 0) else 5
                mm([(PS[bank][:, 0:QBS], A4[:, 3072 + dm * 128:3072 + (dm + 1) * 128],
                     A3[:, qb * QBS:(qb + 1) * QBS], True, True, None)],
                   reads=[rA4[3], rA3[qb]], writes=[rPS[bank]])
                dve(lambda e, d_=dm, bk=bank, q_=qb: e.tensor_tensor(Xs(d_, q_), Xs(d_, q_),
                                                                     PS[bk][:, 0:QBS], ALU.add),
                    reads=[rPS[bank], rX[dm][qb]], writes=[rX[dm][qb]])

        if debug:
            for c in range(NCH):
                dma("sp", dbg["dbg0a"][seq * D + c * 128: seq * D + (c + 1) * 128, :], X[:, c * T:(c + 1) * T],
                    reads=[rX[c][b] for b in range(NQB)])
        if "l0f" not in phases:
            continue
        dma("gp", A0[:, :], dr["w8T"], writes=[rA0[0], rA0[1], rA0[2], rA0[3]])
        ln_pass(quant=True)
        for b in range(NQB):
            xnres = [rXN[c][b] for c in range(NCH)]
            for oc in range(NCH):
                bank = oc % 2
                ms = [(PS[bank][:, 0:QBS], B0R[0:1, oc * 128:(oc + 1) * 128], ONESB[0:1, 0:QBS], True, False, None)]
                ms += [(PS[bank][:, 0:QBS], A0[:, ic * 1024 + oc * 128: ic * 1024 + (oc + 1) * 128],
                        XNs(ic, b), False, ic == NCH - 1, None) for ic in range(NCH)]
                mm(ms, reads=[rROWS, rC, rA0[0], rA0[1], rA0[2], rA0[3]] + xnres, writes=[rPS[bank]])
                dve(lambda e, o_=oc, b_=b, bk=bank: e.scalar_tensor_tensor(
                    Xs(o_, b_), PS[bk][:, 0:QBS], CST[:, C_SS:C_SS + 1], Xs(o_, b_), ALU.mult, ALU.add),
                    reads=[rPS[bank], rC, rX[oc][b]], writes=[rX[oc][b]])

        if debug:
            for c in range(NCH):
                dma("sp", dbg["dbg0f"][seq * D + c * 128: seq * D + (c + 1) * 128, :], X[:, c * T:(c + 1) * T],
                    reads=[rX[c][b] for b in range(NQB)])
        if "l1a" not in phases:
            continue
        ln_pass()
        for g in range(NGRP):
            dma("gp", A4[:, 0:4096], dr["w1q"][g * 128:(g + 1) * 128, :],
                writes=[rA4[0], rA4[1], rA4[2], rA4[3]])
            dma("gp", A4[:, 4096:8192], dr["w1k"][g * 128:(g + 1) * 128, :],
                writes=[rA4[4], rA4[5], rA4[6], rA4[7]])
            for cc in range(CPG):
                for b in range(NQB):
                    xnres = [rXN[c][b] for c in range(NCH)]
                    mm([(PS[0][:, 0:QBS], A4[:, ic * 512 + cc * 128: ic * 512 + (cc + 1) * 128],
                         XNs(ic, b), ic == 0, ic == NCH - 1, None) for ic in range(NCH)],
                       reads=[rA4[0], rA4[1], rA4[2], rA4[3]] + xnres, writes=[rPS[0]])
                    act(lambda e, c_=cc, b_=b, g_=g: e.activation(
                        A0[:, c_ * T + b_ * QBS: c_ * T + (b_ + 1) * QBS], PS[0][:, 0:QBS], AF.Identity,
                        bias=BQC[:, g_ * 4 + c_: g_ * 4 + c_ + 1], scale=CST[:, C_S1:C_S1 + 1]),
                        reads=[rPS[0], rROWS, rC], writes=[rA0[cc]])
                    mm([(PS[1][:, 0:QBS], A4[:, 4096 + ic * 512 + cc * 128: 4096 + ic * 512 + (cc + 1) * 128],
                         XNs(ic, b), ic == 0, ic == NCH - 1, None) for ic in range(NCH)],
                       reads=[rA4[4], rA4[5], rA4[6], rA4[7]] + xnres, writes=[rPS[1]])
                    act(lambda e, c_=cc, b_=b, g_=g: e.activation(
                        A1[:, c_ * T + b_ * QBS: c_ * T + (b_ + 1) * QBS], PS[1][:, 0:QBS], AF.Identity,
                        bias=BKC[:, g_ * 4 + c_: g_ * 4 + c_ + 1]),
                        reads=[rPS[1], rROWS], writes=[rA1[cc]])
            dma("gp", A4[:, 0:4096], dr["w1v"][g * 128:(g + 1) * 128, :],
                writes=[rA4[0], rA4[1], rA4[2], rA4[3]])
            for tt in range(16):
                b = tt // 4
                xnres = [rXN[c][b] for c in range(NCH)]
                ms = [(PS[2][:, 0:QBS], ONESB[0:1, 0:128], BVR[0:1, g * QBS:(g + 1) * QBS], True, False, None)]
                ms += [(PS[2][:, 0:QBS], XNt(ic, tt), A4[:, ic * 512:(ic + 1) * 512], False, ic == NCH - 1, None)
                       for ic in range(NCH)]
                mm(ms, reads=[rC, rROWS, rA4[0], rA4[1], rA4[2], rA4[3]] + xnres, writes=[rPS[2]])
                act(lambda e, t_=tt: e.activation(A2[:, t_ * QBS:(t_ + 1) * QBS], PS[2][:, 0:QBS], AF.Identity),
                    reads=[rPS[2]], writes=[rA2[tt]])
            for (buf, rbuf) in ((A0, rA0), (A1, rA1)):
                for cc in range(CPG):
                    csl = slice(cc * T, (cc + 1) * T)
                    for (dst0, src0) in ((0, 32), (32, 0), (64, 96), (96, 64)):
                        dma("sp", A3[dst0:dst0 + 32, 2048:4096], buf[src0:src0 + 32, csl],
                            reads=[rbuf[cc]], writes=list(rA3[4:8]))
                    dve(lambda e, b_=buf, c_=csl: e.tensor_tensor(A3[:, 0:2048], b_[:, c_], TT1[:], ALU.mult),
                        reads=[rbuf[cc], rTT], writes=list(rA3[0:4]))
                    dve(lambda e: e.tensor_tensor(A3[:, 2048:4096], A3[:, 2048:4096], TT2[:], ALU.mult),
                        reads=list(rA3[4:8]) + [rTT], writes=list(rA3[4:8]))
                    dve(lambda e, b_=buf, c_=csl: e.tensor_tensor(b_[:, c_], A3[:, 0:2048], A3[:, 2048:4096],
                                                                  ALU.add),
                        reads=list(rA3[0:8]), writes=[rbuf[cc]])
            dma("gp", A4[:, 4096:8192], dr["w1o"][g * 128:(g + 1) * 128, :],
                writes=[rA4[4], rA4[5], rA4[6], rA4[7]])
            dve(lambda e: e.memset(SCL[0:1, 7:8], 0.0), reads=[rSCL], writes=[rSCL])
            for b in range(NQB):
                for cc in range(CPG):
                    act(lambda e, c_=cc, b_=b: e.activation(SC2[:], A1[:, c_ * T + b_ * QBS: c_ * T + (b_ + 1) * QBS],
                                                            AF.Square),
                        reads=[rA1[cc]], writes=[rSC2])
                    mm([(PS[4][0:1, 0:QBS], ONEC[:], SC2[:], cc == 0, cc == CPG - 1, None)],
                       reads=[rC, rSC2], writes=[rPS[4]])
                dve(lambda e: e.tensor_reduce(SC1[0:1, 0:1], PS[4][0:1, 0:QBS], mybir.AxisListType.X, ALU.max),
                    reads=[rPS[4]], writes=[rSC1])
                dve(lambda e: e.tensor_tensor(SCL[0:1, 7:8], SCL[0:1, 7:8], SC1[0:1, 0:1], ALU.max),
                    reads=[rSC1, rSCL], writes=[rSCL])
            for cc in range(CPG):
                dve(lambda e, c_=cc: e.memset(QMX[0:1, 2 * c_:2 * c_ + 2], 0.0), reads=[rQM], writes=[rQM])
                for qb in range(NQB):
                    qsl = slice(cc * T + qb * QBS, cc * T + (qb + 1) * QBS)
                    act(lambda e, q_=qsl: e.activation(SC2[:], A0[:, q_], AF.Square), reads=[rA0[cc]], writes=[rSC2])
                    mm([(PS[4][0:1, 0:QBS], INDA[:], SC2[:], True, True, None),
                        (PS[4][32:33, 0:QBS], INDB[:], SC2[:], True, True, None)],
                       reads=[rC, rSC2], writes=[rPS[4]])
                    for (r0, hoff) in ((0, 0), (32, 1)):
                        dve(lambda e, r_=r0: e.tensor_reduce(SC1[0:1, 0:1], PS[4][r_:r_ + 1, 0:QBS],
                                                             mybir.AxisListType.X, ALU.max),
                            reads=[rPS[4]], writes=[rSC1])
                        dve(lambda e, c_=cc, h_=hoff: e.tensor_tensor(
                            QMX[0:1, 2 * c_ + h_:2 * c_ + h_ + 1], QMX[0:1, 2 * c_ + h_:2 * c_ + h_ + 1],
                            SC1[0:1, 0:1], ALU.max),
                            reads=[rSC1, rQM], writes=[rQM])
            dve(lambda e: e.tensor_scalar(QMX[0:1, 0:8], QMX[0:1, 0:8], SCL[0:1, 7:8], None, ALU.mult),
                reads=[rQM, rSCL], writes=[rQM])
            act(lambda e: e.activation(QMX[0:1, 0:8], QMX[0:1, 0:8], AF.Sqrt), reads=[rQM], writes=[rQM])
            dve(lambda e: e.tensor_scalar(QMX[0:1, 0:8], QMX[0:1, 0:8], -1.0, None, ALU.mult),
                reads=[rQM], writes=[rQM])
            mm([(PS[4][:, 0:8], ONES[0:1, 0:128], QMX[0:1, 0:8], True, True, None)], reads=[rC, rQM], writes=[rPS[4]])
            act(lambda e: e.activation(MHC[:, 0:8], PS[4][:, 0:8], AF.Identity), reads=[rPS[4]], writes=[rMH])
            for qb in range(NQB):
                nkc = 4 * (qb + 1)
                for cc in range(CPG):
                    qsl = slice(cc * T + qb * QBS, cc * T + (qb + 1) * QBS)

                    def s_tile1(kc, cc_=cc, qsl_=qsl):
                        bA = 0 if (kc % 2 == 0) else 6
                        bB = 1 if (kc % 2 == 0) else 7
                        ksl = slice(cc_ * T + kc * 128, cc_ * T + (kc + 1) * 128)
                        mm([(PS[bA][:, 0:QBS], A1[0:64, ksl], A0[0:64, qsl_], True, True, None)],
                           reads=[rA1[cc_], rA0[cc_]], writes=[rPS[bA]])
                        mm([(PS[bB][:, 0:QBS], A1[64:128, ksl], A0[64:128, qsl_], True, True, None)],
                           reads=[rA1[cc_], rA0[cc_]], writes=[rPS[bB]])
                        return bA, bB
                    prev = s_tile1(0)
                    for kc in range(nkc):
                        bA, bB = prev
                        if kc + 1 < nkc:
                            prev = s_tile1(kc + 1)
                        if kc >= 4 * qb:
                            dpos = kc - 4 * qb
                            for bk in (bA, bB):
                                dve(lambda e, b_=bk, d_=dpos: e.tensor_tensor(
                                    PS[b_][:, 0:QBS], PS[b_][:, 0:QBS], MSK[:, d_ * QBS:(d_ + 1) * QBS], ALU.add),
                                    reads=[rPS[bk], rMSK], writes=[rPS[bk]])
                        ptA, ptB = (kc % 2) * 2, (kc % 2) * 2 + 1
                        apA = A3[:, 2048 + ptA * QBS: 2048 + (ptA + 1) * QBS]
                        apB = A3[:, 2048 + ptB * QBS: 2048 + (ptB + 1) * QBS]
                        act(lambda e, b_=bA, a_=apA, c_=cc: e.activation(
                            a_, PS[b_][:, 0:QBS], AF.Exp, bias=MHC[:, 2 * c_:2 * c_ + 1]),
                            reads=[rPS[bA], rMH], writes=[rA3[4 + ptA]])
                        act(lambda e, b_=bB, a_=apB, c_=cc: e.activation(
                            a_, PS[b_][:, 0:QBS], AF.Exp, bias=MHC[:, 2 * c_ + 1:2 * c_ + 2]),
                            reads=[rPS[bB], rMH], writes=[rA3[4 + ptB]])
                        first = (cc == 0 and kc == 0)
                        last = (cc == CPG - 1 and kc == nkc - 1)
                        mm([(PS[2][0:8, 0:QBS], SELS[:, (2 * cc) * 8:(2 * cc) * 8 + 8], apA,
                             first, False, None),
                            (PS[2][0:8, 0:QBS], SELS[:, (2 * cc + 1) * 8:(2 * cc + 1) * 8 + 8], apB,
                             False, last, None)],
                           reads=[rSEL, rA3[4 + ptA], rA3[4 + ptB]], writes=[rPS[2]])
                        lA = 2 * cc * DH
                        lB = (2 * cc + 1) * DH
                        mm([(PS[3][0:64, 0:QBS], A2[:, kc * QBS + lA: kc * QBS + lA + DH], apA,
                             kc == 0, kc == nkc - 1, (0, 0)),
                            (PS[3][64:128, 0:QBS], A2[:, kc * QBS + lB: kc * QBS + lB + DH], apB,
                             kc == 0, kc == nkc - 1, (0, 64), True)],
                           reads=[rA2[kc], rA3[4 + ptA], rA3[4 + ptB]], writes=[rPS[3]])
                    act(lambda e, c_=cc: e.activation(A3[:, c_ * QBS:(c_ + 1) * QBS], PS[3][:, 0:QBS],
                                                      AF.Identity),
                        reads=[rPS[3]], writes=[rA3[cc]])
                dve(lambda e: e.reciprocal(SC2[0:8, 0:QBS], PS[2][0:8, 0:QBS]), reads=[rPS[2]], writes=[rSC2])
                dve(lambda e: e.tensor_copy(TBR[:, 0:QBS], SC2[0:8, 0:QBS]), reads=[rSC2], writes=[rTBR])
                for cc in range(CPG):
                    mm([(PS[4][:, 0:QBS], SELB[:, cc * 128:(cc + 1) * 128], TBR[:, 0:QBS], True, True, None)],
                       reads=[rSEL, rTBR], writes=[rPS[4]])
                    act(lambda e: e.activation(TBSB[:], PS[4][:, 0:QBS], AF.Identity), reads=[rPS[4]], writes=[rTBSB])
                    dve(lambda e, c_=cc: e.tensor_tensor(A3[:, c_ * QBS:(c_ + 1) * QBS],
                                                         A3[:, c_ * QBS:(c_ + 1) * QBS], TBSB[:], ALU.mult),
                        reads=[rA3[cc], rTBSB], writes=[rA3[cc]])
                for dm in range(NCH):
                    bank = 5 if (dm % 2 == 0) else 6
                    ms = []
                    if g == 0:
                        ms.append((PS[bank][:, 0:QBS], BOR[0:1, dm * 128:(dm + 1) * 128], ONESB[0:1, 0:QBS],
                                   True, False, None))
                    ms += [(PS[bank][:, 0:QBS], A4[:, 4096 + cc * 1024 + dm * 128: 4096 + cc * 1024 + (dm + 1) * 128],
                            A3[:, cc * QBS:(cc + 1) * QBS], (g != 0 and cc == 0), cc == CPG - 1, None)
                           for cc in range(CPG)]
                    mm(ms, reads=[rROWS, rC, rA4[4], rA4[5], rA4[6], rA4[7]] + list(rA3[0:4]), writes=[rPS[bank]])
                    dve(lambda e, d_=dm, q_=qb, bk=bank: e.tensor_tensor(Xs(d_, q_), Xs(d_, q_),
                                                                         PS[bk][:, 0:QBS], ALU.add),
                        reads=[rPS[bank], rX[dm][qb]], writes=[rX[dm][qb]])

        if debug:
            for c in range(NCH):
                dma("sp", dbg["dbg1a"][seq * D + c * 128: seq * D + (c + 1) * 128, :], X[:, c * T:(c + 1) * T],
                    reads=[rX[c][b] for b in range(NQB)])
        if "l1f" not in phases:
            continue
        ln_pass()
        for fs in range(NFS):
            dma("gp", A0[:, :], dr["w1f"][fs * 128:(fs + 1) * 128, :], writes=[rA0[0], rA0[1], rA0[2], rA0[3]])
            dma("gp", A1[:, :], dr["w2f"][fs * 128:(fs + 1) * 128, :], writes=[rA1[0], rA1[1], rA1[2], rA1[3]])
            wread = [rA0[0], rA0[1], rA0[2], rA0[3]]
            w2read = [rA1[0], rA1[1], rA1[2], rA1[3]]
            for b in range(NQB):
                xnres = [rXN[c][b] for c in range(NCH)]
                for hc in range(NCH):
                    bank = hc % 2
                    mm([(PS[bank][:, 0:QBS], A0[:, ic * 1024 + hc * 128: ic * 1024 + (hc + 1) * 128],
                         XNs(ic, b), ic == 0, ic == NCH - 1, None) for ic in range(NCH)],
                       reads=wread + xnres, writes=[rPS[bank]])
                    act(lambda e, h_=hc, f_=fs, bk=bank: e.activation(
                        A2[:, h_ * QBS:(h_ + 1) * QBS], PS[bk][:, 0:QBS], AF.Gelu,
                        bias=B1C[:, f_ * 8 + h_: f_ * 8 + h_ + 1]),
                        reads=[rPS[bank], rROWS], writes=[rA2[hc]])
                for dm in range(NCH):
                    bank = 2 + (dm % 2)
                    ms = []
                    if fs == 0:
                        ms.append((PS[bank][:, 0:QBS], B2R[0:1, dm * 128:(dm + 1) * 128], ONESB[0:1, 0:QBS],
                                   True, False, None))
                    ms += [(PS[bank][:, 0:QBS], A1[:, hc * 1024 + dm * 128: hc * 1024 + (dm + 1) * 128],
                            A2[:, hc * QBS:(hc + 1) * QBS], (fs != 0 and hc == 0), hc == NCH - 1, None)
                           for hc in range(NCH)]
                    mm(ms, reads=[rROWS, rC] + w2read + [rA2[hc] for hc in range(NCH)], writes=[rPS[bank]])
                    dve(lambda e, d_=dm, b_=b, bk=bank: e.tensor_tensor(Xs(d_, b_), Xs(d_, b_),
                                                                        PS[bk][:, 0:QBS], ALU.add),
                        reads=[rPS[bank], rX[dm][b]], writes=[rX[dm][b]])

        for c in range(NCH):
            dma("sp", out[seq * D + c * 128: seq * D + (c + 1) * 128, :], X[:, c * T:(c + 1) * T],
                reads=[rX[c][b] for b in range(NQB)])

    with ctx:
        sems = {e: ctx.enter_context(nc.semaphore("sem_" + e)) for e in ("te", "act", "dve")}
        qsems = {q: [ctx.enter_context(nc.semaphore("sem_%s%d" % (q, i))) for i in range(QDEPTH)]
                 for q in QUEUES}

        def do_wait(eng, pe, pt):
            if pe in QUEUES:
                eng.wait_ge(qsems[pe][(pt - 1) % QDEPTH], 16 * ((pt - 1) // QDEPTH + 1))
            else:
                eng.wait_ge(sems[pe], pt)

        def run_queue(eng, q):
            for i, (waits, fn) in enumerate(s.stream[q]):
                for pe, pt in waits:
                    do_wait(eng, pe, pt)
                fn(eng).then_inc(qsems[q][i % QDEPTH], 16)
            for t in range(max(0, s.tick[q] - QDEPTH), s.tick[q]):
                do_wait(eng, q, t + 1)

        def run_eng(eng, e):
            for waits, fn in s.stream[e]:
                for pe, pt in waits:
                    do_wait(eng, pe, pt)
                fn(eng).then_inc(sems[e], 1)

        with nc.Block() as block:
            @block.sync
            def _(sync):
                run_queue(sync, "sp")

            @block.gpsimd
            def _(gp):
                run_queue(gp, "gp")

            @block.tensor
            def _(te):
                run_eng(te, "te")

            @block.scalar
            def _(sc):
                run_eng(sc, "act")

            @block.vector
            def _(ve):
                run_eng(ve, "dve")
    n_inst = sum(len(v) for v in s.stream.values())
    return nc, n_inst


def kernel(**inputs):
    L = host_layout(inputs)
    nc, n_inst = build_nc()
    in_maps = []
    for core in range(8):
        m = {nm: L[nm] for nm, _ in DRAM_SPECS if nm != "x_fm"}
        m["x_fm"] = np.ascontiguousarray(L["x_fm"][core * 2 * D:(core + 1) * 2 * D, :])
        in_maps.append(m)
    res = run_bass_kernel_spmd(nc, in_maps, core_ids=list(range(8)))
    outs = []
    for core in range(8):
        o = np.asarray(res.results[core]["out_fm"]).reshape(2, D, T)
        outs.append(np.transpose(o, (0, 2, 1)))
    return np.concatenate(outs, 0)


# version 07
# speedup vs baseline: 1.7149x; 1.0945x over previous
import contextlib
import numpy as np
import concourse.bass as bass
import concourse.mybir as mybir
from concourse.bass_utils import run_bass_kernel_spmd

F32 = mybir.dt.float32
BF16 = mybir.dt.bfloat16
I32 = mybir.dt.int32
ALU = mybir.AluOpType
AF = mybir.ActivationFunctionType

T = 2048
D = 1024
NCH = 8
QBS = 512
NQB = T // QBS
H = 16
DH = 64
NGRP = 2
CPG = 4
FF = 4096
NFS = 4

LN2 = float(np.float32(np.log(2.0)))

ENGS = ("te", "act", "dve", "sp", "gp")
STEP = {"te": 1, "act": 1, "dve": 1, "sp": 16, "gp": 16}


class Res:
    __slots__ = ("w", "r")

    def __init__(self):
        self.w = None
        self.r = []


QUEUES = ("sp", "gp")
QDEPTH = 6


class Sched:

    def __init__(self):
        self.stream = {e: [] for e in ENGS}
        self.tick = {e: 0 for e in ENGS}
        self.seen = {e: {} for e in ENGS}

    def _covered(self, e, pe, pt):
        key = (pe, (pt - 1) % QDEPTH) if pe in QUEUES else pe
        return self.seen[e].get(key, 0) >= pt

    def _cover(self, e, pe, pt):
        key = (pe, (pt - 1) % QDEPTH) if pe in QUEUES else pe
        if self.seen[e].get(key, 0) < pt:
            self.seen[e][key] = pt

    def emit(self, e, fn, reads=(), writes=()):
        need = {}

        def add(pe, pt):
            key = (pe, (pt - 1) % QDEPTH) if pe in QUEUES else pe
            if pt > need.get(key, (None, 0))[1]:
                need[key] = (pe, pt)

        for r in reads:
            if r.w is not None:
                add(*r.w)
        for w in writes:
            if w.w is not None:
                add(*w.w)
            for pe, pt in w.r:
                add(pe, pt)
        waits = []
        self.tick[e] += 1
        t = self.tick[e]
        if e in QUEUES and t > QDEPTH:
            pw = t - QDEPTH
            if not self._covered(e, e, pw):
                waits.append((e, pw))
                self._cover(e, e, pw)
        for key in sorted(need, key=str):
            pe, pt = need[key]
            if pe == "te" and e == "te":
                continue
            if self._covered(e, pe, pt):
                continue
            waits.append((pe, pt))
            self._cover(e, pe, pt)
        for r in reads:
            r.r.append((e, t))
        for w in writes:
            w.w = (e, t)
            w.r = []
        self.stream[e].append((waits, fn))


def host_layout(inputs):
    L = {}
    x = np.asarray(inputs["L0_attn_x"], dtype=np.float32)
    L["x_fm"] = np.ascontiguousarray(np.transpose(x, (0, 2, 1))).reshape(16 * D, T)

    def img(w):
        w = np.asarray(w, dtype=np.float32)
        k, n = w.shape
        return np.ascontiguousarray(w.reshape(k // 128, 128, n).transpose(1, 0, 2).reshape(128, (k // 128) * n))

    L["w0q"] = img(inputs["L0_attn_Wq"])
    L["w0k"] = img(inputs["L0_attn_Wk"])
    L["w0v"] = img(inputs["L0_attn_Wv"])
    L["w0o"] = np.ascontiguousarray(np.asarray(inputs["L0_attn_Wo"], dtype=np.float32))
    L["w8T"] = img(np.asarray(inputs["L0_ffn_weight_fp8"]).T)
    L["b0row"] = np.asarray(inputs["L0_ffn_bias"], dtype=np.float32).reshape(1, D)
    L["s_in"] = np.asarray(inputs["L0_ffn_input_scale"], dtype=np.float32).reshape(1, 1)
    L["s_w"] = np.asarray(inputs["L0_ffn_weight_scale"], dtype=np.float32).reshape(1, 1)

    perm = np.zeros(D, dtype=np.int64)
    for h in range(H):
        b = h * DH
        perm[b:b + 32] = b + np.arange(0, DH, 2)
        perm[b + 32:b + DH] = b + 1 + np.arange(0, DH, 2)
    wq = np.asarray(inputs["L1_attn_Wq"]).T[:, perm]
    wk = np.asarray(inputs["L1_attn_Wk"]).T[:, perm]
    L["w1q"] = np.concatenate([img(wq[:, g * 512:(g + 1) * 512]) for g in range(NGRP)], 0)
    L["w1k"] = np.concatenate([img(wk[:, g * 512:(g + 1) * 512]) for g in range(NGRP)], 0)
    wv = np.asarray(inputs["L1_attn_Wv"]).T
    L["w1v"] = np.concatenate([img(wv[:, g * 512:(g + 1) * 512]) for g in range(NGRP)], 0)
    wo = np.asarray(inputs["L1_attn_Wo"]).T
    L["w1o"] = np.concatenate([img(wo[g * 512:(g + 1) * 512, :]) for g in range(NGRP)], 0)
    bq = np.asarray(inputs["L1_attn_bq"])[perm]
    bk = np.asarray(inputs["L1_attn_bk"])[perm]
    L["bqcol"] = np.ascontiguousarray(bq.reshape(NCH, 128).T.astype(np.float32))
    L["bkcol"] = np.ascontiguousarray(bk.reshape(NCH, 128).T.astype(np.float32))
    L["bvrow"] = np.asarray(inputs["L1_attn_bv"], dtype=np.float32).reshape(1, D)
    L["borow"] = np.asarray(inputs["L1_attn_bo"], dtype=np.float32).reshape(1, D)

    w1f = np.asarray(inputs["L1_ffn_w1"])
    L["w1f"] = np.concatenate([img(w1f[:, s * 1024:(s + 1) * 1024]) for s in range(NFS)], 0)
    w2f = np.asarray(inputs["L1_ffn_w2"])
    L["w2f"] = np.concatenate([img(w2f[s * 1024:(s + 1) * 1024, :]) for s in range(NFS)], 0)
    L["b1col"] = np.ascontiguousarray(np.asarray(inputs["L1_ffn_b1"]).reshape(32, 128).T.astype(np.float32))
    L["b2row"] = np.asarray(inputs["L1_ffn_b2"], dtype=np.float32).reshape(1, D)
    L["iota"] = np.arange(T, dtype=np.float32).reshape(1, T)
    sels = np.zeros((128, 64), dtype=np.float32)
    for j in range(8):
        sels[:, j * 8 + j] = 1.0
    L["sels"] = sels
    selb = np.zeros((8, 1024), dtype=np.float32)
    for cc in range(4):
        selb[2 * cc, cc * 128: cc * 128 + 64] = 1.0
        selb[2 * cc + 1, cc * 128 + 64: cc * 128 + 128] = 1.0
    for qb in range(4):
        selb[qb, 512 + qb * 128: 512 + (qb + 1) * 128] = 1.0
    L["selb"] = selb
    try:
        import jax
        import jax.numpy as jnp
        with jax.default_device(jax.local_devices(backend="cpu")[0]):
            theta_j = 10000.0 ** (-jnp.arange(0, DH, 2, dtype=jnp.float32) / DH)
            seq_j = jnp.arange(T, dtype=jnp.float32)
            fr_j = jnp.einsum('i,j->ij', seq_j, theta_j)
            sn = np.asarray(jnp.sin(fr_j)).astype(np.float32).T
            cs = np.asarray(jnp.cos(fr_j)).astype(np.float32).T
    except Exception:
        theta = (10000.0 ** (-np.arange(0, DH, 2, dtype=np.float32) / DH)).astype(np.float32)
        fr = np.einsum('i,j->ij', np.arange(T, dtype=np.float32), theta).astype(np.float32)
        sn = np.sin(fr).astype(np.float32).T
        cs = np.cos(fr).astype(np.float32).T
    L["tt1"] = np.ascontiguousarray(np.tile(cs, (4, 1)))
    L["tt2"] = np.ascontiguousarray(np.concatenate([-sn, sn, -sn, sn], 0))
    return L


DRAM_SPECS = [
    ("x_fm", [2 * D, T]), ("w0q", [128, 1024]), ("w0k", [128, 1024]), ("w0v", [128, 1024]),
    ("w0o", [128, 1024]), ("w8T", [128, 8 * 1024]), ("b0row", [1, D]), ("s_in", [1, 1]), ("s_w", [1, 1]),
    ("w1q", [256, 4096]), ("w1k", [256, 4096]), ("w1v", [256, 4096]), ("w1o", [256, 4096]),
    ("bqcol", [128, 8]), ("bkcol", [128, 8]), ("bvrow", [1, D]), ("borow", [1, D]),
    ("w1f", [512, 8192]), ("w2f", [512, 8192]), ("b1col", [128, 32]), ("b2row", [1, D]),
    ("sels", [128, 64]), ("selb", [8, 1024]),
    ("iota", [1, T]), ("tt1", [128, T]), ("tt2", [128, T]),
]


def build_nc(n_seq=2, debug=False, phases=("l0a", "l0f", "l1a", "l1f")):
    nc = bass.Bass("TRN2", target_bir_lowering=False, debug=False)
    dr = {nm: nc.dram_tensor(nm, shp, F32, kind="ExternalInput").ap() for nm, shp in DRAM_SPECS}
    out = nc.dram_tensor("out_fm", [n_seq * D, T], F32, kind="ExternalOutput").ap()
    dbg = {}
    if debug:
        for nm in ("dbg0a", "dbg0f", "dbg1a"):
            dbg[nm] = nc.dram_tensor(nm, [n_seq * D, T], F32, kind="ExternalOutput").ap()

    ctx = contextlib.ExitStack()
    sb = lambda nm, shp, dt: ctx.enter_context(nc.sbuf_tensor(nm, shp, dt))

    X = sb("X", [128, NCH * T], F32)
    XN = sb("XN", [128, NCH * T], BF16)
    A0 = sb("A0", [128, 8192], BF16)
    A1 = sb("A1", [128, 8192], BF16)
    A2 = sb("A2", [128, 8192], BF16)
    A3 = sb("A3", [128, 4096], BF16)
    A4 = sb("A4", [128, 8192], BF16)
    TT1 = sb("TT1", [128, T], BF16)
    TT2 = sb("TT2", [128, T], BF16)
    MSK = sb("MSK", [128, 4 * QBS], BF16)
    SELS = sb("SELS", [128, 64], BF16)
    SELB = sb("SELB", [8, 1024], BF16)
    TBR = sb("TBR", [8, 1024], BF16)
    TBSB = sb("TBSB", [128, QBS], BF16)
    MHC = sb("MHC", [128, 16], F32)
    QMX = sb("QMX", [1, 32], F32)
    RR = sb("RR", [1, 1024], F32)
    TBS = sb("TBS", [128, 512], F32)
    SC1 = sb("SC1", [128, 512], F32)
    SC2 = sb("SC2", [128, 512], F32)
    SI = sb("SI", [128, 512], I32)
    B0R = sb("B0R", [1, D], BF16)
    BVR = sb("BVR", [1, D], BF16)
    BOR = sb("BOR", [1, D], BF16)
    B2R = sb("B2R", [1, D], BF16)
    BQC = sb("BQC", [128, NCH], F32)
    BKC = sb("BKC", [128, NCH], F32)
    B1C = sb("B1C", [128, 32], F32)
    ONES = sb("ONES", [33, 128], F32)
    ONESB = sb("ONESB", [128, QBS], BF16)
    ONEC = sb("ONEC", [128, 1], F32)
    INDA = sb("INDA", [128, 1], BF16)
    INDB = sb("INDB", [128, 1], BF16)
    CST = sb("CST", [128, 8], F32)
    SCL = sb("SCL", [1, 16], F32)
    PS = [ctx.enter_context(nc.psum_tensor("PS%d" % i, [128, 512], F32)) for i in range(8)]

    C_LN2, C_3LN2, C_EPS, C_S1, C_SS, C_S0 = 0, 1, 2, 3, 4, 5

    s = Sched()
    R = lambda: Res()
    rX = [[R() for b in range(NQB)] for c in range(NCH)]
    rXN = [[R() for b in range(NQB)] for c in range(NCH)]
    rA0 = [R() for c in range(CPG)]
    rA1 = [R() for c in range(CPG)]
    rA2 = [R() for i in range(16)]
    rA3 = [R() for i in range(8)]
    rA4 = [R() for i in range(8)]
    rTT, rMSK = R(), R()
    rMH, rQM, rRR, rTBS = R(), R(), R(), R()
    rSEL, rTBR, rTBSB = R(), R(), R()
    rSC1, rSC2, rSI = R(), R(), R()
    rROWS, rC, rSCL = R(), R(), R()
    rPS = [R() for i in range(8)]

    def dma(q, outap, inap, reads=(), writes=()):
        s.emit(q, lambda e, o=outap, i=inap: e.dma_start(out=o, in_=i), reads=reads, writes=writes)

    def mm(ms, reads, writes):
        def fn(e, ms_=tuple(ms)):
            last = None
            for m in ms_:
                (o, l, r_, st, sp, tp), skip = m[:6], (len(m) > 6 and m[6])
                last = e.matmul(o, l, r_, start=st, stop=sp, tile_position=tp,
                                skip_group_check=bool(skip))
            return last
        s.emit("te", fn, reads=reads, writes=writes)

    def act(fn, reads, writes):
        s.emit("act", fn, reads=reads, writes=writes)

    def dve(fn, reads, writes):
        s.emit("dve", fn, reads=reads, writes=writes)

    def floorc(wap, rres):
        n = wap.shape[1]
        dve(lambda e: e.tensor_copy(SI[:, 0:n], wap), reads=[rres], writes=[rSI])
        dve(lambda e: e.tensor_tensor(TBS[:, 0:n], SI[:, 0:n], wap, ALU.is_gt),
            reads=[rSI, rres], writes=[rTBS])
        dve(lambda e: e.scalar_tensor_tensor(wap, TBS[:, 0:n], -1.0, SI[:, 0:n], ALU.mult, ALU.add),
            reads=[rTBS, rSI], writes=[rres])

    def floorc2(wap, rres):
        n = wap.shape[1]
        dve(lambda e: e.tensor_copy(SI[:, 0:n], wap), reads=[rres], writes=[rSI])
        dve(lambda e: e.tensor_tensor(TBS[:, 0:n], SI[:, 0:n], wap, ALU.is_gt),
            reads=[rSI, rres], writes=[rTBS])
        dve(lambda e: e.scalar_tensor_tensor(TBSB[:, 0:n], SI[:, 0:n], -1.0, wap, ALU.add, ALU.is_gt),
            reads=[rSI, rres], writes=[rTBSB])
        dve(lambda e: e.scalar_tensor_tensor(wap, SI[:, 0:n], 1.0, wap, ALU.add, ALU.is_le),
            reads=[rSI, rres], writes=[rres])
        dve(lambda e: e.tensor_tensor(wap, wap, TBS[:, 0:n], ALU.subtract),
            reads=[rres, rTBS], writes=[rres])
        dve(lambda e: e.tensor_tensor(wap, wap, TBSB[:, 0:n], ALU.subtract),
            reads=[rres, rTBSB], writes=[rres])
        dve(lambda e: e.tensor_tensor(wap, wap, SI[:, 0:n], ALU.add),
            reads=[rres, rSI], writes=[rres])

    Xs = lambda c, b: X[:, c * T + b * QBS: c * T + (b + 1) * QBS]
    XNs = lambda c, b: XN[:, c * T + b * QBS: c * T + (b + 1) * QBS]
    XNt = lambda c, tt: XN[:, c * T + tt * 128: c * T + (tt + 1) * 128]
    ARow = RR[0:1, 0:QBS]
    BRow = RR[0:1, QBS:2 * QBS]

    dma("gp", B0R[:], dr["b0row"], writes=[rROWS])
    dma("gp", BVR[:], dr["bvrow"], writes=[rROWS])
    dma("gp", BOR[:], dr["borow"], writes=[rROWS])
    dma("gp", B2R[:], dr["b2row"], writes=[rROWS])
    dma("sp", BQC[:], dr["bqcol"], writes=[rROWS])
    dma("sp", BKC[:], dr["bkcol"], writes=[rROWS])
    dma("sp", B1C[:], dr["b1col"], writes=[rROWS])
    dma("sp", SCL[:, 0:1], dr["s_in"], writes=[rSCL])
    dma("sp", SCL[:, 1:2], dr["s_w"], writes=[rSCL])

    def consts_fn(e):
        e.memset(ONES[:], 1.0)
        e.memset(ONESB[:], 1.0)
        e.memset(ONEC[:], 1.0)
        e.memset(INDA[0:64, :], 1.0)
        e.memset(INDA[64:128, :], 0.0)
        e.memset(INDB[0:64, :], 0.0)
        e.memset(INDB[64:128, :], 1.0)
        e.memset(CST[:, C_LN2:C_LN2 + 1], LN2)
        e.memset(CST[:, C_3LN2:C_3LN2 + 1], 3.0 * LN2)
        e.memset(CST[:, C_EPS:C_EPS + 1], 1e-5)
        e.memset(CST[:, C_S1:C_S1 + 1], 0.125)
        return e.memset(SCL[:, 8:9], 128.0)
    s.emit("dve", consts_fn, reads=[], writes=[rC, rSCL])

    dve(lambda e: e.reciprocal(SCL[:, 2:3], SCL[:, 0:1]), reads=[rSCL], writes=[rSCL])
    dve(lambda e: e.tensor_tensor(SCL[:, 3:4], SCL[:, 0:1], SCL[:, 1:2], ALU.mult), reads=[rSCL], writes=[rSCL])
    act(lambda e: e.activation(SCL[:, 6:7], SCL[:, 8:9], AF.Ln), reads=[rSCL], writes=[rSCL])
    dve(lambda e: e.tensor_scalar(SCL[:, 6:7], SCL[:, 6:7], -0.5, None, ALU.mult), reads=[rSCL], writes=[rSCL])
    act(lambda e: e.activation(SCL[:, 6:7], SCL[:, 6:7], AF.Exp), reads=[rSCL], writes=[rSCL])
    dve(lambda e: e.reciprocal(SCL[:, 4:5], SCL[:, 3:4]), reads=[rSCL], writes=[rSCL])
    dve(lambda e: e.tensor_scalar(B0R[:], B0R[:], SCL[:, 4:5], None, ALU.mult), reads=[rROWS, rSCL], writes=[rROWS])
    dve(lambda e: e.tensor_scalar(BQC[:], BQC[:], 0.125, None, ALU.mult), reads=[rROWS], writes=[rROWS])
    mm([(PS[7][:, 0:1], ONES[0:1, 0:128], SCL[0:1, 3:4], True, True, None)], reads=[rC, rSCL], writes=[rPS[7]])
    act(lambda e: e.activation(CST[:, C_SS:C_SS + 1], PS[7][:, 0:1], AF.Identity), reads=[rPS[7]], writes=[rC])
    mm([(PS[7][:, 0:1], ONES[0:1, 0:128], SCL[0:1, 6:7], True, True, None)], reads=[rC, rSCL], writes=[rPS[7]])
    act(lambda e: e.activation(CST[:, C_S0:C_S0 + 1], PS[7][:, 0:1], AF.Identity), reads=[rPS[7]], writes=[rC])

    dma("gp", SELS[:], dr["sels"], writes=[rSEL])
    dma("gp", SELB[:], dr["selb"], writes=[rSEL])
    dve(lambda e: e.memset(TBR[:], 0.0), reads=[], writes=[rTBR])

    dma("gp", TT1[:], dr["tt1"], writes=[rTT])
    dma("gp", TT2[:], dr["tt2"], writes=[rTT])

    dma("sp", BRow, dr["iota"][0:1, 0:QBS], writes=[rRR])
    dve(lambda e: e.memset(TBS[0:1, :], 1.0), reads=[], writes=[rTBS])
    for dpos in range(4):
        mm([(PS[6][:, 0:QBS], BRow[0:1, 0:128], TBS[0:1, :], True, False, None)],
           reads=[rRR, rC, rTBS], writes=[rPS[6]])
        dve(lambda e, d_=dpos: e.tensor_scalar(SC1[0:1, 0:QBS], BRow, -1.0, 128.0 * d_, ALU.mult, ALU.add),
            reads=[rRR], writes=[rSC1])
        mm([(PS[6][:, 0:QBS], ONES[0:1, 0:128], SC1[0:1, 0:QBS], False, True, None)],
           reads=[rC, rSC1], writes=[rPS[6]])
        dve(lambda e: e.tensor_scalar(SC2[:], PS[6][:, 0:QBS], 0.0, -3.0e38, ALU.is_gt, ALU.mult),
            reads=[rPS[6]], writes=[rSC2])
        dve(lambda e, d_=dpos: e.tensor_copy(MSK[:, d_ * QBS:(d_ + 1) * QBS], SC2[:]),
            reads=[rSC2], writes=[rMSK])

    def ln_pass(quant=False):
        for b in range(NQB):
            if not quant:
                srcs, sres = (lambda c_: XNs(c_, b)), (lambda c_: rXN[c_][b])
            else:
                srcs = lambda c_: A3[:, c_ * QBS:(c_ + 1) * QBS]
                sres = lambda c_: rA3[c_]
            for c in range(NCH):
                dve(lambda e, d_=srcs(c), s_=Xs(c, b): e.tensor_copy(d_, s_),
                    reads=[rX[c][b]], writes=[sres(c)])
            mm([(PS[6][0:1, 0:QBS], ONESB[:, 0:1], srcs(c), c == 0, c == NCH - 1, None) for c in range(NCH)],
               reads=[rC] + [sres(c) for c in range(NCH)], writes=[rPS[6]])
            for c in range(NCH):
                if not quant:
                    sq, sqr = A3[:, 3072 + (c % 2) * QBS: 3072 + ((c % 2) + 1) * QBS], rA3[6 + (c % 2)]
                else:
                    sq, sqr = TBSB[:], rTBSB
                act(lambda e, i_=srcs(c), s_=sq: e.activation(s_, i_, AF.Square),
                    reads=[sres(c)], writes=[sqr])
                mm([(PS[6][32:33, 0:QBS], ONESB[:, 0:1], sq, c == 0, c == NCH - 1, None)],
                   reads=[rC, sqr], writes=[rPS[6]])
            dve(lambda e: e.tensor_scalar(BRow, PS[6][0:1, 0:QBS], 1.0 / D, None, ALU.mult),
                reads=[rPS[6]], writes=[rRR])
            dve(lambda e: e.tensor_scalar(ARow, PS[6][32:33, 0:QBS], 1.0 / D, None, ALU.mult),
                reads=[rPS[6]], writes=[rRR])
            dve(lambda e: e.tensor_tensor(SC1[0:1, 0:QBS], BRow, BRow, ALU.mult),
                reads=[rRR], writes=[rSC1])
            dve(lambda e: e.tensor_tensor(ARow, ARow, SC1[0:1, 0:QBS], ALU.subtract),
                reads=[rRR, rSC1], writes=[rRR])
            act(lambda e: e.activation(ARow, ARow, AF.Sqrt, bias=CST[0:1, C_EPS:C_EPS + 1]),
                reads=[rRR, rC], writes=[rRR])
            dve(lambda e: e.reciprocal(ARow, ARow), reads=[rRR], writes=[rRR])
            if quant:
                dve(lambda e: e.tensor_scalar(ARow, ARow, SCL[0:1, 2:3], None, ALU.mult),
                    reads=[rRR, rSCL], writes=[rRR])
            dve(lambda e: e.scalar_tensor_tensor(BRow, BRow, -1.0, ARow, ALU.mult, ALU.mult),
                reads=[rRR], writes=[rRR])
            if not quant:
                dve(lambda e: e.tensor_copy(TBR[0:1, 0:QBS], ARow), reads=[rRR], writes=[rTBR])
                dve(lambda e: e.tensor_copy(TBR[0:1, QBS:2 * QBS], BRow), reads=[rRR], writes=[rTBR])
                mm([(PS[4][:, 0:QBS], ONESB[0:1, 0:128], TBR[0:1, 0:QBS], True, True, None)],
                   reads=[rC, rTBR], writes=[rPS[4]])
                mm([(PS[5][:, 0:QBS], ONESB[0:1, 0:128], TBR[0:1, QBS:2 * QBS], True, True, None)],
                   reads=[rC, rTBR], writes=[rPS[5]])
                act(lambda e: e.activation(A3[:, 3072:3584], PS[4][:, 0:QBS], AF.Identity),
                    reads=[rPS[4]], writes=[rA3[6]])
                act(lambda e: e.activation(A3[:, 3584:4096], PS[5][:, 0:QBS], AF.Identity),
                    reads=[rPS[5]], writes=[rA3[7]])
            else:
                mm([(PS[4][:, 0:QBS], ONES[0:1, 0:128], ARow, True, True, None)], reads=[rC, rRR], writes=[rPS[4]])
                mm([(PS[5][:, 0:QBS], ONES[0:1, 0:128], BRow, True, True, None)], reads=[rC, rRR], writes=[rPS[5]])
            for c in range(NCH):
                if not quant:
                    dve(lambda e, c_=c, b_=b: e.tensor_tensor(XNs(c_, b_), XNs(c_, b_), A3[:, 3072:3584],
                                                              ALU.mult),
                        reads=[rXN[c][b], rA3[6]], writes=[rXN[c][b]])
                    dve(lambda e, c_=c, b_=b: e.tensor_tensor(XNs(c_, b_), XNs(c_, b_), A3[:, 3584:4096],
                                                              ALU.add),
                        reads=[rXN[c][b], rA3[7]], writes=[rXN[c][b]])
                else:
                    quant_block(c, b)

    def quant_block(c, b):
        dve(lambda e, c_=c, b_=b: e.tensor_tensor(SC1[:], Xs(c_, b_), PS[4][:, 0:QBS], ALU.mult),
            reads=[rX[c][b], rPS[4]], writes=[rSC1])
        dve(lambda e: e.tensor_tensor(SC1[:], SC1[:], PS[5][:, 0:QBS], ALU.add),
            reads=[rSC1, rPS[5]], writes=[rSC1])
        dve(lambda e: e.tensor_scalar(SC1[:], SC1[:], 448.0, -448.0, ALU.min, ALU.max),
            reads=[rSC1], writes=[rSC1])
        act(lambda e: e.activation(SC2[:], SC1[:], AF.Square), reads=[rSC1], writes=[rSC2])
        dve(lambda e: e.tensor_scalar(SC2[:], SC2[:], 2.0 ** -24, None, ALU.max), reads=[rSC2], writes=[rSC2])
        act(lambda e: e.activation(SC2[:], SC2[:], AF.Ln), reads=[rSC2], writes=[rSC2])
        dve(lambda e: e.tensor_scalar(SC2[:], SC2[:], float(np.float32(0.5 / np.log(2.0))), None, ALU.mult),
            reads=[rSC2], writes=[rSC2])
        floorc(SC2[:], rSC2)
        dve(lambda e: e.tensor_copy(SI[:], SC2[:]), reads=[rSC2], writes=[rSI])
        dve(lambda e: e.tensor_scalar(SC2[:], SI[:], -6.0, -1.0, ALU.max, ALU.mult), reads=[rSI], writes=[rSC2])
        act(lambda e: e.activation(SC2[:], SC2[:], AF.Exp, bias=CST[:, C_3LN2:C_3LN2 + 1],
                                   scale=CST[:, C_LN2:C_LN2 + 1]),
            reads=[rSC2, rC], writes=[rSC2])
        dve(lambda e: e.tensor_scalar(SC2[:], SC2[:], 64.0, 0.5, ALU.mult, ALU.add), reads=[rSC2], writes=[rSC2])
        floorc(SC2[:], rSC2)
        dve(lambda e: e.tensor_tensor(SC1[:], SC1[:], SC2[:], ALU.mult), reads=[rSC1, rSC2], writes=[rSC1])
        dve(lambda e: e.tensor_scalar(SC1[:], SC1[:], 1.0 / 64.0, 0.5, ALU.mult, ALU.add),
            reads=[rSC1], writes=[rSC1])
        floorc2(SC1[:], rSC1)
        dve(lambda e: e.reciprocal(SC2[:], SC2[:]), reads=[rSC2], writes=[rSC2])
        dve(lambda e: e.tensor_tensor(SC1[:], SC1[:], SC2[:], ALU.mult), reads=[rSC1, rSC2], writes=[rSC1])
        dve(lambda e, c_=c, b_=b: e.tensor_scalar(XNs(c_, b_), SC1[:], 64.0, None, ALU.mult),
            reads=[rSC1], writes=[rXN[c][b]])

    for seq in range(n_seq):
        for c in range(NCH):
            dma("sp", X[:, c * T:(c + 1) * T], dr["x_fm"][seq * D + c * 128: seq * D + (c + 1) * 128, :],
                writes=[rX[c][b] for b in range(NQB)])

        if "l0a" not in phases:
            continue
        dma("gp", A4[:, 0:1024], dr["w0q"], writes=[rA4[0]])
        dma("gp", A4[:, 1024:2048], dr["w0k"], writes=[rA4[1]])
        dma("gp", A4[:, 2048:3072], dr["w0v"], writes=[rA4[2]])
        dma("gp", A4[:, 3072:4096], dr["w0o"], writes=[rA4[3]])
        ln_pass()
        for b in range(NQB):
            xnres = [rXN[c][b] for c in range(NCH)]
            mm([(PS[0][:, 0:QBS], A4[:, ic * 128:(ic + 1) * 128], XNs(ic, b), ic == 0, ic == NCH - 1, None)
                for ic in range(NCH)], reads=[rA4[0]] + xnres, writes=[rPS[0]])
            act(lambda e, b_=b: e.activation(A0[:, b_ * QBS:(b_ + 1) * QBS], PS[0][:, 0:QBS], AF.Identity,
                                             scale=CST[:, C_S0:C_S0 + 1]),
                reads=[rPS[0], rC], writes=[rA0[0]])
            mm([(PS[1][:, 0:QBS], A4[:, 1024 + ic * 128:1024 + (ic + 1) * 128], XNs(ic, b), ic == 0,
                 ic == NCH - 1, None) for ic in range(NCH)], reads=[rA4[1]] + xnres, writes=[rPS[1]])
            act(lambda e, b_=b: e.activation(A1[:, b_ * QBS:(b_ + 1) * QBS], PS[1][:, 0:QBS], AF.Identity),
                reads=[rPS[1]], writes=[rA1[0]])
            for tt in range(4):
                tta = b * 4 + tt
                mm([(PS[2][:, 0:128], XNt(ic, tta), A4[:, 2048 + ic * 128:2048 + (ic + 1) * 128],
                     ic == 0, ic == NCH - 1, None) for ic in range(NCH)],
                   reads=[rA4[2]] + xnres, writes=[rPS[2]])
                act(lambda e, t_=tta: e.activation(A2[:, t_ * 128:(t_ + 1) * 128], PS[2][:, 0:128], AF.Identity),
                    reads=[rPS[2]], writes=[rA2[tta // 4]])
        dve(lambda e: e.memset(SCL[0:1, 7:8], 0.0), reads=[rSCL], writes=[rSCL])
        for b in range(NQB):
            act(lambda e, b_=b: e.activation(TBSB[:], A1[:, b_ * QBS:(b_ + 1) * QBS], AF.Square),
                reads=[rA1[0]], writes=[rTBSB])
            mm([(PS[4][0:1, 0:QBS], ONESB[:, 0:1], TBSB[:], True, True, None)], reads=[rC, rTBSB], writes=[rPS[4]])
            dve(lambda e: e.tensor_reduce(SC1[0:1, 0:1], PS[4][0:1, 0:QBS], mybir.AxisListType.X, ALU.max),
                reads=[rPS[4]], writes=[rSC1])
            dve(lambda e: e.tensor_tensor(SCL[0:1, 7:8], SCL[0:1, 7:8], SC1[0:1, 0:1], ALU.max),
                reads=[rSC1, rSCL], writes=[rSCL])
        dve(lambda e: e.memset(QMX[0:1, 0:1], 0.0), reads=[rQM], writes=[rQM])
        for qb in range(NQB):
            qsl = slice(qb * QBS, (qb + 1) * QBS)
            act(lambda e, q_=qsl: e.activation(TBSB[:], A0[:, q_], AF.Square), reads=[rA0[0]], writes=[rTBSB])
            mm([(PS[4][0:1, 0:QBS], ONESB[:, 0:1], TBSB[:], True, True, None)], reads=[rC, rTBSB], writes=[rPS[4]])
            dve(lambda e: e.tensor_reduce(SC1[0:1, 0:1], PS[4][0:1, 0:QBS], mybir.AxisListType.X, ALU.max),
                reads=[rPS[4]], writes=[rSC1])
            dve(lambda e: e.tensor_tensor(QMX[0:1, 0:1], QMX[0:1, 0:1], SC1[0:1, 0:1], ALU.max),
                reads=[rSC1, rQM], writes=[rQM])
        dve(lambda e: e.tensor_scalar(QMX[0:1, 0:1], QMX[0:1, 0:1], SCL[0:1, 7:8], None, ALU.mult),
            reads=[rQM, rSCL], writes=[rQM])
        act(lambda e: e.activation(QMX[0:1, 0:1], QMX[0:1, 0:1], AF.Sqrt), reads=[rQM], writes=[rQM])
        dve(lambda e: e.tensor_scalar(QMX[0:1, 0:1], QMX[0:1, 0:1], -1.0, None, ALU.mult),
            reads=[rQM], writes=[rQM])
        mm([(PS[4][:, 0:1], ONES[0:1, 0:128], QMX[0:1, 0:1], True, True, None)], reads=[rC, rQM], writes=[rPS[4]])
        act(lambda e: e.activation(MHC[:, 0:1], PS[4][:, 0:1], AF.Identity), reads=[rPS[4]], writes=[rMH])
        OB0 = (3, 5, 6, 7)
        for qb in range(NQB):
            qsl = slice(qb * QBS, (qb + 1) * QBS)
            nkc = 4 * (qb + 1)
            obank = OB0[qb]

            def s_tile0(kc, qsl_=qsl):
                bank = 0 if (kc % 2 == 0) else 1
                mm([(PS[bank][:, 0:QBS], A1[:, kc * 128:(kc + 1) * 128], A0[:, qsl_], True, True, None)],
                   reads=[rA1[0], rA0[0]], writes=[rPS[bank]])
                return bank
            prev = s_tile0(0)
            for kc in range(nkc):
                bank = prev
                if kc + 1 < nkc:
                    prev = s_tile0(kc + 1)
                if kc >= 4 * qb:
                    dpos = kc - 4 * qb
                    dve(lambda e, bk=bank, d_=dpos: e.tensor_tensor(
                        PS[bk][:, 0:QBS], PS[bk][:, 0:QBS], MSK[:, d_ * QBS:(d_ + 1) * QBS], ALU.add),
                        reads=[rPS[bank], rMSK], writes=[rPS[bank]])
                pt = kc % 2
                ptap = A3[:, 2048 + pt * QBS: 2048 + (pt + 1) * QBS]
                act(lambda e, bk=bank, a_=ptap: e.activation(a_, PS[bk][:, 0:QBS], AF.Exp, bias=MHC[:, 0:1]),
                    reads=[rPS[bank], rMH], writes=[rA3[4 + pt]])
                mm([(PS[2][0:8, 0:QBS], SELS[:, qb * 8:(qb + 1) * 8], ptap,
                     qb == 0 and kc == 0, qb == NQB - 1 and kc == nkc - 1, None)],
                   reads=[rSEL, rA3[4 + pt]], writes=[rPS[2]])
                mm([(PS[obank][:, 0:QBS], A2[:, kc * 128:(kc + 1) * 128], ptap, kc == 0, kc == nkc - 1, None)],
                   reads=[rA2[kc // 4], rA3[4 + pt]], writes=[rPS[obank]])
        dve(lambda e: e.reciprocal(SC2[0:4, 0:QBS], PS[2][0:4, 0:QBS]), reads=[rPS[2]], writes=[rSC2])
        dve(lambda e: e.tensor_copy(TBR[0:4, 0:QBS], SC2[0:4, 0:QBS]), reads=[rSC2], writes=[rTBR])
        for qb in range(NQB):
            obank = OB0[qb]
            mm([(PS[4][:, 0:QBS], SELB[:, 512 + qb * 128: 512 + (qb + 1) * 128], TBR[:, 0:QBS], True, True, None)],
               reads=[rSEL, rTBR], writes=[rPS[4]])
            act(lambda e: e.activation(TBS[:], PS[4][:, 0:QBS], AF.Identity), reads=[rPS[4]], writes=[rTBS])
            dve(lambda e, bk=obank, q_=qb: e.tensor_tensor(A3[:, q_ * QBS:(q_ + 1) * QBS], PS[bk][:, 0:QBS],
                                                           TBS[:], ALU.mult),
                reads=[rPS[obank], rTBS], writes=[rA3[qb]])
        for qb in range(NQB):
            for dm in range(NCH):
                bank = 3 if (dm % 2 == 0) else 5
                mm([(PS[bank][:, 0:QBS], A4[:, 3072 + dm * 128:3072 + (dm + 1) * 128],
                     A3[:, qb * QBS:(qb + 1) * QBS], True, True, None)],
                   reads=[rA4[3], rA3[qb]], writes=[rPS[bank]])
                dve(lambda e, d_=dm, bk=bank, q_=qb: e.tensor_tensor(Xs(d_, q_), Xs(d_, q_),
                                                                     PS[bk][:, 0:QBS], ALU.add),
                    reads=[rPS[bank], rX[dm][qb]], writes=[rX[dm][qb]])

        if debug:
            for c in range(NCH):
                dma("sp", dbg["dbg0a"][seq * D + c * 128: seq * D + (c + 1) * 128, :], X[:, c * T:(c + 1) * T],
                    reads=[rX[c][b] for b in range(NQB)])
        if "l0f" not in phases:
            continue
        dma("gp", A0[:, :], dr["w8T"], writes=[rA0[0], rA0[1], rA0[2], rA0[3]])
        ln_pass(quant=True)
        for b in range(NQB):
            xnres = [rXN[c][b] for c in range(NCH)]
            for oc in range(NCH):
                bank = oc % 2
                ms = [(PS[bank][:, 0:QBS], B0R[0:1, oc * 128:(oc + 1) * 128], ONESB[0:1, 0:QBS], True, False, None)]
                ms += [(PS[bank][:, 0:QBS], A0[:, ic * 1024 + oc * 128: ic * 1024 + (oc + 1) * 128],
                        XNs(ic, b), False, ic == NCH - 1, None) for ic in range(NCH)]
                mm(ms, reads=[rROWS, rC, rA0[0], rA0[1], rA0[2], rA0[3]] + xnres, writes=[rPS[bank]])
                dve(lambda e, o_=oc, b_=b, bk=bank: e.scalar_tensor_tensor(
                    Xs(o_, b_), PS[bk][:, 0:QBS], CST[:, C_SS:C_SS + 1], Xs(o_, b_), ALU.mult, ALU.add),
                    reads=[rPS[bank], rC, rX[oc][b]], writes=[rX[oc][b]])

        if debug:
            for c in range(NCH):
                dma("sp", dbg["dbg0f"][seq * D + c * 128: seq * D + (c + 1) * 128, :], X[:, c * T:(c + 1) * T],
                    reads=[rX[c][b] for b in range(NQB)])
        if "l1a" not in phases:
            continue
        ln_pass()
        for g in range(NGRP):
            dma("gp", A4[:, 0:4096], dr["w1q"][g * 128:(g + 1) * 128, :],
                writes=[rA4[0], rA4[1], rA4[2], rA4[3]])
            dma("gp", A4[:, 4096:8192], dr["w1k"][g * 128:(g + 1) * 128, :],
                writes=[rA4[4], rA4[5], rA4[6], rA4[7]])
            for cc in range(CPG):
                for b in range(NQB):
                    xnres = [rXN[c][b] for c in range(NCH)]
                    mm([(PS[0][:, 0:QBS], A4[:, ic * 512 + cc * 128: ic * 512 + (cc + 1) * 128],
                         XNs(ic, b), ic == 0, ic == NCH - 1, None) for ic in range(NCH)],
                       reads=[rA4[0], rA4[1], rA4[2], rA4[3]] + xnres, writes=[rPS[0]])
                    act(lambda e, c_=cc, b_=b, g_=g: e.activation(
                        A0[:, c_ * T + b_ * QBS: c_ * T + (b_ + 1) * QBS], PS[0][:, 0:QBS], AF.Identity,
                        bias=BQC[:, g_ * 4 + c_: g_ * 4 + c_ + 1], scale=CST[:, C_S1:C_S1 + 1]),
                        reads=[rPS[0], rROWS, rC], writes=[rA0[cc]])
                    mm([(PS[1][:, 0:QBS], A4[:, 4096 + ic * 512 + cc * 128: 4096 + ic * 512 + (cc + 1) * 128],
                         XNs(ic, b), ic == 0, ic == NCH - 1, None) for ic in range(NCH)],
                       reads=[rA4[4], rA4[5], rA4[6], rA4[7]] + xnres, writes=[rPS[1]])
                    act(lambda e, c_=cc, b_=b, g_=g: e.activation(
                        A1[:, c_ * T + b_ * QBS: c_ * T + (b_ + 1) * QBS], PS[1][:, 0:QBS], AF.Identity,
                        bias=BKC[:, g_ * 4 + c_: g_ * 4 + c_ + 1]),
                        reads=[rPS[1], rROWS], writes=[rA1[cc]])
            dma("gp", A4[:, 0:4096], dr["w1v"][g * 128:(g + 1) * 128, :],
                writes=[rA4[0], rA4[1], rA4[2], rA4[3]])
            for tt in range(16):
                b = tt // 4
                xnres = [rXN[c][b] for c in range(NCH)]
                ms = [(PS[2][:, 0:QBS], ONESB[0:1, 0:128], BVR[0:1, g * QBS:(g + 1) * QBS], True, False, None)]
                ms += [(PS[2][:, 0:QBS], XNt(ic, tt), A4[:, ic * 512:(ic + 1) * 512], False, ic == NCH - 1, None)
                       for ic in range(NCH)]
                mm(ms, reads=[rC, rROWS, rA4[0], rA4[1], rA4[2], rA4[3]] + xnres, writes=[rPS[2]])
                act(lambda e, t_=tt: e.activation(A2[:, t_ * QBS:(t_ + 1) * QBS], PS[2][:, 0:QBS], AF.Identity),
                    reads=[rPS[2]], writes=[rA2[tt]])
            for (buf, rbuf) in ((A0, rA0), (A1, rA1)):
                for cc in range(CPG):
                    csl = slice(cc * T, (cc + 1) * T)
                    for (dst0, src0) in ((0, 32), (32, 0), (64, 96), (96, 64)):
                        dma("sp", A3[dst0:dst0 + 32, 2048:4096], buf[src0:src0 + 32, csl],
                            reads=[rbuf[cc]], writes=list(rA3[4:8]))
                    dve(lambda e, b_=buf, c_=csl: e.tensor_tensor(A3[:, 0:2048], b_[:, c_], TT1[:], ALU.mult),
                        reads=[rbuf[cc], rTT], writes=list(rA3[0:4]))
                    dve(lambda e: e.tensor_tensor(A3[:, 2048:4096], A3[:, 2048:4096], TT2[:], ALU.mult),
                        reads=list(rA3[4:8]) + [rTT], writes=list(rA3[4:8]))
                    dve(lambda e, b_=buf, c_=csl: e.tensor_tensor(b_[:, c_], A3[:, 0:2048], A3[:, 2048:4096],
                                                                  ALU.add),
                        reads=list(rA3[0:8]), writes=[rbuf[cc]])
            dma("gp", A4[:, 4096:8192], dr["w1o"][g * 128:(g + 1) * 128, :],
                writes=[rA4[4], rA4[5], rA4[6], rA4[7]])
            dve(lambda e: e.memset(SCL[0:1, 7:8], 0.0), reads=[rSCL], writes=[rSCL])
            for b in range(NQB):
                for cc in range(CPG):
                    act(lambda e, c_=cc, b_=b: e.activation(TBSB[:], A1[:, c_ * T + b_ * QBS: c_ * T + (b_ + 1) * QBS],
                                                            AF.Square),
                        reads=[rA1[cc]], writes=[rTBSB])
                    mm([(PS[4][0:1, 0:QBS], ONESB[:, 0:1], TBSB[:], cc == 0, cc == CPG - 1, None)],
                       reads=[rC, rTBSB], writes=[rPS[4]])
                dve(lambda e: e.tensor_reduce(SC1[0:1, 0:1], PS[4][0:1, 0:QBS], mybir.AxisListType.X, ALU.max),
                    reads=[rPS[4]], writes=[rSC1])
                dve(lambda e: e.tensor_tensor(SCL[0:1, 7:8], SCL[0:1, 7:8], SC1[0:1, 0:1], ALU.max),
                    reads=[rSC1, rSCL], writes=[rSCL])
            for cc in range(CPG):
                dve(lambda e, c_=cc: e.memset(QMX[0:1, 2 * c_:2 * c_ + 2], 0.0), reads=[rQM], writes=[rQM])
                for qb in range(NQB):
                    qsl = slice(cc * T + qb * QBS, cc * T + (qb + 1) * QBS)
                    act(lambda e, q_=qsl: e.activation(TBSB[:], A0[:, q_], AF.Square), reads=[rA0[cc]], writes=[rTBSB])
                    mm([(PS[4][0:1, 0:QBS], INDA[:], TBSB[:], True, True, None),
                        (PS[4][32:33, 0:QBS], INDB[:], TBSB[:], True, True, None)],
                       reads=[rC, rTBSB], writes=[rPS[4]])
                    for (r0, hoff) in ((0, 0), (32, 1)):
                        dve(lambda e, r_=r0: e.tensor_reduce(SC1[0:1, 0:1], PS[4][r_:r_ + 1, 0:QBS],
                                                             mybir.AxisListType.X, ALU.max),
                            reads=[rPS[4]], writes=[rSC1])
                        dve(lambda e, c_=cc, h_=hoff: e.tensor_tensor(
                            QMX[0:1, 2 * c_ + h_:2 * c_ + h_ + 1], QMX[0:1, 2 * c_ + h_:2 * c_ + h_ + 1],
                            SC1[0:1, 0:1], ALU.max),
                            reads=[rSC1, rQM], writes=[rQM])
            dve(lambda e: e.tensor_scalar(QMX[0:1, 0:8], QMX[0:1, 0:8], SCL[0:1, 7:8], None, ALU.mult),
                reads=[rQM, rSCL], writes=[rQM])
            act(lambda e: e.activation(QMX[0:1, 0:8], QMX[0:1, 0:8], AF.Sqrt), reads=[rQM], writes=[rQM])
            dve(lambda e: e.tensor_scalar(QMX[0:1, 0:8], QMX[0:1, 0:8], -1.0, None, ALU.mult),
                reads=[rQM], writes=[rQM])
            mm([(PS[4][:, 0:8], ONES[0:1, 0:128], QMX[0:1, 0:8], True, True, None)], reads=[rC, rQM], writes=[rPS[4]])
            act(lambda e: e.activation(MHC[:, 0:8], PS[4][:, 0:8], AF.Identity), reads=[rPS[4]], writes=[rMH])
            for qb in range(NQB):
                nkc = 4 * (qb + 1)
                for cc in range(CPG):
                    qsl = slice(cc * T + qb * QBS, cc * T + (qb + 1) * QBS)

                    def s_tile1(kc, cc_=cc, qsl_=qsl):
                        bA = 0 if (kc % 2 == 0) else 6
                        bB = 1 if (kc % 2 == 0) else 7
                        ksl = slice(cc_ * T + kc * 128, cc_ * T + (kc + 1) * 128)
                        mm([(PS[bA][:, 0:QBS], A1[0:64, ksl], A0[0:64, qsl_], True, True, None)],
                           reads=[rA1[cc_], rA0[cc_]], writes=[rPS[bA]])
                        mm([(PS[bB][:, 0:QBS], A1[64:128, ksl], A0[64:128, qsl_], True, True, None)],
                           reads=[rA1[cc_], rA0[cc_]], writes=[rPS[bB]])
                        return bA, bB
                    prev = s_tile1(0)
                    for kc in range(nkc):
                        bA, bB = prev
                        if kc + 1 < nkc:
                            prev = s_tile1(kc + 1)
                        if kc >= 4 * qb:
                            dpos = kc - 4 * qb
                            for bk in (bA, bB):
                                dve(lambda e, b_=bk, d_=dpos: e.tensor_tensor(
                                    PS[b_][:, 0:QBS], PS[b_][:, 0:QBS], MSK[:, d_ * QBS:(d_ + 1) * QBS], ALU.add),
                                    reads=[rPS[bk], rMSK], writes=[rPS[bk]])
                        ptA, ptB = (kc % 2) * 2, (kc % 2) * 2 + 1
                        apA = A3[:, 2048 + ptA * QBS: 2048 + (ptA + 1) * QBS]
                        apB = A3[:, 2048 + ptB * QBS: 2048 + (ptB + 1) * QBS]
                        act(lambda e, b_=bA, a_=apA, c_=cc: e.activation(
                            a_, PS[b_][:, 0:QBS], AF.Exp, bias=MHC[:, 2 * c_:2 * c_ + 1]),
                            reads=[rPS[bA], rMH], writes=[rA3[4 + ptA]])
                        act(lambda e, b_=bB, a_=apB, c_=cc: e.activation(
                            a_, PS[b_][:, 0:QBS], AF.Exp, bias=MHC[:, 2 * c_ + 1:2 * c_ + 2]),
                            reads=[rPS[bB], rMH], writes=[rA3[4 + ptB]])
                        first = (cc == 0 and kc == 0)
                        last = (cc == CPG - 1 and kc == nkc - 1)
                        mm([(PS[2][0:8, 0:QBS], SELS[:, (2 * cc) * 8:(2 * cc) * 8 + 8], apA,
                             first, False, None),
                            (PS[2][0:8, 0:QBS], SELS[:, (2 * cc + 1) * 8:(2 * cc + 1) * 8 + 8], apB,
                             False, last, None)],
                           reads=[rSEL, rA3[4 + ptA], rA3[4 + ptB]], writes=[rPS[2]])
                        lA = 2 * cc * DH
                        lB = (2 * cc + 1) * DH
                        mm([(PS[3][0:64, 0:QBS], A2[:, kc * QBS + lA: kc * QBS + lA + DH], apA,
                             kc == 0, kc == nkc - 1, (0, 0)),
                            (PS[3][64:128, 0:QBS], A2[:, kc * QBS + lB: kc * QBS + lB + DH], apB,
                             kc == 0, kc == nkc - 1, (0, 64), True)],
                           reads=[rA2[kc], rA3[4 + ptA], rA3[4 + ptB]], writes=[rPS[3]])
                    act(lambda e, c_=cc: e.activation(A3[:, c_ * QBS:(c_ + 1) * QBS], PS[3][:, 0:QBS],
                                                      AF.Identity),
                        reads=[rPS[3]], writes=[rA3[cc]])
                dve(lambda e: e.reciprocal(SC2[0:8, 0:QBS], PS[2][0:8, 0:QBS]), reads=[rPS[2]], writes=[rSC2])
                dve(lambda e: e.tensor_copy(TBR[:, 0:QBS], SC2[0:8, 0:QBS]), reads=[rSC2], writes=[rTBR])
                for cc in range(CPG):
                    mm([(PS[4][:, 0:QBS], SELB[:, cc * 128:(cc + 1) * 128], TBR[:, 0:QBS], True, True, None)],
                       reads=[rSEL, rTBR], writes=[rPS[4]])
                    act(lambda e: e.activation(TBSB[:], PS[4][:, 0:QBS], AF.Identity), reads=[rPS[4]], writes=[rTBSB])
                    dve(lambda e, c_=cc: e.tensor_tensor(A3[:, c_ * QBS:(c_ + 1) * QBS],
                                                         A3[:, c_ * QBS:(c_ + 1) * QBS], TBSB[:], ALU.mult),
                        reads=[rA3[cc], rTBSB], writes=[rA3[cc]])
                for dm in range(NCH):
                    bank = 5 if (dm % 2 == 0) else 6
                    ms = []
                    if g == 0:
                        ms.append((PS[bank][:, 0:QBS], BOR[0:1, dm * 128:(dm + 1) * 128], ONESB[0:1, 0:QBS],
                                   True, False, None))
                    ms += [(PS[bank][:, 0:QBS], A4[:, 4096 + cc * 1024 + dm * 128: 4096 + cc * 1024 + (dm + 1) * 128],
                            A3[:, cc * QBS:(cc + 1) * QBS], (g != 0 and cc == 0), cc == CPG - 1, None)
                           for cc in range(CPG)]
                    mm(ms, reads=[rROWS, rC, rA4[4], rA4[5], rA4[6], rA4[7]] + list(rA3[0:4]), writes=[rPS[bank]])
                    dve(lambda e, d_=dm, q_=qb, bk=bank: e.tensor_tensor(Xs(d_, q_), Xs(d_, q_),
                                                                         PS[bk][:, 0:QBS], ALU.add),
                        reads=[rPS[bank], rX[dm][qb]], writes=[rX[dm][qb]])

        if debug:
            for c in range(NCH):
                dma("sp", dbg["dbg1a"][seq * D + c * 128: seq * D + (c + 1) * 128, :], X[:, c * T:(c + 1) * T],
                    reads=[rX[c][b] for b in range(NQB)])
        if "l1f" not in phases:
            continue
        ln_pass()
        for fs in range(NFS):
            dma("gp", A0[:, :], dr["w1f"][fs * 128:(fs + 1) * 128, :], writes=[rA0[0], rA0[1], rA0[2], rA0[3]])
            dma("gp", A1[:, :], dr["w2f"][fs * 128:(fs + 1) * 128, :], writes=[rA1[0], rA1[1], rA1[2], rA1[3]])
            wread = [rA0[0], rA0[1], rA0[2], rA0[3]]
            w2read = [rA1[0], rA1[1], rA1[2], rA1[3]]
            for b in range(NQB):
                xnres = [rXN[c][b] for c in range(NCH)]
                for hc in range(NCH):
                    bank = hc % 2
                    mm([(PS[bank][:, 0:QBS], A0[:, ic * 1024 + hc * 128: ic * 1024 + (hc + 1) * 128],
                         XNs(ic, b), ic == 0, ic == NCH - 1, None) for ic in range(NCH)],
                       reads=wread + xnres, writes=[rPS[bank]])
                    act(lambda e, h_=hc, f_=fs, bk=bank: e.activation(
                        A2[:, h_ * QBS:(h_ + 1) * QBS], PS[bk][:, 0:QBS], AF.Gelu,
                        bias=B1C[:, f_ * 8 + h_: f_ * 8 + h_ + 1]),
                        reads=[rPS[bank], rROWS], writes=[rA2[hc]])
                for dm in range(NCH):
                    bank = 2 + (dm % 2)
                    ms = []
                    if fs == 0:
                        ms.append((PS[bank][:, 0:QBS], B2R[0:1, dm * 128:(dm + 1) * 128], ONESB[0:1, 0:QBS],
                                   True, False, None))
                    ms += [(PS[bank][:, 0:QBS], A1[:, hc * 1024 + dm * 128: hc * 1024 + (dm + 1) * 128],
                            A2[:, hc * QBS:(hc + 1) * QBS], (fs != 0 and hc == 0), hc == NCH - 1, None)
                           for hc in range(NCH)]
                    mm(ms, reads=[rROWS, rC] + w2read + [rA2[hc] for hc in range(NCH)], writes=[rPS[bank]])
                    dve(lambda e, d_=dm, b_=b, bk=bank: e.tensor_tensor(Xs(d_, b_), Xs(d_, b_),
                                                                        PS[bk][:, 0:QBS], ALU.add),
                        reads=[rPS[bank], rX[dm][b]], writes=[rX[dm][b]])

        for c in range(NCH):
            dma("sp", out[seq * D + c * 128: seq * D + (c + 1) * 128, :], X[:, c * T:(c + 1) * T],
                reads=[rX[c][b] for b in range(NQB)])

    with ctx:
        sems = {e: ctx.enter_context(nc.semaphore("sem_" + e)) for e in ("te", "act", "dve")}
        qsems = {q: [ctx.enter_context(nc.semaphore("sem_%s%d" % (q, i))) for i in range(QDEPTH)]
                 for q in QUEUES}

        def do_wait(eng, pe, pt):
            if pe in QUEUES:
                eng.wait_ge(qsems[pe][(pt - 1) % QDEPTH], 16 * ((pt - 1) // QDEPTH + 1))
            else:
                eng.wait_ge(sems[pe], pt)

        def run_queue(eng, q):
            for i, (waits, fn) in enumerate(s.stream[q]):
                for pe, pt in waits:
                    do_wait(eng, pe, pt)
                fn(eng).then_inc(qsems[q][i % QDEPTH], 16)
            for t in range(max(0, s.tick[q] - QDEPTH), s.tick[q]):
                do_wait(eng, q, t + 1)

        def run_eng(eng, e):
            for waits, fn in s.stream[e]:
                for pe, pt in waits:
                    do_wait(eng, pe, pt)
                fn(eng).then_inc(sems[e], 1)

        with nc.Block() as block:
            @block.sync
            def _(sync):
                run_queue(sync, "sp")

            @block.gpsimd
            def _(gp):
                run_queue(gp, "gp")

            @block.tensor
            def _(te):
                run_eng(te, "te")

            @block.scalar
            def _(sc):
                run_eng(sc, "act")

            @block.vector
            def _(ve):
                run_eng(ve, "dve")
    n_inst = sum(len(v) for v in s.stream.values())
    return nc, n_inst


def kernel(**inputs):
    L = host_layout(inputs)
    nc, n_inst = build_nc()
    in_maps = []
    for core in range(8):
        m = {nm: L[nm] for nm, _ in DRAM_SPECS if nm != "x_fm"}
        m["x_fm"] = np.ascontiguousarray(L["x_fm"][core * 2 * D:(core + 1) * 2 * D, :])
        in_maps.append(m)
    res = run_bass_kernel_spmd(nc, in_maps, core_ids=list(range(8)))
    outs = []
    for core in range(8):
        o = np.asarray(res.results[core]["out_fm"]).reshape(2, D, T)
        outs.append(np.transpose(o, (0, 2, 1)))
    return np.concatenate(outs, 0)
